# version 04
# baseline (speedup 1.0000x reference)
import os
import numpy as np
import ml_dtypes
from contextlib import ExitStack

import concourse.bass as bass
import concourse.mybir as mybir
from concourse.bass_utils import run_bass_kernel_spmd

F32 = mybir.dt.float32
BF16 = mybir.dt.bfloat16
AF = mybir.ActivationFunctionType
ALU = mybir.AluOpType
AX = mybir.AxisListType
BF = ml_dtypes.bfloat16

D, DIN, H, PDIM, NG, NSTATE, CK, CH = 4096, 8192, 128, 64, 8, 128, 4, 256
DPROJ = 2 * DIN + 2 * NG * NSTATE + H
CONVD = DIN + 2 * NG * NSTATE
NSTRIP = DPROJ // 128
DTSTRIP = 144
EPS = 1e-5
ECLAMP = 50.0
KD = D // 128
KDIN = DIN // 128


class Cnt:
    def __init__(self, sem):
        self.sem = sem
        self.n = 0

    def inc(self, inst, k=1):
        inst.then_inc(self.sem, k)
        self.n += k
        return self.n


class RingSem:
    def __init__(self, sems, per=16):
        self.sems = sems
        self.n = len(sems)
        self.per = per
        self.base = [c.n for c in sems]

    def inc(self, inst, idx):
        sl = idx % self.n
        inst.then_inc(self.sems[sl].sem, self.per)
        self.sems[sl].n += self.per

    def wait_done(self, eng, idx, uses=None):
        sl = idx % self.n
        u = (idx // self.n + 1) if uses is None else uses
        eng.wait_ge(self.sems[sl].sem, self.base[sl] + self.per * u)

    def wait_all(self, eng, total_items):
        for sl in range(self.n):
            cnt = (total_items - sl + self.n - 1) // self.n
            if cnt > 0:
                eng.wait_ge(self.sems[sl].sem, self.base[sl] + self.per * cnt)


def build_program(L=2048, debug=(), last_phase="D"):
    DSKIP = set(os.environ.get("MK_DSKIP", "").split(","))
    NCH = L // CH
    NTB = L // 512
    NLT = L // 128
    PH = {"A": 0, "B": 1, "C": 2, "D": 3}[last_phase]

    a1_groups = [(64 + 2 * i, 65 + 2 * i) for i in range(40)] + \
                [(2 * i, 2 * i + 1) for i in range(32)]
    NGRP = len(a1_groups)
    strip_order = []
    for g in a1_groups:
        strip_order.extend(g)
    spill_idx = {s: i for i, s in enumerate(strip_order)}

    nc = bass.Bass("TRN2", target_bir_lowering=False)
    ap = lambda t: t.ap()

    X = ap(nc.dram_tensor("x", [L, D], F32, kind="ExternalInput"))
    WIN = ap(nc.dram_tensor("win", [DPROJ, D], F32, kind="ExternalInput"))
    CONVW = ap(nc.dram_tensor("convw", [CONVD, CK], F32, kind="ExternalInput"))
    CONVB = ap(nc.dram_tensor("convb", [CONVD], F32, kind="ExternalInput"))
    DTB = ap(nc.dram_tensor("dtbias", [H], F32, kind="ExternalInput"))
    ALOG = ap(nc.dram_tensor("alog", [H], F32, kind="ExternalInput"))
    DHEAD = ap(nc.dram_tensor("dhead", [1, H], F32, kind="ExternalInput"))
    NRMW = ap(nc.dram_tensor("nrmw", [DIN], F32, kind="ExternalInput"))
    WOUT = ap(nc.dram_tensor("wout", [D, DIN], F32, kind="ExternalInput"))
    WBASE = ap(nc.dram_tensor("wbase", [D, D], F32, kind="ExternalInput"))
    BBASE = ap(nc.dram_tensor("bbase", [D], F32, kind="ExternalInput"))
    WGATE9 = ap(nc.dram_tensor("wgate9", [9, D], F32, kind="ExternalInput"))
    BTHR = ap(nc.dram_tensor("bthr", [1, 1], F32, kind="ExternalInput"))
    AEXP = ap(nc.dram_tensor("aexp", [128, D], F32, kind="ExternalInput"))
    BEXP = ap(nc.dram_tensor("bexp", [8 * D, 16], F32, kind="ExternalInput"))
    TRIL = ap(nc.dram_tensor("tril", [128, 512], BF16, kind="ExternalInput"))
    I128F = ap(nc.dram_tensor("i128f", [128, 128], F32, kind="ExternalInput"))
    I128B = ap(nc.dram_tensor("i128b", [128, 128], BF16, kind="ExternalInput"))
    EBC8 = ap(nc.dram_tensor("ebc8", [8, 128], BF16, kind="ExternalInput"))
    ONESB = ap(nc.dram_tensor("onesb", [1, 128], BF16, kind="ExternalInput"))
    ONESF = ap(nc.dram_tensor("onesf", [1, 128], F32, kind="ExternalInput"))

    OUT = ap(nc.dram_tensor("out", [L, D], F32, kind="ExternalOutput"))

    def internal(name, shape, dt):
        kind = "ExternalOutput" if name in debug else "Internal"
        return ap(nc.dram_tensor(name, shape, dt, kind=kind))

    WIN_BF = internal("win_bf", [DPROJ, D], BF16)
    WOUTT = internal("woutt", [DIN, D], BF16)
    WBASET = internal("wbaset", [D, D], BF16)
    WOUT_BF = internal("wout_bf", [D, DIN], BF16)
    WBASE_BF = internal("wbase_bf", [D, D], BF16)
    A_BF = internal("a_bf", [128, D], BF16)
    B_BF = internal("b_bf", [8 * D, 16], BF16)
    WG_BF = internal("wg_bf", [9, D], BF16)
    XLNT = internal("xlnt", [L, D], BF16)
    ZX = internal("zx", [NSTRIP, 128, L], BF16)
    CS_SP = internal("cs_sp", [128, L], F32)
    QN_SP = internal("qn_sp", [128, L], F32)
    OD_SP = internal("od_sp", [128, L], BF16)
    EA_SP = internal("ea_sp", [NCH, 128], F32)
    Y_SP = internal("y_sp", [64, 128, L], BF16)
    ATTN_SP = internal("attn_sp", [32, 128, L], BF16)
    X2_SP = internal("x2_sp", [L, D], F32)
    X2LNT = internal("x2lnt", [L, D], BF16)
    FFN_SP = internal("ffn_sp", [32, 128, L], BF16)

    big = ExitStack()
    with big:
        E = big.enter_context
        S = {}
        for sn in ["bc_r0", "bc_r1", "xt_r0", "xt_r1", "xt_r2",
                   "bt_r0", "bt_r1", "bt_r2", "bt_r3",
                   "csf_r0", "csf_r1", "odf_r0", "odf_r1",
                   "eaf_r0", "eaf_r1", "ysp_r0", "ysp_r1", "bdd", "bpea",
                   "cast_r0", "cast_r1", "a0ld_r0", "a0ld_r1",
                   "a0st_r0", "a0st_r1", "wt_r0", "wt_r1",
                   "a1sp_r0", "a1sp_r1", "a1sp_r2", "a1sp_r3",
                   "p0", "p0t", "p0v", "cast2",
                   "a0ac", "a0v", "xlf",
                   "abWt", "abpe", "abev", "abac", "abdv", "abps", "absp",
                   "a1pe", "a1ev", "a1cv", "a1ac",
                   "bld",
                   "bpeB", "bpeG", "bpeS", "bpeY",
                   "bgev", "bdvE", "bacX", "bodv", "bwof", "bmtt", "bsr",
                   "bsbf", "bev", "bear",
                   "cpe", "cdv", "cac", "cevs",
                   "cx2v", "cx2a",
                   "dpe", "ddv", "dac", "dtrX", "dtrF", "dout"]:
            S[sn] = Cnt(E(nc.semaphore("s_" + sn)))
        for sn in ["zl_r0", "zl_r1", "xl_r0", "xl_r1",
                   "yl_r0", "as_r0", "as_r1",
                   "xt2_r0", "xt2_r1", "x2_r0", "x2_r1",
                   "xl2_r0", "xl2_r1", "ffs_r0", "ffs_r1",
                   "xf_r0", "xf_r1", "ou_r0", "ou_r1", "dbl", "ft2_r0", "ft2_r1", "wb2_r0", "wb2_r1"]:
            S[sn] = Cnt(E(nc.semaphore("s_" + sn)))
        for alias, orig in [("cw_r0", "wt_r0"), ("cw_r1", "wt_r1"),
                            ("at_r0", "xt_r2"), ("at_r1", "bt_r0"),
                            ("ft_r0", "odf_r1"), ("ft_r1", "abWt"),
                            ("wb_r0", "abps"), ("wb_r1", "abpe")]:
            S[alias] = S[orig]

        sb_ones_f = E(nc.sbuf_tensor("c_ones_f", [33, 128], F32))
        sb_ones_b = E(nc.sbuf_tensor("c_ones_b", [33, 128], BF16))
        sb_tril = E(nc.sbuf_tensor("c_tril", [128, 512], BF16))
        sb_i128f = E(nc.sbuf_tensor("c_i128f", [128, 128], F32))
        sb_i128b = E(nc.sbuf_tensor("c_i128b", [128, 128], BF16))
        sb_bthr = E(nc.sbuf_tensor("c_bthr", [128, 1], F32))
        sb_convw = E(nc.sbuf_tensor("c_convw", [128, 80 * CK], F32))
        sb_convb = E(nc.sbuf_tensor("c_convb", [128, 80], F32))
        sb_dtb = E(nc.sbuf_tensor("c_dtb", [128, 1], F32))
        sb_onecol = E(nc.sbuf_tensor("c_onecol", [128, 1], F32))
        sb_epscol = E(nc.sbuf_tensor("c_epscol", [128, 1], F32))
        sb_alog = E(nc.sbuf_tensor("c_alog", [128, 1], F32))
        sb_ealog = E(nc.sbuf_tensor("c_ealog", [128, 1], F32))
        sb_dexp = E(nc.sbuf_tensor("c_dexp", [128, 64], F32))
        sb_dcols = E(nc.sbuf_tensor("c_dcols", [128, 128], F32))
        sb_nrmw = E(nc.sbuf_tensor("c_nrmw", [128, 64], F32))
        sb_bbase = E(nc.sbuf_tensor("c_bbase", [128, 32], F32))
        sQ = ExitStack()
        sb_qT = sQ.enter_context(nc.sbuf_tensor("p_qT", [128, L], F32))
        sb_ddtT = sQ.enter_context(nc.sbuf_tensor("p_ddtT", [128, L], F32))

        psF = E(nc.psum_tensor([128, 3584], F32))
        psB16 = E(nc.psum_tensor([128, 1024], BF16))

        def bank(i, o=0, w=512, p0=0, p1=128):
            return psF[p0:p1, i * 512 + o:i * 512 + o + w]

        castR = RingSem([S["cast_r0"], S["cast_r1"]])
        a0ldR = RingSem([S["a0ld_r0"], S["a0ld_r1"]])
        a0stR = RingSem([S["a0st_r0"], S["a0st_r1"]])
        wtR = RingSem([S["wt_r0"], S["wt_r1"]])
        a1spR = RingSem([S["a1sp_r0"], S["a1sp_r1"], S["a1sp_r2"], S["a1sp_r3"]])

        with nc.Block("p0") as blk:
            @blk.scalar
            def _(sc):
                def slow(dst, src):
                    return sc.dma_start(dst, src, allow_slow_non_contiguous=True)
                for dst, src in [(sb_ones_f[0:1, :], ONESF), (sb_ones_b[0:1, :], ONESB),
                                 (sb_ones_f[32:33, :], ONESF), (sb_ones_b[32:33, :], ONESB),
                                 (sb_tril[:], TRIL), (sb_i128f[:], I128F),
                                 (sb_i128b[:], I128B),
                                 (sb_dtb[:], DTB.rearrange("(p o) -> p o", o=1)),
                                 (sb_alog[:], ALOG.rearrange("(p o) -> p o", o=1)),
                                 (sb_nrmw[:], NRMW.rearrange("(s p) -> p s", p=128)),
                                 (sb_bbase[:], BBASE.rearrange("(s p) -> p s", p=128)),
                                 (sb_dcols[0:1, :], DHEAD)]:
                    S["p0"].inc(slow(dst, src), 16)
                cw = CONVW.rearrange("(s p) k -> p s k", p=128)
                cwd = sb_convw[:].rearrange("p (s k) -> p s k", k=CK)
                for i in range(4):
                    S["p0"].inc(sc.dma_start(cwd[:, i * 20:(i + 1) * 20, :],
                                             cw[:, i * 20:(i + 1) * 20, :]), 16)
                S["p0"].inc(sc.dma_start(sb_bthr[:], BTHR.broadcast_to([128, 1])
                                         if hasattr(BTHR, "broadcast_to") else BTHR), 16)
                cb = CONVB.rearrange("(s p) -> p s", p=128)
                for i in range(2):
                    S["p0"].inc(slow(sb_convb[:, i * 40:(i + 1) * 40],
                                     cb[:, i * 40:(i + 1) * 40]), 16)
                sc.wait_ge(S["p0"].sem, S["p0"].n)
                S["p0"].inc(sc.activation(sb_ealog[:], sb_alog[:], AF.Exp))

            @blk.gpsimd
            def _(gp):
                castR.inc(gp.dma_start(WIN_BF[DTSTRIP * 128:(DTSTRIP + 1) * 128, :],
                                       WIN[DTSTRIP * 128:(DTSTRIP + 1) * 128, :]), 0)
                for gi, (sa, _sb) in enumerate(a1_groups):
                    j = gi + 1
                    if j >= 2:
                        castR.wait_done(gp, j - 2)
                    castR.inc(gp.dma_start(WIN_BF[sa * 128:sa * 128 + 256, :],
                                           WIN[sa * 128:sa * 128 + 256, :]), j)

            @blk.tensor
            def _(t):
                t.wait_ge(S["p0"].sem, 16 * 19 + 1)
                S["p0t"].inc(t.matmul(bank(6, 0, 128), sb_ones_f[0:1, 0:128],
                                      sb_dcols[0:1, :], start=True, stop=True))

            @blk.vector
            def _(v):
                v.memset(sb_onecol[:], 1.0)
                v.memset(sb_epscol[:], EPS)
                v.wait_ge(S["p0t"].sem, 1)
                S["p0v"].inc(v.tensor_copy(sb_dcols[:], bank(6, 0, 128)))
                v.wait_ge(S["p0v"].sem, 1)
                dc = sb_dcols[:].rearrange("p (g two) -> p g two", two=2)
                S["p0v"].inc(v.tensor_copy(sb_dexp[0:64, :], dc[0:64, :, 0]))
                S["p0v"].inc(v.tensor_copy(sb_dexp[64:128, :], dc[64:128, :, 1]))

        saX = ExitStack()
        sb_xlnf = saX.enter_context(nc.sbuf_tensor("a_xlnf", [128, KD * L], BF16))

        with ExitStack() as sa0:
            EA = sa0.enter_context
            xt = [EA(nc.sbuf_tensor(f"a0_xt{i}", [128, D], F32)) for i in range(2)]
            sqb = EA(nc.sbuf_tensor("a0_sq", [128, D], F32))
            stat = EA(nc.sbuf_tensor("a0_stat", [128, 2 * NLT], F32))
            rstd = EA(nc.sbuf_tensor("a0_rstd", [128, NLT], F32))
            msh = EA(nc.sbuf_tensor("a0_msh", [128, NLT], F32))
            mb = EA(nc.sbuf_tensor("a0_m", [128, NLT], F32))
            vb = EA(nc.sbuf_tensor("a0_v", [128, NLT], F32))
            xlnb = EA(nc.sbuf_tensor("a0_xln", [128, D], BF16))

            with nc.Block("a0") as blk:
                @blk.sync
                def _(sy):
                    for i in range(NLT):
                        if i >= 2:
                            sy.wait_ge(S["a0v"].sem, 2 * (i - 1))
                        a0ldR.inc(sy.dma_start(xt[i % 2][:], X[i * 128:(i + 1) * 128, :]), i)
                    for i in range(NLT):
                        sy.wait_ge(S["a0v"].sem, 2 * NLT + 5)
                        if i >= 2:
                            sy.wait_ge(S["a0ac"].sem, NLT + 2 + (i - 1))
                        a0ldR.inc(sy.dma_start(xt[i % 2][:], X[i * 128:(i + 1) * 128, :]), NLT + i)
                        if i >= 1:
                            sy.wait_ge(S["a0ac"].sem, NLT + 2 + i)
                            a0stR.inc(sy.dma_start(XLNT[(i - 1) * 128:i * 128, :],
                                                   xlnb[:]), i - 1)
                    sy.wait_ge(S["a0ac"].sem, NLT + 2 + NLT)
                    a0stR.inc(sy.dma_start(XLNT[(NLT - 1) * 128:NLT * 128, :],
                                           xlnb[:]), NLT - 1)
                    a0stR.wait_all(sy, NLT)
                    for k in range(KD):
                        S["xlf"].inc(sy.dma_start_transpose(
                            sb_xlnf[:, k * L:(k + 1) * L], XLNT[:, k * 128:(k + 1) * 128]), 16)

                @blk.scalar
                def _(sc):
                    for i in range(NLT):
                        a0ldR.wait_done(sc, i)
                        if i >= 1:
                            sc.wait_ge(S["a0v"].sem, 2 * i)
                        S["a0ac"].inc(sc.activation(sqb[:], xt[i % 2][:], AF.Square))
                    sc.wait_ge(S["a0v"].sem, 2 * NLT + 3)
                    S["a0ac"].inc(sc.activation(rstd[:], vb[:], AF.Ln, bias=sb_epscol[:, 0:1]))
                    sc.wait_ge(S["a0ac"].sem, NLT + 1)
                    S["a0ac"].inc(sc.activation(rstd[:], rstd[:], AF.Exp, scale=-0.5))
                    for i in range(NLT):
                        a0ldR.wait_done(sc, NLT + i)
                        sc.wait_ge(S["a0v"].sem, 2 * NLT + 5)
                        if i >= 1:
                            a0stR.wait_done(sc, i - 1)
                        S["a0ac"].inc(sc.activation(xlnb[:], xt[i % 2][:], AF.Identity,
                                                    bias=msh[:, i:i + 1], scale=rstd[:, i:i + 1]))

                @blk.vector
                def _(v):
                    for i in range(NLT):
                        v.wait_ge(S["a0ac"].sem, i + 1)
                        S["a0v"].inc(v.reduce_sum(stat[:, i:i + 1], xt[i % 2][:], axis=AX.X))
                        S["a0v"].inc(v.reduce_sum(stat[:, NLT + i:NLT + i + 1],
                                                  sqb[:], axis=AX.X))
                    v.wait_ge(S["a0v"].sem, 2 * NLT)
                    S["a0v"].inc(v.tensor_scalar(mb[:], stat[:, 0:NLT], 1.0 / D, None, ALU.mult))
                    v.wait_ge(S["a0v"].sem, 2 * NLT + 1)
                    S["a0v"].inc(v.tensor_tensor(vb[:], mb[:], mb[:], ALU.mult))
                    v.wait_ge(S["a0v"].sem, 2 * NLT + 2)
                    S["a0v"].inc(v.scalar_tensor_tensor(vb[:], stat[:, NLT:2 * NLT], 1.0 / D,
                                                        vb[:], ALU.mult, ALU.subtract))
                    v.wait_ge(S["a0ac"].sem, NLT + 2)
                    S["a0v"].inc(v.tensor_tensor(msh[:], mb[:], rstd[:], ALU.mult))
                    v.wait_ge(S["a0v"].sem, 2 * NLT + 4)
                    S["a0v"].inc(v.tensor_scalar(msh[:], msh[:], -1.0, None, ALU.mult))

        with ExitStack() as sab:
            EA = sab.enter_context
            wt144 = EA(nc.sbuf_tensor("ab_wt", [128, KD * 128], BF16))
            dtraw = EA(nc.sbuf_tensor("ab_dtraw", [128, L], F32))
            scr1 = EA(nc.sbuf_tensor("ab_s1", [128, L], F32))
            scr2 = EA(nc.sbuf_tensor("ab_s2", [128, L], F32))
            scr3 = EA(nc.sbuf_tensor("ab_s3", [128, L], F32))
            qnb = EA(nc.sbuf_tensor("ab_qn", [128, L], F32))
            odb = EA(nc.sbuf_tensor("ab_od", [128, L], BF16))
            z256 = EA(nc.sbuf_tensor("ab_z", [128, CH], F32))
            eav = EA(nc.sbuf_tensor("ab_ea", [128, NCH], F32))
            eat = EA(nc.sbuf_tensor("ab_eat", [NCH, 128], F32))

            NT128 = L // 128
            AB_EVAC = NTB
            AB_RELU = AB_EVAC + 1
            AB_DT = AB_RELU + 1
            AB_DA = AB_DT + 1
            AB_MZ = AB_DA + 1
            AB_CS = AB_MZ + NCH
            AB_DDT = AB_CS + 1
            AB_Q = AB_DDT + 1
            AB_TEV = AB_Q + 2 * NT128
            AB_EAT = AB_TEV + 1
            AC_SP = 4
            AC_DDX = AC_SP + NCH
            AC_OD = AC_DDX + 1
            AC_EA = AC_OD + 1
            AC_LNDT = AC_EA + 1

            with nc.Block("a0b") as blk:
                @blk.sync
                def _(sy):
                    castR.wait_done(sy, 0)
                    r0 = DTSTRIP * 128
                    for k in range(KD):
                        S["abWt"].inc(sy.dma_start_transpose(
                            wt144[:, k * 128:(k + 1) * 128],
                            WIN_BF[r0:r0 + 128, k * 128:(k + 1) * 128]), 16)

                @blk.tensor
                def _(t):
                    t.wait_ge(S["abWt"].sem, 16 * KD)
                    t.wait_ge(S["xlf"].sem, 16 * KD)
                    for tb in range(NTB):
                        for k in range(KD):
                            mm = t.matmul(bank(tb), wt144[:, k * 128:(k + 1) * 128],
                                          sb_xlnf[:, k * L + tb * 512:k * L + (tb + 1) * 512],
                                          start=(k == 0), stop=(k == KD - 1))
                        S["abpe"].inc(mm)
                    t.wait_ge(S["abdv"].sem, AB_DDT)
                    for i in range(NT128):
                        if i >= 2:
                            t.wait_ge(S["abev"].sem, i - 1)
                        S["abps"].inc(t.transpose(bank(4 + (i % 2), 0, 128),
                                                  scr2[:, i * 128:(i + 1) * 128], sb_i128f[:]))
                    t.wait_ge(S["abdv"].sem, AB_Q)
                    for i in range(NT128):
                        t.wait_ge(S["abev"].sem, NT128 + i - 1)
                        S["abps"].inc(t.transpose(bank(4 + (i % 2), 0, 128),
                                                  qnb[:, i * 128:(i + 1) * 128], sb_i128f[:]))
                    t.wait_ge(S["abac"].sem, AC_EA)
                    t.wait_ge(S["abev"].sem, 2 * NT128)
                    S["abps"].inc(t.transpose(bank(4, 0, 128, 0, NCH), eav[:], sb_i128f[:]))

                @blk.scalar
                def _(sc):
                    sc.wait_ge(S["abdv"].sem, AB_EVAC)
                    S["abac"].inc(sc.activation(scr1[:], dtraw[:], AF.Identity, bias=sb_dtb[:, 0:1]))
                    sc.wait_ge(S["abac"].sem, 1)
                    S["abac"].inc(sc.activation(scr2[:], scr1[:], AF.Abs))
                    sc.wait_ge(S["abac"].sem, 2)
                    S["abac"].inc(sc.activation(scr3[:], scr2[:], AF.Exp, scale=-1.0))
                    sc.wait_ge(S["abac"].sem, 3)
                    S["abac"].inc(sc.activation(scr2[:], scr3[:], AF.Ln, bias=sb_onecol[:, 0:1]))
                    sc.wait_ge(S["abdv"].sem, AB_CS)
                    for c in range(NCH):
                        S["abac"].inc(sc.activation(scr2[:, c * CH:(c + 1) * CH],
                                                    scr3[:, c * CH:(c + 1) * CH],
                                                    AF.Exp, scale=-1.0,
                                                    bias=scr3[:, c * CH + CH - 1:c * CH + CH]))
                    S["abac"].inc(sc.activation(odb[:], scr3[:], AF.Exp))
                    S["abac"].inc(sc.activation(eav[:], scr3[:, CH - 1::CH], AF.Exp))
                    sc.wait_ge(S["abdv"].sem, AB_DDT)
                    S["abac"].inc(sc.activation(scr1[:], scr1[:], AF.Ln))

                @blk.vector
                def _(v):
                    for tb in range(NTB):
                        v.wait_ge(S["abpe"].sem, tb + 1)
                        S["abdv"].inc(v.tensor_copy(dtraw[:, tb * 512:(tb + 1) * 512], bank(tb)))
                    v.wait_ge(S["abac"].sem, AC_SP)
                    S["abdv"].inc(v.tensor_scalar(scr3[:], scr1[:], 0.0, None, ALU.max))
                    v.wait_ge(S["abdv"].sem, AB_RELU)
                    S["abdv"].inc(v.tensor_tensor(scr1[:], scr2[:], scr3[:], ALU.add))
                    v.wait_ge(S["abdv"].sem, AB_DT)
                    S["abdv"].inc(v.tensor_scalar(scr2[:], scr1[:], sb_ealog[:, 0:1], -1.0,
                                                  ALU.mult, ALU.mult))
                    S["abdv"].inc(v.memset(z256[:], 0.0))
                    v.wait_ge(S["abdv"].sem, AB_MZ)
                    for c in range(NCH):
                        S["abdv"].inc(v.tensor_tensor_scan(scr3[:, c * CH:(c + 1) * CH],
                                                           scr2[:, c * CH:(c + 1) * CH],
                                                           z256[:], 0.0, ALU.add, ALU.add))
                    v.wait_ge(S["abac"].sem, AC_DDX)
                    S["abdv"].inc(v.tensor_tensor(scr2[:], scr2[:], scr1[:], ALU.mult))
                    v.wait_ge(S["abac"].sem, AC_LNDT)
                    S["abdv"].inc(v.tensor_tensor(qnb[:], scr3[:], scr1[:], ALU.subtract))
                    for i in range(NT128):
                        v.wait_ge(S["abps"].sem, i + 1)
                        S["abev"].inc(v.tensor_copy(sb_ddtT[:, i * 128:(i + 1) * 128],
                                                    bank(4 + (i % 2), 0, 128)))
                    for i in range(NT128):
                        v.wait_ge(S["abps"].sem, NT128 + i + 1)
                        S["abev"].inc(v.tensor_copy(sb_qT[:, i * 128:(i + 1) * 128],
                                                    bank(4 + (i % 2), 0, 128)))
                    v.wait_ge(S["abps"].sem, 2 * NT128 + 1)
                    S["abev"].inc(v.tensor_copy(eat[:], bank(4, 0, 128, 0, NCH)))

                @blk.gpsimd
                def _(gp):
                    gp.wait_ge(S["abdv"].sem, AB_CS)
                    S["absp"].inc(gp.dma_start(CS_SP[:, :], scr3[:]), 16)
                    gp.wait_ge(S["abac"].sem, AC_OD)
                    S["absp"].inc(gp.dma_start(OD_SP[:, :], odb[:]), 16)
                    gp.wait_ge(S["abev"].sem, 2 * NT128 + 1)
                    S["absp"].inc(gp.dma_start(EA_SP[:, :], eat[:]), 16)
                    gp.wait_ge(S["absp"].sem, 48)

            ab_guards = [(S["absp"], 48), (S["abev"], 2 * NT128 + 1),
                         (S["abac"], AC_LNDT), (S["abdv"], AB_Q),
                         (S["abpe"], NTB), (S["abps"], 2 * NT128 + 1)]

        with ExitStack() as sa1:
            EA = sa1.enter_context
            wtbuf = EA(nc.sbuf_tensor("a1_wt", [128, 2 * KD * 256], BF16))
            rawring = [EA(nc.sbuf_tensor(f"a1_raw{i}", [128, L + 3], BF16)) for i in range(4)]
            t0 = EA(nc.sbuf_tensor("a1_t0", [128, L], BF16))
            t1 = EA(nc.sbuf_tensor("a1_t1", [128, L], BF16))

            def guards(eng):
                for cnt, val in ab_guards:
                    eng.wait_ge(cnt.sem, val)

            nconv_before = {}
            nc_ = 0
            for s_ in strip_order:
                nconv_before[s_] = nc_
                if s_ >= 64:
                    nc_ += 1
            NCONV = nc_

            with nc.Block("a1") as blk:
                @blk.sync
                def _(sy):
                    guards(sy)
                    for g in range(NGRP):
                        sa, sbb = a1_groups[g]
                        sl = g % 2
                        castR.wait_done(sy, g + 1)
                        if g >= 2:
                            sy.wait_ge(S["a1pe"].sem, (g - 1) * NTB)
                        for k in range(KD):
                            wtR.inc(sy.dma_start_transpose(
                                wtbuf[:, sl * KD * 256 + k * 256:sl * KD * 256 + (k + 1) * 256],
                                WIN_BF[sa * 128:sa * 128 + 256, k * 128:(k + 1) * 128]), g)

                @blk.tensor
                def _(t):
                    guards(t)
                    for g in range(NGRP):
                        sl = g % 2
                        wtR.wait_done(t, g, uses=KD * (g // 2 + 1))
                        for tb in range(NTB):
                            s = g * NTB + tb
                            rr = s % 3
                            if s >= 3:
                                t.wait_ge(S["a1ev"].sem, 2 * (s - 2))
                            for k in range(KD):
                                w0 = sl * KD * 256 + k * 256
                                mm = t.matmul(bank(2 * rr),
                                              wtbuf[:, w0:w0 + 128],
                                              sb_xlnf[:, k * L + tb * 512:k * L + (tb + 1) * 512],
                                              start=(k == 0), stop=(k == KD - 1))
                                mm = t.matmul(bank(2 * rr + 1),
                                              wtbuf[:, w0 + 128:w0 + 256],
                                              sb_xlnf[:, k * L + tb * 512:k * L + (tb + 1) * 512],
                                              start=(k == 0), stop=(k == KD - 1))
                            S["a1pe"].inc(mm)

                @blk.vector
                def _(v):
                    guards(v)
                    for rr_ in range(4):
                        v.memset(rawring[rr_][:, 0:3], 0.0)
                    ncv = 0
                    for g in range(NGRP):
                        for tb in range(NTB):
                            s = g * NTB + tb
                            rr = s % 3
                            v.wait_ge(S["a1pe"].sem, s + 1)
                            for j in range(2):
                                strip = a1_groups[g][j]
                                idx = spill_idx[strip]
                                r = idx % 4
                                if tb == 0 and idx >= 4:
                                    a1spR.wait_done(v, idx - 4)
                                S["a1ev"].inc(v.tensor_copy(
                                    rawring[r][:, 3 + tb * 512:3 + (tb + 1) * 512],
                                    bank(2 * rr + j)))
                        if a1_groups[g][0] >= 64:
                            v.wait_ge(S["a1ev"].sem, 2 * NTB * (g + 1))
                        for j in range(2):
                            strip = a1_groups[g][j]
                            if strip < 64:
                                continue
                            idx = spill_idx[strip]
                            r = idx % 4
                            wc = (strip - 64) * CK
                            if nconv_before[strip] >= 1:
                                v.wait_ge(S["a1ac"].sem, nconv_before[strip])
                            raw = rawring[r]
                            S["a1cv"].inc(v.tensor_scalar(t0[:], raw[:, 0:L],
                                                          sb_convw[:, wc:wc + 1], None, ALU.mult))
                            v.wait_ge(S["a1cv"].sem, ncv + 1)
                            S["a1cv"].inc(v.scalar_tensor_tensor(t1[:], raw[:, 1:1 + L],
                                                                 sb_convw[:, wc + 1:wc + 2],
                                                                 t0[:], ALU.mult, ALU.add))
                            v.wait_ge(S["a1cv"].sem, ncv + 2)
                            S["a1cv"].inc(v.scalar_tensor_tensor(t0[:], raw[:, 2:2 + L],
                                                                 sb_convw[:, wc + 2:wc + 3],
                                                                 t1[:], ALU.mult, ALU.add))
                            v.wait_ge(S["a1cv"].sem, ncv + 3)
                            S["a1cv"].inc(v.scalar_tensor_tensor(t1[:], raw[:, 3:3 + L],
                                                                 sb_convw[:, wc + 3:wc + 4],
                                                                 t0[:], ALU.mult, ALU.add))
                            ncv += 4

                @blk.scalar
                def _(sc):
                    guards(sc)
                    nsilu = 0
                    for g in range(NGRP):
                        for j in range(2):
                            strip = a1_groups[g][j]
                            if strip < 64:
                                continue
                            idx = spill_idx[strip]
                            r = idx % 4
                            nsilu += 1
                            sc.wait_ge(S["a1cv"].sem, 4 * nsilu)
                            cb = strip - 64
                            S["a1ac"].inc(sc.activation(rawring[r][:, 3:3 + L], t1[:],
                                                        AF.Silu, bias=sb_convb[:, cb:cb + 1]))

                @blk.gpsimd
                def _(gp):
                    guards(gp)
                    nsilu = 0
                    for g in range(NGRP):
                        for j in range(2):
                            strip = a1_groups[g][j]
                            idx = spill_idx[strip]
                            r = idx % 4
                            if strip >= 64:
                                nsilu += 1
                                gp.wait_ge(S["a1ac"].sem, nsilu)
                            else:
                                gp.wait_ge(S["a1ev"].sem, 2 * NTB * (g + 1))
                            a1spR.inc(gp.dma_start(ZX[strip, :, :],
                                                    rawring[r][:, 3:3 + L]), idx)

            ns_ = len(strip_order)
            a1_guards = [(S["a1sp_r0"], 16 * ((ns_ + 3) // 4)), (S["a1sp_r1"], 16 * ((ns_ + 2) // 4)),
                         (S["a1sp_r2"], 16 * ((ns_ + 1) // 4)), (S["a1sp_r3"], 16 * (ns_ // 4)),
                         (S["a1ev"], 2 * NTB * NGRP),
                         (S["a1ac"], NCONV), (S["a1cv"], 4 * NCONV),
                         (S["a1pe"], NTB * NGRP)]

        saX.close()

        if PH >= 1:
            with ExitStack() as sB:
                EB = sB.enter_context
                xtr = [EB(nc.sbuf_tensor(f"b_xt{i}", [128, 8192], BF16)) for i in range(3)]
                btr = [EB(nc.sbuf_tensor(f"b_bt{i}", [128, 1024], BF16)) for i in range(4)]
                bcb = [EB(nc.sbuf_tensor(f"b_bc{i}", [128, 4096], BF16)) for i in range(2)]
                csfl_t = EB(nc.sbuf_tensor("b_csf", [33, 2048], F32))
                odfl_t = EB(nc.sbuf_tensor("b_odf", [33, 2048], BF16))
                eafl_t = EB(nc.sbuf_tensor("b_eaf", [33, 128], F32))
                csfl = [csfl_t[0:1, :], csfl_t[32:33, :]]
                odfl = [odfl_t[0:1, :], odfl_t[32:33, :]]
                eafl = [eafl_t[0:1, :], eafl_t[32:33, :]]
                earep = [EB(nc.sbuf_tensor(f"b_ear{i}", [128, 128], F32)) for i in range(2)]
                gm = [EB(nc.sbuf_tensor(f"b_gm{i}", [128, 512], BF16)) for i in range(2)]
                earg = [EB(nc.sbuf_tensor(f"b_earg{i}", [128, 512], BF16)) for i in range(2)]
                ee = [EB(nc.sbuf_tensor(f"b_ee{i}", [128, 512], BF16)) for i in range(2)]
                mbf = [EB(nc.sbuf_tensor(f"b_m{i}", [128, 512], BF16)) for i in range(2)]
                wof = [EB(nc.sbuf_tensor(f"b_wof{i}", [128, 256], BF16)) for i in range(2)]
                odrep = [EB(nc.sbuf_tensor(f"b_odr{i}", [128, 256], BF16)) for i in range(2)]
                bddb = [EB(nc.sbuf_tensor(f"b_bdd{i}", [128, 256], BF16)) for i in range(2)]
                sbfb = [EB(nc.sbuf_tensor(f"b_sbf{i}", [128, 64], BF16)) for i in range(4)]
                smas = EB(nc.sbuf_tensor("b_s", [128, 8192], F32))
                yst = [EB(nc.sbuf_tensor(f"b_yst{i}", [128, 2048], BF16)) for i in range(2)]

                bcR = RingSem([S["bc_r0"], S["bc_r1"]])
                xtRg = RingSem([S["xt_r0"], S["xt_r1"], S["xt_r2"]])
                btRg = RingSem([S["bt_r0"], S["bt_r1"], S["bt_r2"], S["bt_r3"]])
                csfR = RingSem([S["csf_r0"], S["csf_r1"]])
                odfR = RingSem([S["odf_r0"], S["odf_r1"]])
                eafR = RingSem([S["eaf_r0"], S["eaf_r1"]])
                yspR = RingSem([S["ysp_r0"], S["ysp_r1"]])

                def bguards(eng):
                    for cnt, val in a1_guards:
                        eng.wait_ge(cnt.sem, val)
                    eng.wait_ge(S["absp"].sem, 48)

                XBCV = ZX[64:128, :, :].rearrange("s p l -> (s p) l")
                BTV = ZX[128:136, :, :].rearrange("s p l -> (s p) l")
                BCV = ZX[128:144, :, :].rearrange("s p l -> p s l")

                NGI = NCH * 16

                def flat_load(gp, gi):
                    c0, i = gi // 16, gi % 16
                    if gi >= 2:
                        gp.wait_ge(S["bpeB"].sem, 8 * (gi - 1))
                    csfR.inc(gp.dma_start(
                        csfl[gi % 2].rearrange("o (h l) -> o h l", l=CH),
                        CS_SP[8 * i:8 * i + 8, c0 * CH:(c0 + 1) * CH].rearrange("(o h) l -> o h l", o=1)), gi)
                    odfR.inc(gp.dma_start(
                        odfl[gi % 2].rearrange("o (h l) -> o h l", l=CH),
                        OD_SP[8 * i:8 * i + 8, c0 * CH:(c0 + 1) * CH].rearrange("(o h) l -> o h l", o=1)), gi)

                wtt = [EB(nc.sbuf_tensor(f"b_wtt{i}", [128, 4096], BF16)) for i in range(2)]
                NWT = KDIN + KD if PH >= 2 else 0
                wtrR = RingSem([S["p0"], S["p0t"]])
                wstR = RingSem([S["p0v"], S["dtrX"]])

                def emit_wt(sy, k):
                    sy.wait_ge(S["cast2"].sem, 16 * 15)
                    if k >= 2:
                        wstR.wait_done(sy, k - 2)
                    if k < KDIN:
                        wtrR.inc(sy.dma_start_transpose(
                            wtt[k % 2][:], WOUT_BF[:, k * 128:(k + 1) * 128]), k)
                        wtrR.wait_done(sy, k)
                        wstR.inc(sy.dma_start(WOUTT[k * 128:(k + 1) * 128, :],
                                              wtt[k % 2][:]), k)
                    else:
                        kk = k - KDIN
                        wtrR.inc(sy.dma_start_transpose(
                            wtt[k % 2][:], WBASE_BF[:, kk * 128:(kk + 1) * 128]), k)
                        wtrR.wait_done(sy, k)
                        wstR.inc(sy.dma_start(WBASET[kk * 128:(kk + 1) * 128, :],
                                              wtt[k % 2][:]), k)

                with nc.Block("ssd") as blk:
                    @blk.sync
                    def _(sy):
                        bguards(sy)
                        nwt = 0
                        for c in range(NCH):
                            for half in range(2):
                                hf = 2 * c + half
                                if hf >= 3:
                                    sy.wait_ge(S["bpeY"].sem, 128 * ((hf - 3) // 2 + 1))
                                xtRg.inc(sy.dma_start_transpose(
                                    xtr[hf % 3][:], XBCV[:, hf * 128:(hf + 1) * 128]), hf)
                                if hf >= 4:
                                    sy.wait_ge(S["bsr"].sem, 128 * ((hf - 4) // 2 + 1))
                                btRg.inc(sy.dma_start_transpose(
                                    btr[hf % 4][:], BTV[:, hf * 128:(hf + 1) * 128]), hf)
                            per = (NWT + NCH - 1) // NCH if NCH else 0
                            for _w in range(per):
                                if nwt >= NWT:
                                    break
                                k = nwt
                                emit_wt(sy, k)
                                nwt += 1
                        while nwt < NWT:
                            k = nwt
                            emit_wt(sy, k)
                            nwt += 1

                    @blk.gpsimd
                    def _(gp):
                        bguards(gp)
                        if PH >= 2:
                            for r in range(8):
                                S["cast2"].inc(gp.dma_start(
                                    WOUT_BF[r * 512:(r + 1) * 512, :],
                                    WOUT[r * 512:(r + 1) * 512, :]), 16)
                            for r in range(4):
                                S["cast2"].inc(gp.dma_start(
                                    WBASE_BF[r * 1024:(r + 1) * 1024, :],
                                    WBASE[r * 1024:(r + 1) * 1024, :]), 16)
                            S["cast2"].inc(gp.dma_start(A_BF, AEXP), 16)
                            S["cast2"].inc(gp.dma_start(B_BF, BEXP), 16)
                            S["cast2"].inc(gp.dma_start(WG_BF, WGATE9), 16)
                        S["bld"].inc(gp.memset(smas[:], 0.0))
                        bcv3 = lambda i: bcb[i % 2][:].rearrange("p (s l) -> p s l", l=CH)
                        bcR.inc(gp.dma_start(bcv3(0), BCV[:, :, 0:CH]), 0)
                        if NCH > 1:
                            bcR.inc(gp.dma_start(bcv3(1), BCV[:, :, CH:2 * CH]), 1)
                        eafR.inc(gp.dma_start(eafl[0], EA_SP[0:1, :]), 0)
                        if NCH > 1:
                            eafR.inc(gp.dma_start(eafl[1], EA_SP[1:2, :]), 1)
                        for gi in range(min(2, NGI)):
                            flat_load(gp, gi)
                        for c in range(NCH):
                            for h in range(128):
                                gh = c * 128 + h
                                g = h // 16
                                if h % 8 == 7:
                                    gi = (gh + 1) // 8 + 1
                                    if gi < NGI:
                                        flat_load(gp, gi)
                                if h == 0 and c + 2 < NCH:
                                    gp.wait_ge(S["bear"].sem, c + 1)
                                    eafR.inc(gp.dma_start(eafl[(c + 2) % 2],
                                                          EA_SP[c + 2:c + 3, :]), c + 2)
                                if gh == 0:
                                    gp.wait_ge(S["bld"].sem, 1)
                                if gh >= 4:
                                    gp.wait_ge(S["bpeY"].sem, gh - 3)
                                S["bsbf"].inc(gp.tensor_copy(sbfb[h % 4][:],
                                                             smas[:, h * 64:(h + 1) * 64]))
                                gp.wait_ge(S["bacX"].sem, gh + 1)
                                gp.wait_ge(S["bgev"].sem, 2 * (c * 8 + g + 1))
                                if gh >= 2:
                                    gp.wait_ge(S["bpeY"].sem, gh - 1)
                                S["bmtt"].inc(gp.tensor_tensor(mbf[h % 2][:, 0:256],
                                                               gm[g % 2][:, 0:256],
                                                               ee[h % 2][:, 0:256], ALU.mult))
                                S["bmtt"].inc(gp.tensor_tensor(mbf[h % 2][:, 256:512],
                                                               gm[g % 2][:, 256:512],
                                                               ee[h % 2][:, 256:512], ALU.mult))
                                if h % 16 == 15:
                                    b = h // 16
                                    gb = c * 8 + b
                                    gp.wait_ge(S["bev"].sem, 64 * c + 8 * (b + 1))
                                    yspR.inc(gp.dma_start(
                                        Y_SP[8 * b:8 * b + 8, :, c * CH:(c + 1) * CH]
                                        .rearrange("s p l -> p s l"),
                                        yst[b % 2][:].rearrange("p (s l) -> p s l", l=CH)), gb)
                            if c + 2 < NCH:
                                gp.wait_ge(S["bev"].sem, 64 * (c + 1))
                                bcR.inc(gp.dma_start(bcv3(c + 2),
                                                     BCV[:, :, (c + 2) * CH:(c + 3) * CH]), c + 2)

                    @blk.tensor
                    def _(t):
                        bguards(t)

                        def emit_G(c, g):
                            gg = c * 8 + g
                            bcR.wait_done(t, c)
                            if g == 0:
                                t.wait_ge(S["bear"].sem, c + 1)
                            if gg >= 1:
                                t.wait_ge(S["bgev"].sem, 2 * gg)
                            bb = bcb[c % 2]
                            t.matmul(bank(0, 0, 256), bb[:, g * 256:g * 256 + 128],
                                     bb[:, 2048 + g * 256:2048 + (g + 1) * 256],
                                     start=True, stop=True)
                            S["bpeG"].inc(t.matmul(bank(0, 256, 256),
                                                   bb[:, g * 256 + 128:(g + 1) * 256],
                                                   bb[:, 2048 + g * 256:2048 + (g + 1) * 256],
                                                   start=True, stop=True))

                        def emit_bcast(c, hb):
                            ghb = c * 128 + hb
                            gi = ghb // 8
                            csfR.wait_done(t, gi)
                            odfR.wait_done(t, gi)
                            if ghb >= 2:
                                t.wait_ge(S["bdvE"].sem, 2 * (ghb - 1))
                                t.wait_ge(S["bodv"].sem, ghb - 1)
                            i8 = hb % 8
                            sl32 = 32 * (gi % 2)
                            t.matmul(bank(2 + hb % 2, 0, 256),
                                     sb_ones_f[sl32:sl32 + 1, 0:128],
                                     csfl[gi % 2][:, i8 * 256:(i8 + 1) * 256],
                                     start=True, stop=True)
                            S["bpeB"].inc(t.matmul(bank(2 + hb % 2, 256, 256),
                                                   sb_ones_b[sl32:sl32 + 1, 0:128],
                                                   odfl[gi % 2][:, i8 * 256:(i8 + 1) * 256],
                                                   start=True, stop=True))

                        def emit_states(c, h):
                            gh = c * 128 + h
                            sbk = 1 if h % 2 == 0 else 6
                            t.wait_ge(S["bdd"].sem, 2 * (gh + 1))
                            if gh >= 2:
                                t.wait_ge(S["bsr"].sem, gh - 1)
                            t.matmul(bank(sbk, 0, 64), bddb[h % 2][:, 0:128],
                                     xtr[(2 * c) % 3][:, h * 64:(h + 1) * 64],
                                     start=True, stop=False)
                            S["bpeS"].inc(t.matmul(bank(sbk, 0, 64), bddb[h % 2][:, 128:256],
                                                   xtr[(2 * c + 1) % 3][:, h * 64:(h + 1) * 64],
                                                   start=False, stop=True))

                        def emit_Y(c, h):
                            gh = c * 128 + h
                            pp = h // 2
                            gpp = c * 64 + pp
                            t.wait_ge(S["bmtt"].sem, 2 * (gh + 1))
                            if c > 0:
                                t.wait_ge(S["bwof"].sem, gh + 1)
                                t.wait_ge(S["bsbf"].sem, gh + 1)
                            if h % 2 == 0 and gpp >= 2:
                                t.wait_ge(S["bev"].sem, gpp - 1)
                            reg = bank(4 + pp % 2, 0, 256, p0=(h % 2) * 64, p1=(h % 2) * 64 + 64)
                            last_d = (c == 0)
                            t.matmul(reg, xtr[(2 * c) % 3][:, h * 64:(h + 1) * 64],
                                     mbf[h % 2][:, 0:256], start=True, stop=False,
                                     skip_group_check=True)
                            mm = t.matmul(reg, xtr[(2 * c + 1) % 3][:, h * 64:(h + 1) * 64],
                                          mbf[h % 2][:, 256:512], start=False,
                                          stop=(last_d and h % 2 == 1), skip_group_check=True)
                            if c > 0:
                                mm = t.matmul(reg, sbfb[h % 4][:], wof[h % 2][:],
                                              start=False, stop=(h % 2 == 1),
                                              skip_group_check=True)
                            S["bpeY"].inc(mm)

                        for c in range(NCH):
                            eafR.wait_done(t, c)
                            if c >= 1:
                                t.wait_ge(S["bear"].sem, c)
                                t.wait_ge(S["bgev"].sem, 16 * c)
                                t.wait_ge(S["bsr"].sem, 128 * c)
                            ea32 = 32 * (c % 2)
                            S["bpea"].inc(t.matmul(bank(0, 0, 128),
                                                   sb_ones_f[ea32:ea32 + 1, 0:128],
                                                   eafl[c % 2], start=True, stop=True))
                            if c == 0:
                                xtRg.wait_done(t, 0)
                                xtRg.wait_done(t, 1)
                            else:
                                xtRg.wait_done(t, 2 * c)
                                xtRg.wait_done(t, 2 * c + 1)
                            emit_G(c, 0)
                            emit_bcast(c, 0)
                            emit_bcast(c, 1)
                            for h in range(128):
                                if (h + 2) < 128 and (h + 2) % 16 == 0:
                                    emit_G(c, (h + 2) // 16)
                                if h + 2 < 128:
                                    emit_bcast(c, h + 2)
                                emit_states(c, h)
                                emit_Y(c, h)

                    @blk.vector
                    def _(v):
                        bguards(v)
                        for c in range(NCH):
                            v.wait_ge(S["bpea"].sem, c + 1)
                            S["bear"].inc(v.tensor_copy(earep[c % 2][:], bank(0, 0, 128)))
                            if c == 0:
                                btRg.wait_done(v, 0)
                                btRg.wait_done(v, 1)
                            else:
                                btRg.wait_done(v, 2 * c)
                                btRg.wait_done(v, 2 * c + 1)
                            for h in range(-1, 128):
                                gh = c * 128 + h
                                if h + 1 < 128:
                                    hn = h + 1
                                    ghn = gh + 1
                                    gn = hn // 16
                                    if hn % 16 == 0:
                                        g = gn
                                        gg = c * 8 + g
                                        v.wait_ge(S["bpeG"].sem, gg + 1)
                                        if gg >= 2:
                                            v.wait_ge(S["bmtt"].sem, 32 * (gg - 1))
                                        S["bgev"].inc(v.tensor_tensor(
                                            gm[g % 2][:, 0:256], bank(0, 0, 256),
                                            sb_tril[:, 0:256], ALU.mult))
                                        S["bgev"].inc(v.tensor_tensor(
                                            gm[g % 2][:, 256:512], bank(0, 256, 256),
                                            sb_tril[:, 256:512], ALU.mult))
                                    if ghn >= 2:
                                        v.wait_ge(S["bpeS"].sem, ghn - 1)
                                    S["bdd"].inc(v.tensor_scalar(
                                        bddb[hn % 2][:, 0:128],
                                        btr[(2 * c) % 4][:, gn * 128:(gn + 1) * 128],
                                        sb_ddtT[:, (2 * c) * 128 + hn:(2 * c) * 128 + hn + 1],
                                        None, ALU.mult))
                                    S["bdd"].inc(v.tensor_scalar(
                                        bddb[hn % 2][:, 128:256],
                                        btr[(2 * c + 1) % 4][:, gn * 128:(gn + 1) * 128],
                                        sb_ddtT[:, (2 * c + 1) * 128 + hn:(2 * c + 1) * 128 + hn + 1],
                                        None, ALU.mult))
                                    v.wait_ge(S["bpeB"].sem, ghn + 1)
                                    if ghn >= 2:
                                        v.wait_ge(S["bacX"].sem, ghn - 1)
                                    S["bdvE"].inc(v.tensor_scalar(
                                        earg[hn % 2][:, 0:256], bank(2 + hn % 2, 0, 256),
                                        sb_qT[:, (2 * c) * 128 + hn:(2 * c) * 128 + hn + 1],
                                        ECLAMP, ALU.subtract, ALU.min))
                                    S["bdvE"].inc(v.tensor_scalar(
                                        earg[hn % 2][:, 256:512], bank(2 + hn % 2, 0, 256),
                                        sb_qT[:, (2 * c + 1) * 128 + hn:(2 * c + 1) * 128 + hn + 1],
                                        ECLAMP, ALU.subtract, ALU.min))
                                    v.wait_ge(S["bodv"].sem, ghn + 1)
                                    if ghn >= 2:
                                        v.wait_ge(S["bpeY"].sem, ghn - 1)
                                    S["bwof"].inc(v.tensor_tensor(
                                        wof[hn % 2][:],
                                        bcb[c % 2][:, 2048 + gn * 256:2048 + (gn + 1) * 256],
                                        odrep[hn % 2][:], ALU.mult))
                                if h >= 0:
                                    v.wait_ge(S["bpeS"].sem, gh + 1)
                                    v.wait_ge(S["bear"].sem, c + 1)
                                    v.wait_ge(S["bsbf"].sem, gh + 1)
                                    S["bsr"].inc(v.scalar_tensor_tensor(
                                        smas[:, h * 64:(h + 1) * 64],
                                        smas[:, h * 64:(h + 1) * 64],
                                        earep[c % 2][:, h:h + 1],
                                        bank(1 if h % 2 == 0 else 6, 0, 64),
                                        ALU.mult, ALU.add))
                                    if h % 2 == 1:
                                        pp = h // 2
                                        b = pp // 8
                                        gb = c * 8 + b
                                        v.wait_ge(S["bpeY"].sem, gh + 1)
                                        if gb >= 2:
                                            yspR.wait_done(v, gb - 2)
                                        S["bev"].inc(v.tensor_copy(
                                            yst[b % 2][:, (pp % 8) * 256:(pp % 8 + 1) * 256],
                                            bank(4 + pp % 2, 0, 256)))

                    @blk.scalar
                    def _(sc):
                        bguards(sc)
                        for c in range(NCH):
                            for h in range(128):
                                gh = c * 128 + h
                                sc.wait_ge(S["bdvE"].sem, 2 * (gh + 1))
                                if gh >= 2:
                                    sc.wait_ge(S["bmtt"].sem, 2 * (gh - 1))
                                S["bacX"].inc(sc.activation(ee[h % 2][:], earg[h % 2][:], AF.Exp))
                                sc.wait_ge(S["bpeB"].sem, gh + 1)
                                if gh >= 2:
                                    sc.wait_ge(S["bwof"].sem, gh - 1)
                                S["bodv"].inc(sc.activation(odrep[h % 2][:],
                                                            bank(2 + h % 2, 256, 256),
                                                            AF.Identity))

            b_guards = [(S["p0"], S["p0"].n), (S["p0t"], S["p0t"].n),
                        (S["p0v"], S["p0v"].n), (S["dtrX"], S["dtrX"].n),
                        (S["bsr"], 128 * NCH), (S["bev"], 64 * NCH),
                        (S["ysp_r0"], 16 * (4 * NCH)), (S["ysp_r1"], 16 * (4 * NCH)),
                        (S["bacX"], 128 * NCH), (S["bodv"], 128 * NCH),
                        (S["bmtt"], 256 * NCH), (S["bdd"], 256 * NCH),
                        (S["bdvE"], 256 * NCH), (S["bpeY"], 128 * NCH),
                        (S["bgev"], 16 * NCH)]

        sQ.close()

        if PH >= 2:
            NQ = L // 512
            cwR = RingSem([S["cw_r0"], S["cw_r1"]])
            zldR = RingSem([S["zl_r0"], S["zl_r1"]])
            xdlR = RingSem([S["xl_r0"], S["xl_r1"]])
            ylR = RingSem([S["yl_r0"]])
            aspR = RingSem([S["as_r0"], S["as_r1"]])

            def cdvp(gq, st, op):
                return gq * 320 + st * 4 + op

            def cacp(gq, st):
                return gq * 66 + st + 1

            with ExitStack() as sC:
                EC = sC.enter_context
                ybuf = EC(nc.sbuf_tensor("c_y", [128, 64 * 512], BF16))
                wtog = [EC(nc.sbuf_tensor(f"c_wt{i}", [128, KDIN * 256], BF16)) for i in range(2)]
                zring = [EC(nc.sbuf_tensor(f"c_z{i}", [128, 512], BF16)) for i in range(4)]
                szb = [EC(nc.sbuf_tensor(f"c_sz{i}", [128, 512], BF16)) for i in range(4)]
                xdr = [EC(nc.sbuf_tensor(f"c_xd{i}", [128, 512], BF16)) for i in range(4)]
                sqb2 = EC(nc.sbuf_tensor("c_sq", [128, 512], F32))
                acc = EC(nc.sbuf_tensor("c_acc", [128, 512], F32))
                rsqv = EC(nc.sbuf_tensor("c_rsq", [1, 512], F32))
                ost = [EC(nc.sbuf_tensor(f"c_o{i}", [128, 512], BF16)) for i in range(4)]

                def cguards(eng):
                    for cnt, val in b_guards:
                        eng.wait_ge(cnt.sem, val)

                with nc.Block("c1") as blk:
                    @blk.gpsimd
                    def _(gp):
                        cguards(gp)
                        for q in range(NQ):
                            if q >= 1:
                                gp.wait_ge(S["cevs"].sem, 32 * q)
                            ylR.inc(gp.dma_start(
                                ybuf[:].rearrange("p (s l) -> p s l", l=512),
                                Y_SP[:, :, q * 512:(q + 1) * 512]
                                .rearrange("s p l -> p s l")), q)
                            for st in range(64):
                                gi = q * 64 + st
                                if gi >= 2:
                                    pq, pst = divmod(gi - 2, 64)
                                    gp.wait_ge(S["cac"].sem, cacp(pq, pst))
                                    gp.wait_ge(S["cdv"].sem, cdvp(pq, pst, 1))
                                zldR.inc(gp.dma_start(zring[st % 4][:],
                                                      ZX[st, :, q * 512:(q + 1) * 512]), gi)
                                xdlR.inc(gp.dma_start(xdr[st % 4][:],
                                                      ZX[64 + st, :, q * 512:(q + 1) * 512]), gi)
                            for og in range(16):
                                for j in range(2):
                                    sidx = q * 32 + 2 * og + j
                                    gp.wait_ge(S["cevs"].sem, sidx + 1)
                                    if sidx >= 2:
                                        aspR.wait_done(gp, sidx - 2)
                                    aspR.inc(gp.dma_start(
                                        ATTN_SP[2 * og + j, :, q * 512:(q + 1) * 512],
                                        ost[sidx % 4][:]), sidx)

                    @blk.scalar
                    def _(sc):
                        cguards(sc)
                        for q in range(NQ):
                            for st in range(64):
                                gi = q * 64 + st
                                zldR.wait_done(sc, gi)
                                if gi >= 4:
                                    pq, pst = divmod(gi - 4, 64)
                                    sc.wait_ge(S["cdv"].sem, cdvp(pq, pst, 2))
                                S["cac"].inc(sc.activation(szb[st % 4][:], zring[st % 4][:],
                                                           AF.Silu))
                            sc.wait_ge(S["cpe"].sem, q * 18 + 1)
                            S["cac"].inc(sc.activation(rsqv[0:1, :], bank(0, 0, 512, 0, 1),
                                                       AF.Ln, bias=sb_epscol[0:1, 0:1],
                                                       scale=1.0 / DIN))
                            sc.wait_ge(S["cac"].sem, q * 66 + 65)
                            S["cac"].inc(sc.activation(rsqv[0:1, :], rsqv[0:1, :],
                                                       AF.Exp, scale=-0.5))

                    @blk.vector
                    def _(v):
                        cguards(v)
                        for q in range(NQ):
                            ylR.wait_done(v, q)
                            for st in range(64):
                                gi = q * 64 + st
                                xdlR.wait_done(v, gi)
                                ys = ybuf[:, st * 512:(st + 1) * 512]
                                S["cdv"].inc(v.scalar_tensor_tensor(
                                    ys, xdr[st % 4][:], sb_dexp[:, st:st + 1], ys,
                                    ALU.mult, ALU.add))
                                v.wait_ge(S["cac"].sem, cacp(q, st))
                                v.wait_ge(S["cdv"].sem, cdvp(q, st, 1))
                                S["cdv"].inc(v.tensor_tensor(ys, ys, szb[st % 4][:], ALU.mult))
                                v.wait_ge(S["cdv"].sem, cdvp(q, st, 2))
                                S["cdv"].inc(v.tensor_tensor(sqb2[:], ys, ys, ALU.mult))
                                v.wait_ge(S["cdv"].sem, cdvp(q, st, 3))
                                if st == 0:
                                    S["cdv"].inc(v.tensor_copy(acc[:], sqb2[:]))
                                else:
                                    S["cdv"].inc(v.tensor_tensor(acc[:], acc[:], sqb2[:],
                                                                 ALU.add))
                            v.wait_ge(S["cpe"].sem, q * 18 + 2)
                            for st in range(64):
                                ys = ybuf[:, st * 512:(st + 1) * 512]
                                S["cdv"].inc(v.scalar_tensor_tensor(
                                    ys, ys, sb_nrmw[:, st:st + 1], bank(1, 0, 512),
                                    ALU.mult, ALU.mult))
                            for og in range(16):
                                gog = q * 16 + og
                                rr = gog % 2
                                v.wait_ge(S["cpe"].sem, q * 18 + 2 + og + 1)
                                for j in range(2):
                                    sidx = q * 32 + 2 * og + j
                                    if sidx >= 4:
                                        aspR.wait_done(v, sidx - 4, uses=(sidx - 4) // 2 + 1)
                                    S["cevs"].inc(v.tensor_copy(ost[sidx % 4][:],
                                                                bank(2 + 2 * ((rr * 2 + j) // 2) + j, 0, 512)
                                                                if False else
                                                                bank(2 + 2 * rr + j, 0, 512)))

                    @blk.tensor
                    def _(t):
                        cguards(t)
                        wstR.wait_done(t, KDIN - 2)
                        wstR.wait_done(t, KDIN - 1)
                        for q in range(NQ):
                            t.wait_ge(S["cdv"].sem, q * 320 + 256)
                            S["cpe"].inc(t.matmul(bank(0, 0, 512, 0, 1), sb_onecol[:, 0:1],
                                                  acc[:], start=True, stop=True))
                            t.wait_ge(S["cac"].sem, q * 66 + 66)
                            S["cpe"].inc(t.matmul(bank(1, 0, 512), sb_ones_f[0:1, 0:128],
                                                  rsqv[0:1, :], start=True, stop=True))
                            for og in range(16):
                                gog = q * 16 + og
                                cwR.wait_done(t, gog)
                                rr = gog % 2
                                t.wait_ge(S["cdv"].sem, (q + 1) * 320)
                                if gog >= 2:
                                    t.wait_ge(S["cevs"].sem, 2 * (gog - 1))
                                for k in range(KDIN):
                                    for j in range(2):
                                        mm = t.matmul(
                                            bank(2 + 2 * rr + j, 0, 512),
                                            wtog[gog % 2][:, k * 256 + j * 128:k * 256 + (j + 1) * 128],
                                            ybuf[:, k * 512:(k + 1) * 512],
                                            start=(k == 0), stop=(k == KDIN - 1))
                                    if k == KDIN - 1:
                                        S["cpe"].inc(mm)

                    @blk.sync
                    def _(sy):
                        cguards(sy)
                        wstR.wait_done(sy, KDIN - 2)
                        wstR.wait_done(sy, KDIN - 1)
                        for q in range(NQ):
                            for og in range(16):
                                gog = q * 16 + og
                                if gog >= 2:
                                    pq, pog = divmod(gog - 2, 16)
                                    sy.wait_ge(S["cpe"].sem, pq * 18 + 2 + pog + 1)
                                cwR.inc(sy.dma_start(
                                    wtog[gog % 2][:].rearrange("p (k o) -> p k o", o=256),
                                    WOUTT[:, og * 256:(og + 1) * 256]
                                    .rearrange("(k p) o -> p k o", p=128)), gog)

            with ExitStack() as sC2:
                EC = sC2.enter_context
                att = [EC(nc.sbuf_tensor(f"c2_a{i}", [128, D], BF16)) for i in range(2)]
                xtl = [EC(nc.sbuf_tensor(f"c2_x{i}", [128, D], F32)) for i in range(2)]
                xlb = [EC(nc.sbuf_tensor(f"c2_l{i}", [128, D], BF16)) for i in range(2)]
                x2b = [EC(nc.sbuf_tensor(f"c2_x2{i}", [128, D], F32)) for i in range(2)]
                x2l = EC(nc.sbuf_tensor("c2_xl", [128, D], BF16))
                sqc = EC(nc.sbuf_tensor("c2_sq", [128, D], F32))
                stc = EC(nc.sbuf_tensor("c2_st", [128, 8], F32))
                atlR = RingSem([S["at_r0"], S["at_r1"]])
                xtlR = RingSem([S["xt2_r0"], S["xt2_r1"]])
                xlbR = RingSem([S["zl_r0"], S["zl_r1"]])
                x2sR = RingSem([S["x2_r0"], S["x2_r1"]])
                xlsR = RingSem([S["xl2_r0"], S["xl2_r1"]])

                ATV = ATTN_SP.rearrange("s p l -> (s p) l")

                with nc.Block("c2") as blk:
                    @blk.sync
                    def _(sy):
                        aspR.wait_all(sy, 32 * NQ)
                        for i in range(NLT):
                            if i >= 2:
                                sy.wait_ge(S["cx2v"].sem, 9 * (i - 1))
                            atlR.inc(sy.dma_start_transpose(
                                att[i % 2][:], ATV[:, i * 128:(i + 1) * 128]), i)

                    @blk.gpsimd
                    def _(gp):
                        for i in range(NLT):
                            if i >= 2:
                                gp.wait_ge(S["cx2v"].sem, 9 * (i - 1))
                            xtlR.inc(gp.dma_start(xtl[i % 2][:], X[i * 128:(i + 1) * 128, :]), i)
                            xlbR.inc(gp.dma_start(xlb[i % 2][:],
                                                  XLNT[i * 128:(i + 1) * 128, :]), i)
                            gp.wait_ge(S["cx2v"].sem, 9 * i + 2)
                            x2sR.inc(gp.dma_start(X2_SP[i * 128:(i + 1) * 128, :],
                                                  x2b[i % 2][:]), i)
                            gp.wait_ge(S["cx2a"].sem, 4 * i + 4)
                            xlsR.inc(gp.dma_start(X2LNT[i * 128:(i + 1) * 128, :], x2l[:]), i)

                    @blk.vector
                    def _(v):
                        for i in range(NLT):
                            atlR.wait_done(v, i)
                            xtlR.wait_done(v, i)
                            xlbR.wait_done(v, i)
                            if i >= 2:
                                x2sR.wait_done(v, i - 2)
                                v.wait_ge(S["cx2a"].sem, 4 * (i - 1))
                            S["cx2v"].inc(v.tensor_tensor(x2b[i % 2][:], xtl[i % 2][:],
                                                          att[i % 2][:], ALU.add))
                            v.wait_ge(S["cx2v"].sem, 9 * i + 1)
                            S["cx2v"].inc(v.tensor_tensor(x2b[i % 2][:], x2b[i % 2][:],
                                                          xlb[i % 2][:], ALU.add))
                            v.wait_ge(S["cx2v"].sem, 9 * i + 2)
                            S["cx2v"].inc(v.reduce_sum(stc[:, 0:1], x2b[i % 2][:], axis=AX.X))
                            v.wait_ge(S["cx2a"].sem, 4 * i + 1)
                            S["cx2v"].inc(v.reduce_sum(stc[:, 1:2], sqc[:], axis=AX.X))
                            v.wait_ge(S["cx2v"].sem, 9 * i + 4)
                            S["cx2v"].inc(v.tensor_scalar(stc[:, 2:3], stc[:, 0:1], 1.0 / D,
                                                          None, ALU.mult))
                            v.wait_ge(S["cx2v"].sem, 9 * i + 5)
                            S["cx2v"].inc(v.tensor_tensor(stc[:, 3:4], stc[:, 2:3],
                                                          stc[:, 2:3], ALU.mult))
                            v.wait_ge(S["cx2v"].sem, 9 * i + 6)
                            S["cx2v"].inc(v.scalar_tensor_tensor(
                                stc[:, 4:5], stc[:, 1:2], 1.0 / D, stc[:, 3:4],
                                ALU.mult, ALU.subtract))
                            v.wait_ge(S["cx2a"].sem, 4 * i + 3)
                            S["cx2v"].inc(v.tensor_tensor(stc[:, 7:8], stc[:, 2:3],
                                                          stc[:, 6:7], ALU.mult))
                            v.wait_ge(S["cx2v"].sem, 9 * i + 8)
                            S["cx2v"].inc(v.tensor_scalar(stc[:, 7:8], stc[:, 7:8], -1.0,
                                                          None, ALU.mult))

                    @blk.scalar
                    def _(sc):
                        for i in range(NLT):
                            sc.wait_ge(S["cx2v"].sem, 9 * i + 2)
                            S["cx2a"].inc(sc.activation(sqc[:], x2b[i % 2][:], AF.Square))
                            sc.wait_ge(S["cx2v"].sem, 9 * i + 7)
                            S["cx2a"].inc(sc.activation(stc[:, 5:6], stc[:, 4:5], AF.Ln,
                                                        bias=sb_epscol[:, 0:1]))
                            sc.wait_ge(S["cx2a"].sem, 4 * i + 2)
                            S["cx2a"].inc(sc.activation(stc[:, 6:7], stc[:, 5:6], AF.Exp,
                                                        scale=-0.5))
                            sc.wait_ge(S["cx2v"].sem, 9 * i + 9)
                            if i >= 1:
                                xlsR.wait_done(sc, i - 1)
                            S["cx2a"].inc(sc.activation(x2l[:], x2b[i % 2][:], AF.Identity,
                                                        scale=stc[:, 6:7], bias=stc[:, 7:8]))

            c2_guards = [(S["x2_r0"], 16 * ((NLT + 1) // 2)), (S["x2_r1"], 16 * (NLT // 2)),
                         (S["xl2_r0"], 16 * ((NLT + 1) // 2)), (S["xl2_r1"], 16 * (NLT // 2)),
                         (S["cx2a"], 4 * NLT), (S["cx2v"], 9 * NLT)]

        if PH >= 3:
            d_guards = list(b_guards) + [
                (S["cevs"], 32 * NQ), (S["cpe"], 18 * NQ),
                (S["cdv"], 320 * NQ), (S["cac"], 66 * NQ),
                (S["cx2v"], 9 * NLT), (S["cx2a"], 4 * NLT),
                (S["x2_r0"], S["x2_r0"].n), (S["x2_r1"], S["x2_r1"].n),
                (S["xl2_r0"], S["xl2_r0"].n), (S["xl2_r1"], S["xl2_r1"].n),
                (S["as_r0"], S["as_r0"].n), (S["as_r1"], S["as_r1"].n)]
            wbR = RingSem([S["wb2_r0"], S["wb2_r1"]])
            ffsR = RingSem([S["ffs_r0"], S["ffs_r1"]])
            bldR = RingSem([S["dbl"]])
            xlfD = Cnt(S["xlf"].sem)
            xlfD.n = S["xlf"].n
            S["xlf"] = xlfD
            dtrD = S["dtrX"]
            dtr_base = dtrD.n
            xlf_base = xlfD.n

            def DPE_BFT(i):
                return i + 1
            DPE_WGT0 = 64
            DPE_GATE0 = 96
            def GDONE(i):
                return 97 + i if i <= 2 else 95 + 2 * i
            def TRDONE(j):
                if j <= NLT - 3:
                    return 100 + 2 * j
                return 95 + 2 * NLT + (j - (NLT - 2))
            DPE_WEBC0 = 96 + 2 * NLT
            DPE_H0 = DPE_WEBC0 + NTB
            DPE_OG0 = DPE_H0 + NTB
            def DDV_BFT(i):
                return i + 2
            DDV_WGT0 = 65
            DDV_GATE0 = 97
            NGOP = 11
            DDV_WT0 = DDV_GATE0 + NGOP * NLT
            DDV_WEBC0 = DDV_WT0
            DDV_H0 = DDV_WEBC0 + NTB
            DDV_HW = DDV_H0 + NTB + 1
            DDV_EV0 = DDV_HW

            with ExitStack() as sD:
                ED = sD.enter_context
                x2lnf = ED(nc.sbuf_tensor("d_xlf", [128, KD * L], BF16))
                wtb = [ED(nc.sbuf_tensor(f"d_wt{i}", [128, KD * 256], BF16)) for i in range(2)]
                atb = ED(nc.sbuf_tensor("d_at", [128, KD * 128], BF16))
                wgbf = ED(nc.sbuf_tensor("d_wg", [9, D], BF16))
                wgT = ED(nc.sbuf_tensor("d_wgT", [128, KD * 16], BF16))
                ebc8b = ED(nc.sbuf_tensor("d_ebc", [8, 128], BF16))
                bft = ED(nc.sbuf_tensor("d_bft", [128, D], BF16))
                btmp = ED(nc.sbuf_tensor("d_btmp", [128, 64], BF16))
                gbuf = ED(nc.sbuf_tensor("d_gb", [128, 40], F32))
                wnb = [ED(nc.sbuf_tensor(f"d_wn{i}", [128, 8], BF16)) for i in range(2)]
                wT = ED(nc.sbuf_tensor("d_wT", [8, L], BF16))
                webc = ED(nc.sbuf_tensor("d_webc", [128, L], BF16))
                hbuf = ED(nc.sbuf_tensor("d_h", [128, L], BF16))
                fstg = [ED(nc.sbuf_tensor(f"d_f{i}", [128, 512], BF16)) for i in range(4)]

                def dguards(eng):
                    for cnt, val in d_guards:
                        eng.wait_ge(cnt.sem, val)

                DCUT2 = os.environ.get("MK_DCUT") == "2"
                DCUT3 = os.environ.get("MK_DCUT") in ("3", "4")
                DCUT4 = os.environ.get("MK_DCUT") in ("4", "5")
                DCUT5 = os.environ.get("MK_DCUT") in ("5", "6")
                DCUT6 = os.environ.get("MK_DCUT") == "6"
                with nc.Block("d12") as blk:
                    @blk.sync
                    def _(sy):
                        dguards(sy)
                        for k in range(KD):
                            dtrD.inc(sy.dma_start_transpose(
                                x2lnf[:, k * L:(k + 1) * L],
                                X2LNT[:, k * 128:(k + 1) * 128]), 16)
                        for k in range(KD):
                            xlfD.inc(sy.dma_start_transpose(
                                atb[:, k * 128:(k + 1) * 128],
                                A_BF[:, k * 128:(k + 1) * 128]), 16)
                        xlfD.inc(sy.dma_start(wgbf[:], WG_BF), 16)
                        xlfD.inc(sy.dma_start(ebc8b[:], EBC8), 16)
                        for og in range(0 if (DCUT2 or DCUT3) else 16):
                            if og >= 2:
                                sy.wait_ge(S["dpe"].sem, DPE_OG0 + (og - 1) * NTB)
                            wbR.inc(sy.dma_start(
                                wtb[og % 2][:].rearrange("p (k o) -> p k o", o=256),
                                WBASET[:, og * 256:(og + 1) * 256]
                                .rearrange("(k p) o -> p k o", p=128)), og)

                    @blk.gpsimd
                    def _(gp):
                        dguards(gp)
                        for it in range(64):
                            g4, ot = divmod(it, KD)
                            if it >= 1:
                                gp.wait_ge(S["ddv"].sem, DDV_BFT(it - 1))
                            for j in range(4):
                                e = 4 * g4 + j
                                bldR.inc(gp.dma_start(
                                    btmp[:, j * 16:(j + 1) * 16],
                                    B_BF[e * D + ot * 128:e * D + (ot + 1) * 128, :]),
                                    it * 4 + j)
                        for og in range(0 if (DCUT2 or DCUT3) else 16):
                            for tb in range(NTB):
                                for j in range(2):
                                    sidx = (og * NTB + tb) * 2 + j
                                    gp.wait_ge(S["ddv"].sem, DDV_EV0 + sidx + 1)
                                    if sidx >= 2:
                                        ffsR.wait_done(gp, sidx - 2)
                                    ffsR.inc(gp.dma_start(
                                        FFN_SP[2 * og + j, :, tb * 512:(tb + 1) * 512],
                                        fstg[sidx % 4][:]), sidx)

                    @blk.tensor
                    def _(t):
                        dguards(t)
                        for it in range(64):
                            t.wait_ge(S["dbl"].sem, 16 * 4 * (it + 1))
                            S["dpe"].inc(t.transpose(psB16[0:64, 0:128], btmp[:],
                                                     sb_i128b[:]))
                        t.wait_ge(S["xlf"].sem, xlf_base + 16 * (KD + 2))
                        t.wait_ge(S["ddv"].sem, 65)
                        for k in range(KD):
                            if k >= 1:
                                t.wait_ge(S["ddv"].sem, DDV_WGT0 + k)
                            S["dpe"].inc(t.transpose(
                                psB16[0:128, 128 + (k % 2) * 16:128 + (k % 2) * 16 + 9],
                                wgbf[0:9, k * 128:(k + 1) * 128], sb_i128b[0:9, 0:9]))
                        t.wait_ge(S["dtrX"].sem, dtr_base + 16 * KD)
                        t.wait_ge(S["ddv"].sem, DDV_WGT0 + KD)
                        if DCUT2:
                            return

                        def wn_tr(i):
                            t.wait_ge(S["ddv"].sem, DDV_GATE0 + i * NGOP + NGOP)
                            if i >= 1:
                                t.wait_ge(S["dtrF"].sem, i)
                            S["dpe"].inc(t.transpose(
                                psB16[0:8, 512 + (i % 2) * 128:512 + (i % 2) * 128 + 128],
                                wnb[i % 2][:], sb_i128b[:]))

                        ntr = 0
                        for i in range(NLT if "gates" not in DSKIP else 0):
                            if i >= 2:
                                t.wait_ge(S["ddv"].sem, DDV_GATE0 + (i - 1) * NGOP)
                            for k in range(KD):
                                mm = t.matmul(bank(i % 2, 0, 16),
                                              x2lnf[:, k * L + i * 128:k * L + (i + 1) * 128],
                                              wgT[:, k * 16:(k + 1) * 16],
                                              start=(k == 0), stop=(k == KD - 1))
                            S["dpe"].inc(mm)
                            if not DCUT5 and i >= 2:
                                wn_tr(i - 2)
                                ntr += 1
                        while not DCUT5 and ntr < NLT:
                            wn_tr(ntr)
                            ntr += 1
                        for tb in range(0 if DCUT4 else NTB):
                            t.wait_ge(S["ddv"].sem, DDV_WT0)
                            t.wait_ge(S["dtrF"].sem, NLT)
                            if tb >= 2:
                                t.wait_ge(S["ddv"].sem, DDV_WEBC0 + tb - 1)
                            S["dpe"].inc(t.matmul(bank(tb % 2), ebc8b[:],
                                                  wT[:, tb * 512:(tb + 1) * 512],
                                                  start=True, stop=True))
                        for tb in range(0 if DCUT4 else NTB):
                            if tb >= 2:
                                t.wait_ge(S["ddv"].sem, DDV_H0 + tb - 1)
                            for k in range(KD):
                                mm = t.matmul(bank(2 + tb % 2),
                                              atb[:, k * 128:(k + 1) * 128],
                                              x2lnf[:, k * L + tb * 512:k * L + (tb + 1) * 512],
                                              start=(k == 0), stop=(k == KD - 1))
                            S["dpe"].inc(mm)
                        t.wait_ge(S["ddv"].sem, DDV_HW)
                        for og in range(0 if DCUT3 else 16):
                            wbR.wait_done(t, og)
                            for tb in range(NTB):
                                gg = og * NTB + tb
                                rr = gg % 2
                                if gg >= 2:
                                    t.wait_ge(S["ddv"].sem, DDV_EV0 + 2 * (gg - 1))
                                for j in range(2):
                                    for k in range(KD):
                                        mm = t.matmul(
                                            bank(2 + 2 * rr + j),
                                            wtb[og % 2][:, k * 256 + j * 128:k * 256 + (j + 1) * 128],
                                            x2lnf[:, k * L + tb * 512:k * L + (tb + 1) * 512],
                                            start=(k == 0), stop=False)
                                    mm = t.matmul(
                                        bank(2 + 2 * rr + j),
                                        bft[:, (2 * og + j) * 128:(2 * og + j + 1) * 128],
                                        hbuf[:, tb * 512:(tb + 1) * 512],
                                        start=False, stop=True)
                                S["dpe"].inc(mm)

                    @blk.vector
                    def _(v):
                        dguards(v)
                        S["ddv"].inc(v.memset(wgT[:], 0.0))
                        for it in range(64):
                            g4 = it // KD
                            ot = it % KD
                            v.wait_ge(S["dpe"].sem, DPE_BFT(it))
                            S["ddv"].inc(v.tensor_copy(
                                bft[64 * g4:64 * g4 + 64, ot * 128:(ot + 1) * 128],
                                psB16[0:64, 0:128]))
                        for k in range(KD):
                            v.wait_ge(S["dpe"].sem, DPE_WGT0 + k + 1)
                            S["ddv"].inc(v.tensor_copy(
                                wgT[:, k * 16:k * 16 + 9],
                                psB16[0:128, 128 + (k % 2) * 16:128 + (k % 2) * 16 + 9]))
                        if DCUT2:
                            return
                        for i in range(NLT):
                            v.wait_ge(S["dpe"].sem, GDONE(i))
                            gb = gbuf[:, (i % 2) * 16:(i % 2) * 16 + 16]
                            S["ddv"].inc(v.tensor_copy(gb[:, 0:9], bank(i % 2, 0, 9)))
                            if DCUT6:
                                for _pad in range(NGOP - 1):
                                    S["ddv"].inc(v.tensor_copy(gbuf[:, 32 + (_pad % 8):33 + (_pad % 8)],
                                                               gbuf[:, 0:1]))
                                continue
                            v.wait_ge(S["ddv"].sem, DDV_GATE0 + i * NGOP + 1)
                            S["ddv"].inc(v.reduce_max(gb[:, 9:10], gb[:, 0:8], axis=AX.X))
                            v.wait_ge(S["ddv"].sem, DDV_GATE0 + i * NGOP + 2)
                            S["ddv"].inc(v.tensor_scalar(gb[:, 10:11], gb[:, 9:10], -1.0,
                                                         None, ALU.mult))
                            v.wait_ge(S["dac"].sem, 3 * i + 1)
                            S["ddv"].inc(v.reduce_sum(gb[:, 11:12], gb[:, 0:8], axis=AX.X))
                            v.wait_ge(S["ddv"].sem, DDV_GATE0 + i * NGOP + 4)
                            S["ddv"].inc(v.reciprocal(gb[:, 12:13], gb[:, 11:12]))
                            v.wait_ge(S["dac"].sem, 3 * i + 3)
                            v.wait_ge(S["ddv"].sem, DDV_GATE0 + i * NGOP + 5)
                            S["ddv"].inc(v.tensor_scalar(gb[:, 0:8], gb[:, 0:8],
                                                         gb[:, 12:13], None, ALU.mult))
                            v.wait_ge(S["ddv"].sem, DDV_GATE0 + i * NGOP + 6)
                            S["ddv"].inc(v.tensor_scalar(gb[:, 0:8], gb[:, 0:8],
                                                         gb[:, 13:14], 0.0,
                                                         ALU.subtract, ALU.max))
                            v.wait_ge(S["ddv"].sem, DDV_GATE0 + i * NGOP + 7)
                            S["ddv"].inc(v.reduce_sum(gb[:, 14:15], gb[:, 0:8], axis=AX.X))
                            v.wait_ge(S["ddv"].sem, DDV_GATE0 + i * NGOP + 8)
                            S["ddv"].inc(v.tensor_scalar(gb[:, 15:16], gb[:, 14:15], 1e-30,
                                                         None, ALU.max))
                            v.wait_ge(S["ddv"].sem, DDV_GATE0 + i * NGOP + 9)
                            S["ddv"].inc(v.reciprocal(gb[:, 15:16], gb[:, 15:16]))
                            v.wait_ge(S["ddv"].sem, DDV_GATE0 + i * NGOP + 10)
                            if i >= 2 and not DCUT5:
                                v.wait_ge(S["dpe"].sem, TRDONE(i - 2))
                            S["ddv"].inc(v.tensor_scalar(wnb[i % 2][:], gb[:, 0:8],
                                                         gb[:, 15:16], None, ALU.mult))
                            if not DCUT5 and i >= 1:
                                v.wait_ge(S["dpe"].sem, TRDONE(i - 1))
                                S["dtrF"].inc(v.tensor_copy(
                                    wT[:, (i - 1) * 128:i * 128],
                                    psB16[0:8, 512 + ((i - 1) % 2) * 128:512 + ((i - 1) % 2) * 128 + 128]))
                        if not DCUT5:
                            v.wait_ge(S["dpe"].sem, TRDONE(NLT - 1))
                            S["dtrF"].inc(v.tensor_copy(
                                wT[:, (NLT - 1) * 128:NLT * 128],
                                psB16[0:8, 512 + ((NLT - 1) % 2) * 128:512 + ((NLT - 1) % 2) * 128 + 128]))
                        for tb in range(0 if DCUT4 else NTB):
                            v.wait_ge(S["dpe"].sem, DPE_WEBC0 + tb + 1)
                            S["ddv"].inc(v.tensor_scalar(webc[:, tb * 512:(tb + 1) * 512],
                                                         bank(tb % 2), 2.0, None, ALU.mult))
                        for tb in range(0 if DCUT4 else NTB):
                            v.wait_ge(S["dpe"].sem, DPE_H0 + tb + 1)
                            S["ddv"].inc(v.tensor_copy(hbuf[:, tb * 512:(tb + 1) * 512],
                                                       bank(2 + tb % 2)))
                        if not DCUT4:
                            v.wait_ge(S["ddv"].sem, DDV_H0 + NTB)
                            S["ddv"].inc(v.tensor_tensor(hbuf[:], hbuf[:], webc[:], ALU.mult))
                        for og in range(0 if DCUT3 else 16):
                            for tb in range(NTB):
                                gg = og * NTB + tb
                                rr = gg % 2
                                v.wait_ge(S["dpe"].sem, DPE_OG0 + gg + 1)
                                for j in range(2):
                                    sidx = gg * 2 + j
                                    if sidx >= 4:
                                        ffsR.wait_done(v, sidx - 4,
                                                       uses=(sidx - 4) // 2 + 1)
                                    S["ddv"].inc(v.tensor_scalar(
                                        fstg[sidx % 4][:], bank(2 + 2 * rr + j),
                                        sb_bbase[:, 2 * og + j:2 * og + j + 1], None,
                                        ALU.add))

                    @blk.scalar
                    def _(sc):
                        dguards(sc)
                        for i in range(0 if (DCUT2 or DCUT6) else NLT):
                            gb = gbuf[:, (i % 2) * 16:(i % 2) * 16 + 16]
                            sc.wait_ge(S["ddv"].sem, DDV_GATE0 + i * NGOP + 3)
                            S["dac"].inc(sc.activation(gb[:, 0:8], gb[:, 0:8], AF.Exp,
                                                       bias=gb[:, 10:11]))
                            S["dac"].inc(sc.activation(gb[:, 13:14], gb[:, 8:9], AF.Sigmoid,
                                                       bias=sb_bthr[:, 0:1]))
                            sc.wait_ge(S["dac"].sem, 3 * i + 2)
                            S["dac"].inc(sc.activation(gb[:, 13:14], gb[:, 13:14],
                                                       AF.Identity, scale=0.125))

            if os.environ.get("MK_DCUT") in ("1", "2", "3", "4", "5", "6"):
                with nc.Block("fincut") as blk:
                    @blk.gpsimd
                    def _(gp):
                        if os.environ.get("MK_DCUT") == "2":
                            gp.wait_ge(S["ddv"].sem, DDV_WGT0 + KD)
                            gp.wait_ge(S["dpe"].sem, 96)
                        elif os.environ.get("MK_DCUT") == "3":
                            gp.wait_ge(S["ddv"].sem, DDV_HW)
                            gp.wait_ge(S["dpe"].sem, DPE_OG0)
                        elif os.environ.get("MK_DCUT") == "4":
                            gp.wait_ge(S["ddv"].sem, DDV_WT0 + NLT)
                            gp.wait_ge(S["dpe"].sem, DPE_WEBC0)
                        elif os.environ.get("MK_DCUT") == "5":
                            gp.wait_ge(S["ddv"].sem, DDV_GATE0 + NLT * NGOP)
                            gp.wait_ge(S["dpe"].sem, DPE_WTR0)
                        else:
                            ffsR.wait_all(gp, 32 * NTB)
                            gp.wait_ge(S["ddv"].sem, DDV_EV0 + 2 * 32 * NTB)
                        S["dout"].inc(gp.dma_start(OUT, X), 16)
                        gp.wait_ge(S["dout"].sem, 16)
                return nc
            with ExitStack() as sD3:
                ED = sD3.enter_context
                ftb = [ED(nc.sbuf_tensor(f"d3_f{i}", [128, D], BF16)) for i in range(2)]
                x2t = [ED(nc.sbuf_tensor(f"d3_x{i}", [128, D], F32)) for i in range(2)]
                ostg = [ED(nc.sbuf_tensor(f"d3_o{i}", [128, D], F32)) for i in range(2)]
                ftR = RingSem([S["ft2_r0"], S["ft2_r1"]])
                xfR = RingSem([S["xf_r0"], S["xf_r1"]])
                ouR = RingSem([S["ou_r0"], S["ou_r1"]])
                FFV = FFN_SP.rearrange("s p l -> (s p) l")

                with nc.Block("d3") as blk:
                    @blk.sync
                    def _(sy):
                        ffsR.wait_all(sy, 32 * NTB)
                        for i in range(NLT):
                            if i >= 2:
                                sy.wait_ge(S["ddv"].sem, DDV_EV0 + 32 * NTB + i - 1)
                            ftR.inc(sy.dma_start_transpose(
                                ftb[i % 2][:], FFV[:, i * 128:(i + 1) * 128]), i)

                    @blk.gpsimd
                    def _(gp):
                        for i in range(NLT):
                            if i >= 2:
                                gp.wait_ge(S["ddv"].sem, DDV_EV0 + 32 * NTB + i - 1)
                            xfR.inc(gp.dma_start(x2t[i % 2][:],
                                                 X2_SP[i * 128:(i + 1) * 128, :]), i)
                            gp.wait_ge(S["ddv"].sem, DDV_EV0 + 32 * NTB + i + 1)
                            ouR.inc(gp.dma_start(OUT[i * 128:(i + 1) * 128, :],
                                                 ostg[i % 2][:]), i)
                        ouR.wait_all(gp, NLT)

                    @blk.vector
                    def _(v):
                        for i in range(NLT):
                            ftR.wait_done(v, i)
                            xfR.wait_done(v, i)
                            if i >= 2:
                                ouR.wait_done(v, i - 2)
                            S["ddv"].inc(v.tensor_tensor(ostg[i % 2][:], x2t[i % 2][:],
                                                         ftb[i % 2][:], ALU.add))

        fin_guards = a1_guards if PH == 0 else b_guards
        if PH < 3:
            fin_guards = a1_guards if PH == 0 else b_guards
            with nc.Block("fin") as blk:
                @blk.gpsimd
                def _(gp):
                    for cnt, val in fin_guards:
                        gp.wait_ge(cnt.sem, val)
                    S["dout"].inc(gp.dma_start(OUT, X), 16)
                    gp.wait_ge(S["dout"].sem, 16)
        return nc

    return nc


def _consts(L):
    tril = np.zeros((128, 512), np.float32)
    for p in range(128):
        tril[p, p:256] = 1.0
        if 128 + p < 256:
            pass
        tril[p, 256 + 128 + p:512] = 1.0
    ebc = np.zeros((8, 128), np.float32)
    for e in range(8):
        ebc[e, e * 16:(e + 1) * 16] = 1.0
    return dict(
        tril=tril.astype(BF),
        i128f=np.eye(128, dtype=np.float32),
        i128b=np.eye(128, dtype=np.float32).astype(BF),
        ebc8=ebc.astype(BF),
        onesb=np.ones((1, 128), BF),
        onesf=np.ones((1, 128), np.float32),
    )


def make_in_maps(inputs, L=2048, ncores=8):
    w = {
        "win": np.ascontiguousarray(inputs["L0_attn_in_proj_w"], np.float32),
        "convw": np.ascontiguousarray(inputs["L0_attn_conv_w"], np.float32),
        "convb": np.ascontiguousarray(inputs["L0_attn_conv_b"], np.float32),
        "dtbias": np.ascontiguousarray(inputs["L0_attn_dt_bias"], np.float32),
        "alog": np.ascontiguousarray(inputs["L0_attn_A_log"], np.float32),
        "dhead": np.ascontiguousarray(inputs["L0_attn_D"], np.float32).reshape(1, H),
        "nrmw": np.ascontiguousarray(inputs["L0_attn_norm_w"], np.float32),
        "wout": np.ascontiguousarray(inputs["L0_attn_out_proj_w"], np.float32),
        "wbase": np.ascontiguousarray(inputs["L0_ffn_W_base"], np.float32),
        "bbase": np.ascontiguousarray(inputs["L0_ffn_b_base"], np.float32),
        "wgate9": np.ascontiguousarray(
            np.concatenate([np.asarray(inputs["L0_ffn_W_gate"], np.float32),
                            np.asarray(inputs["L0_ffn_W_thr"], np.float32)], axis=0)),
        "bthr": np.ascontiguousarray(inputs["L0_ffn_b_thr"], np.float32).reshape(1, 1),
        "aexp": np.ascontiguousarray(inputs["L0_ffn_A"], np.float32).reshape(128, D),
        "bexp": np.ascontiguousarray(inputs["L0_ffn_B"], np.float32).reshape(8 * D, 16),
    }
    w.update(_consts(L))
    xs = np.asarray(inputs["L0_attn_hidden_states"], np.float32)
    maps = []
    for b in range(ncores):
        m = dict(w)
        m["x"] = np.ascontiguousarray(xs[b, :L, :])
        maps.append(m)
    return maps


def kernel(**inputs):
    L = 2048
    nc = build_program(L=L)
    maps = make_in_maps(inputs, L=L)
    res = run_bass_kernel_spmd(nc, maps, core_ids=list(range(8)))
    out = np.stack([np.asarray(r["out"], np.float32) for r in res.results], axis=0)
    return out


# version 05
# speedup vs baseline: 1.0137x; 1.0137x over previous
import os
import numpy as np
import ml_dtypes
from contextlib import ExitStack

import concourse.bass as bass
import concourse.mybir as mybir
from concourse.bass_utils import run_bass_kernel_spmd

F32 = mybir.dt.float32
BF16 = mybir.dt.bfloat16
AF = mybir.ActivationFunctionType
ALU = mybir.AluOpType
AX = mybir.AxisListType
BF = ml_dtypes.bfloat16

D, DIN, H, PDIM, NG, NSTATE, CK, CH = 4096, 8192, 128, 64, 8, 128, 4, 256
DPROJ = 2 * DIN + 2 * NG * NSTATE + H
CONVD = DIN + 2 * NG * NSTATE
NSTRIP = DPROJ // 128
DTSTRIP = 144
EPS = 1e-5
ECLAMP = 50.0
KD = D // 128
KDIN = DIN // 128


class Cnt:
    def __init__(self, sem):
        self.sem = sem
        self.n = 0

    def inc(self, inst, k=1):
        inst.then_inc(self.sem, k)
        self.n += k
        return self.n


class RingSem:
    def __init__(self, sems, per=16):
        self.sems = sems
        self.n = len(sems)
        self.per = per
        self.base = [c.n for c in sems]

    def inc(self, inst, idx):
        sl = idx % self.n
        inst.then_inc(self.sems[sl].sem, self.per)
        self.sems[sl].n += self.per

    def wait_done(self, eng, idx, uses=None):
        sl = idx % self.n
        u = (idx // self.n + 1) if uses is None else uses
        eng.wait_ge(self.sems[sl].sem, self.base[sl] + self.per * u)

    def wait_all(self, eng, total_items):
        for sl in range(self.n):
            cnt = (total_items - sl + self.n - 1) // self.n
            if cnt > 0:
                eng.wait_ge(self.sems[sl].sem, self.base[sl] + self.per * cnt)


def build_program(L=2048, debug=(), last_phase="D"):
    DSKIP = set(os.environ.get("MK_DSKIP", "").split(","))
    NCH = L // CH
    NTB = L // 512
    NLT = L // 128
    PH = {"A": 0, "B": 1, "C": 2, "D": 3}[last_phase]

    a1_groups = [(64 + 2 * i, 65 + 2 * i) for i in range(40)] + \
                [(2 * i, 2 * i + 1) for i in range(32)]
    NGRP = len(a1_groups)
    strip_order = []
    for g in a1_groups:
        strip_order.extend(g)
    spill_idx = {s: i for i, s in enumerate(strip_order)}

    nc = bass.Bass("TRN2", target_bir_lowering=False)
    ap = lambda t: t.ap()

    X = ap(nc.dram_tensor("x", [L, D], F32, kind="ExternalInput"))
    WIN = ap(nc.dram_tensor("win", [DPROJ, D], F32, kind="ExternalInput"))
    CONVW = ap(nc.dram_tensor("convw", [CONVD, CK], F32, kind="ExternalInput"))
    CONVB = ap(nc.dram_tensor("convb", [CONVD], F32, kind="ExternalInput"))
    DTB = ap(nc.dram_tensor("dtbias", [H], F32, kind="ExternalInput"))
    ALOG = ap(nc.dram_tensor("alog", [H], F32, kind="ExternalInput"))
    DHEAD = ap(nc.dram_tensor("dhead", [1, H], F32, kind="ExternalInput"))
    NRMW = ap(nc.dram_tensor("nrmw", [DIN], F32, kind="ExternalInput"))
    WOUT = ap(nc.dram_tensor("wout", [D, DIN], F32, kind="ExternalInput"))
    WBASE = ap(nc.dram_tensor("wbase", [D, D], F32, kind="ExternalInput"))
    BBASE = ap(nc.dram_tensor("bbase", [D], F32, kind="ExternalInput"))
    WGATE9 = ap(nc.dram_tensor("wgate9", [9, D], F32, kind="ExternalInput"))
    BTHR = ap(nc.dram_tensor("bthr", [1, 1], F32, kind="ExternalInput"))
    AEXP = ap(nc.dram_tensor("aexp", [128, D], F32, kind="ExternalInput"))
    BEXP = ap(nc.dram_tensor("bexp", [8 * D, 16], F32, kind="ExternalInput"))
    TRIL = ap(nc.dram_tensor("tril", [128, 512], BF16, kind="ExternalInput"))
    I128F = ap(nc.dram_tensor("i128f", [128, 128], F32, kind="ExternalInput"))
    I128B = ap(nc.dram_tensor("i128b", [128, 128], BF16, kind="ExternalInput"))
    EBC8 = ap(nc.dram_tensor("ebc8", [8, 128], BF16, kind="ExternalInput"))
    ONESB = ap(nc.dram_tensor("onesb", [1, 128], BF16, kind="ExternalInput"))
    ONESF = ap(nc.dram_tensor("onesf", [1, 128], F32, kind="ExternalInput"))

    OUT = ap(nc.dram_tensor("out", [L, D], F32, kind="ExternalOutput"))

    def internal(name, shape, dt):
        kind = "ExternalOutput" if name in debug else "Internal"
        return ap(nc.dram_tensor(name, shape, dt, kind=kind))

    WIN_BF = internal("win_bf", [DPROJ, D], BF16)
    WINT = internal("wint", [D, DPROJ], BF16)
    WOUTT = internal("woutt", [DIN, D], BF16)
    WBASET = internal("wbaset", [D, D], BF16)
    WOUT_BF = internal("wout_bf", [D, DIN], BF16)
    WBASE_BF = internal("wbase_bf", [D, D], BF16)
    A_BF = internal("a_bf", [128, D], BF16)
    B_BF = internal("b_bf", [8 * D, 16], BF16)
    WG_BF = internal("wg_bf", [9, D], BF16)
    XLNT = internal("xlnt", [L, D], BF16)
    ZX = internal("zx", [NSTRIP, 128, L], BF16)
    CS_SP = internal("cs_sp", [128, L], F32)
    QN_SP = internal("qn_sp", [128, L], F32)
    OD_SP = internal("od_sp", [128, L], BF16)
    EA_SP = internal("ea_sp", [NCH, 128], F32)
    Y_SP = internal("y_sp", [64, 128, L], BF16)
    ATTN_SP = internal("attn_sp", [32, 128, L], BF16)
    X2_SP = internal("x2_sp", [L, D], F32)
    X2LNT = internal("x2lnt", [L, D], BF16)
    FFN_SP = internal("ffn_sp", [32, 128, L], BF16)

    big = ExitStack()
    with big:
        E = big.enter_context
        S = {}
        for sn in ["bc_r0", "bc_r1", "xt_r0", "xt_r1", "xt_r2",
                   "bt_r0", "bt_r1", "bt_r2", "bt_r3",
                   "csf_r0", "csf_r1", "odf_r0", "odf_r1",
                   "eaf_r0", "eaf_r1", "ysp_r0", "ysp_r1", "bdd", "bpea",
                   "cast_r0", "cast_r1", "a0ld_r0", "a0ld_r1",
                   "a0st_r0", "a0st_r1", "wt_r0", "wt_r1",
                   "a1sp_r0", "a1sp_r1", "a1sp_r2", "a1sp_r3",
                   "p0", "p0t", "p0v", "cast2",
                   "a0ac", "a0v", "xlf",
                   "abWt", "abpe", "abev", "abac", "abdv", "abps", "absp",
                   "a1pe", "a1ev", "a1cv", "a1ac",
                   "bld",
                   "bpeB", "bpeG", "bpeS", "bpeY",
                   "bgev", "bdvE", "bacX", "bodv", "bwof", "bmtt", "bsr",
                   "bsbf", "bev", "bear",
                   "cpe", "cdv", "cac", "cevs",
                   "cx2v", "cx2a",
                   "dpe", "ddv", "dac", "dtrX", "dtrF", "dout"]:
            S[sn] = Cnt(E(nc.semaphore("s_" + sn)))
        for sn in ["zl_r0", "zl_r1", "xl_r0", "xl_r1",
                   "yl_r0", "as_r0", "as_r1",
                   "xt2_r0", "xt2_r1", "x2_r0", "x2_r1",
                   "xl2_r0", "xl2_r1", "ffs_r0", "ffs_r1",
                   "xf_r0", "xf_r1", "ou_r0", "ou_r1", "dbl", "ft2_r0", "ft2_r1", "wb2_r0", "wb2_r1"]:
            S[sn] = Cnt(E(nc.semaphore("s_" + sn)))
        for alias, orig in [("cw_r0", "wt_r0"), ("cw_r1", "wt_r1"),
                            ("at_r0", "xt_r2"), ("at_r1", "bt_r0"),
                            ("ft_r0", "odf_r1"), ("ft_r1", "abWt"),
                            ("wb_r0", "abps"), ("wb_r1", "abpe")]:
            S[alias] = S[orig]

        sb_ones_f = E(nc.sbuf_tensor("c_ones_f", [33, 128], F32))
        sb_ones_b = E(nc.sbuf_tensor("c_ones_b", [33, 128], BF16))
        sb_tril = E(nc.sbuf_tensor("c_tril", [128, 512], BF16))
        sb_i128f = E(nc.sbuf_tensor("c_i128f", [128, 128], F32))
        sb_i128b = E(nc.sbuf_tensor("c_i128b", [128, 128], BF16))
        sb_bthr = E(nc.sbuf_tensor("c_bthr", [128, 1], F32))
        sb_convw = E(nc.sbuf_tensor("c_convw", [128, 80 * CK], F32))
        sb_convb = E(nc.sbuf_tensor("c_convb", [128, 80], F32))
        sb_dtb = E(nc.sbuf_tensor("c_dtb", [128, 1], F32))
        sb_onecol = E(nc.sbuf_tensor("c_onecol", [128, 1], F32))
        sb_epscol = E(nc.sbuf_tensor("c_epscol", [128, 1], F32))
        sb_alog = E(nc.sbuf_tensor("c_alog", [128, 1], F32))
        sb_ealog = E(nc.sbuf_tensor("c_ealog", [128, 1], F32))
        sb_dexp = E(nc.sbuf_tensor("c_dexp", [128, 64], F32))
        sb_dcols = E(nc.sbuf_tensor("c_dcols", [128, 128], F32))
        sb_nrmw = E(nc.sbuf_tensor("c_nrmw", [128, 64], F32))
        sb_bbase = E(nc.sbuf_tensor("c_bbase", [128, 32], F32))
        sQ = ExitStack()
        sb_qT = sQ.enter_context(nc.sbuf_tensor("p_qT", [128, L], F32))
        sb_ddtT = sQ.enter_context(nc.sbuf_tensor("p_ddtT", [128, L], F32))

        psF = E(nc.psum_tensor([128, 3584], F32))
        psB16 = E(nc.psum_tensor([128, 1024], BF16))

        def bank(i, o=0, w=512, p0=0, p1=128):
            return psF[p0:p1, i * 512 + o:i * 512 + o + w]

        castR = RingSem([S["cast_r0"], S["cast_r1"]])
        wt2R = RingSem([S["p0t"], S["p0v"]])
        a0ldR = RingSem([S["a0ld_r0"], S["a0ld_r1"]])
        a0stR = RingSem([S["a0st_r0"], S["a0st_r1"]])
        wtR = RingSem([S["wt_r0"], S["wt_r1"]])
        a1spR = RingSem([S["a1sp_r0"], S["a1sp_r1"], S["a1sp_r2"], S["a1sp_r3"]])

        with nc.Block("p0") as blk:
            @blk.scalar
            def _(sc):
                def slow(dst, src):
                    return sc.dma_start(dst, src, allow_slow_non_contiguous=True)
                for dst, src in [(sb_ones_f[0:1, :], ONESF), (sb_ones_b[0:1, :], ONESB),
                                 (sb_ones_f[32:33, :], ONESF), (sb_ones_b[32:33, :], ONESB),
                                 (sb_tril[:], TRIL), (sb_i128f[:], I128F),
                                 (sb_i128b[:], I128B),
                                 (sb_dtb[:], DTB.rearrange("(p o) -> p o", o=1)),
                                 (sb_alog[:], ALOG.rearrange("(p o) -> p o", o=1)),
                                 (sb_nrmw[:], NRMW.rearrange("(s p) -> p s", p=128)),
                                 (sb_bbase[:], BBASE.rearrange("(s p) -> p s", p=128)),
                                 (sb_dcols[0:1, :], DHEAD)]:
                    S["p0"].inc(slow(dst, src), 16)
                cw = CONVW.rearrange("(s p) k -> p s k", p=128)
                cwd = sb_convw[:].rearrange("p (s k) -> p s k", k=CK)
                for i in range(4):
                    S["p0"].inc(sc.dma_start(cwd[:, i * 20:(i + 1) * 20, :],
                                             cw[:, i * 20:(i + 1) * 20, :]), 16)
                S["p0"].inc(sc.dma_start(sb_bthr[:], BTHR.broadcast_to([128, 1])
                                         if hasattr(BTHR, "broadcast_to") else BTHR), 16)
                cb = CONVB.rearrange("(s p) -> p s", p=128)
                for i in range(2):
                    S["p0"].inc(slow(sb_convb[:, i * 40:(i + 1) * 40],
                                     cb[:, i * 40:(i + 1) * 40]), 16)
                sc.wait_ge(S["p0"].sem, S["p0"].n)
                S["p0"].inc(sc.activation(sb_ealog[:], sb_alog[:], AF.Exp))

            @blk.gpsimd
            def _(gp):
                HDP0 = DPROJ // 2
                for c_ in range(2 * KD):
                    k, hf = c_ // 2, c_ % 2
                    castR.inc(gp.dma_start(
                        WIN_BF[hf * HDP0:(hf + 1) * HDP0, k * 128:(k + 1) * 128],
                        WIN[hf * HDP0:(hf + 1) * HDP0, k * 128:(k + 1) * 128]), c_)

            @blk.tensor
            def _(t):
                t.wait_ge(S["p0"].sem, 16 * 19 + 1)
                S["p0t"].inc(t.matmul(bank(6, 0, 128), sb_ones_f[0:1, 0:128],
                                      sb_dcols[0:1, :], start=True, stop=True))

            @blk.vector
            def _(v):
                v.memset(sb_onecol[:], 1.0)
                v.memset(sb_epscol[:], EPS)
                v.wait_ge(S["p0t"].sem, 1)
                S["p0v"].inc(v.tensor_copy(sb_dcols[:], bank(6, 0, 128)))
                v.wait_ge(S["p0v"].sem, 1)
                dc = sb_dcols[:].rearrange("p (g two) -> p g two", two=2)
                S["p0v"].inc(v.tensor_copy(sb_dexp[0:64, :], dc[0:64, :, 0]))
                S["p0v"].inc(v.tensor_copy(sb_dexp[64:128, :], dc[64:128, :, 1]))

        saX = ExitStack()
        sb_xlnf = saX.enter_context(nc.sbuf_tensor("a_xlnf", [128, KD * L], BF16))

        with ExitStack() as sa0:
            EA = sa0.enter_context
            xt = [EA(nc.sbuf_tensor(f"a0_xt{i}", [128, D], F32)) for i in range(2)]
            stat = EA(nc.sbuf_tensor("a0_stat", [128, 2 * NLT], F32))
            rstd = EA(nc.sbuf_tensor("a0_rstd", [128, NLT], F32))
            msh = EA(nc.sbuf_tensor("a0_msh", [128, NLT], F32))
            mb = EA(nc.sbuf_tensor("a0_m", [128, NLT], F32))
            vb = EA(nc.sbuf_tensor("a0_v", [128, NLT], F32))
            xlnb = EA(nc.sbuf_tensor("a0_xln", [128, D], BF16))
            wintb = [EA(nc.sbuf_tensor(f"a0_wi{i}", [128, DPROJ // 8], BF16)) for i in range(2)]

            with nc.Block("a0") as blk:
                @blk.sync
                def _(sy):
                    for i in range(NLT):
                        if i >= 2:
                            sy.wait_ge(S["a0v"].sem, 2 * (i - 1))
                        a0ldR.inc(sy.dma_start(xt[i % 2][:], X[i * 128:(i + 1) * 128, :]), i)
                    for i in range(NLT):
                        sy.wait_ge(S["a0v"].sem, 2 * NLT + 5)
                        if i >= 2:
                            sy.wait_ge(S["a0ac"].sem, NLT + 2 + (i - 1))
                        a0ldR.inc(sy.dma_start(xt[i % 2][:], X[i * 128:(i + 1) * 128, :]), NLT + i)
                        if i >= 1:
                            sy.wait_ge(S["a0ac"].sem, NLT + 2 + i)
                            a0stR.inc(sy.dma_start(XLNT[(i - 1) * 128:i * 128, :],
                                                   xlnb[:]), i - 1)
                    sy.wait_ge(S["a0ac"].sem, NLT + 2 + NLT)
                    a0stR.inc(sy.dma_start(XLNT[(NLT - 1) * 128:NLT * 128, :],
                                           xlnb[:]), NLT - 1)
                    a0stR.wait_all(sy, NLT)
                    for k in range(KD):
                        S["xlf"].inc(sy.dma_start_transpose(
                            sb_xlnf[:, k * L:(k + 1) * L], XLNT[:, k * 128:(k + 1) * 128]), 16)
                    QDP = DPROJ // 8
                    for t_ in range(8 * KD):
                        k, qf = t_ // 8, t_ % 8
                        castR.wait_done(sy, 2 * k + qf // 4)
                        if t_ >= 2:
                            wt2R.wait_done(sy, t_ - 2)
                        S["dout"].inc(sy.dma_start_transpose(
                            wintb[t_ % 2][:],
                            WIN_BF[qf * QDP:(qf + 1) * QDP, k * 128:(k + 1) * 128]), 16)
                        sy.wait_ge(S["dout"].sem, 16 * (t_ + 1))
                        wt2R.inc(sy.dma_start(
                            WINT[k * 128:(k + 1) * 128, qf * QDP:(qf + 1) * QDP],
                            wintb[t_ % 2][:]), t_)

                @blk.scalar
                def _(sc):
                    for i in range(NLT):
                        a0ldR.wait_done(sc, i)
                        if i >= 1:
                            sc.wait_ge(S["a0v"].sem, 2 * i)
                        S["a0ac"].inc(sc.activation(xlnb[:], xt[i % 2][:], AF.Square))
                    sc.wait_ge(S["a0v"].sem, 2 * NLT + 3)
                    S["a0ac"].inc(sc.activation(rstd[:], vb[:], AF.Ln, bias=sb_epscol[:, 0:1]))
                    sc.wait_ge(S["a0ac"].sem, NLT + 1)
                    S["a0ac"].inc(sc.activation(rstd[:], rstd[:], AF.Exp, scale=-0.5))
                    for i in range(NLT):
                        a0ldR.wait_done(sc, NLT + i)
                        sc.wait_ge(S["a0v"].sem, 2 * NLT + 5)
                        if i >= 1:
                            a0stR.wait_done(sc, i - 1)
                        S["a0ac"].inc(sc.activation(xlnb[:], xt[i % 2][:], AF.Identity,
                                                    bias=msh[:, i:i + 1], scale=rstd[:, i:i + 1]))

                @blk.vector
                def _(v):
                    for i in range(NLT):
                        v.wait_ge(S["a0ac"].sem, i + 1)
                        S["a0v"].inc(v.reduce_sum(stat[:, i:i + 1], xt[i % 2][:], axis=AX.X))
                        S["a0v"].inc(v.reduce_sum(stat[:, NLT + i:NLT + i + 1],
                                                  xlnb[:], axis=AX.X))
                    v.wait_ge(S["a0v"].sem, 2 * NLT)
                    S["a0v"].inc(v.tensor_scalar(mb[:], stat[:, 0:NLT], 1.0 / D, None, ALU.mult))
                    v.wait_ge(S["a0v"].sem, 2 * NLT + 1)
                    S["a0v"].inc(v.tensor_tensor(vb[:], mb[:], mb[:], ALU.mult))
                    v.wait_ge(S["a0v"].sem, 2 * NLT + 2)
                    S["a0v"].inc(v.scalar_tensor_tensor(vb[:], stat[:, NLT:2 * NLT], 1.0 / D,
                                                        vb[:], ALU.mult, ALU.subtract))
                    v.wait_ge(S["a0ac"].sem, NLT + 2)
                    S["a0v"].inc(v.tensor_tensor(msh[:], mb[:], rstd[:], ALU.mult))
                    v.wait_ge(S["a0v"].sem, 2 * NLT + 4)
                    S["a0v"].inc(v.tensor_scalar(msh[:], msh[:], -1.0, None, ALU.mult))

        with ExitStack() as sab:
            EA = sab.enter_context
            wt144 = EA(nc.sbuf_tensor("ab_wt", [128, KD * 128], BF16))
            dtraw = EA(nc.sbuf_tensor("ab_dtraw", [128, L], F32))
            scr1 = EA(nc.sbuf_tensor("ab_s1", [128, L], F32))
            scr2 = EA(nc.sbuf_tensor("ab_s2", [128, L], F32))
            scr3 = EA(nc.sbuf_tensor("ab_s3", [128, L], F32))
            odb = EA(nc.sbuf_tensor("ab_od", [128, L], BF16))
            z256 = EA(nc.sbuf_tensor("ab_z", [128, CH], F32))
            eav = EA(nc.sbuf_tensor("ab_ea", [128, NCH], F32))
            eat = EA(nc.sbuf_tensor("ab_eat", [NCH, 128], F32))

            NT128 = L // 128
            AB_EVAC = NTB
            AB_RELU = AB_EVAC + 1
            AB_DT = AB_RELU + 1
            AB_DA = AB_DT + 1
            AB_MZ = AB_DA + 1
            AB_CS = AB_MZ + NCH
            AB_DDT = AB_CS + 1
            AB_Q = AB_DDT + 1
            AB_TEV = AB_Q + 2 * NT128
            AB_EAT = AB_TEV + 1
            AC_SP = 4
            AC_DDX = AC_SP + NCH
            AC_OD = AC_DDX + 1
            AC_EA = AC_OD + 1
            AC_LNDT = AC_EA + 1

            HDP = DPROJ // 2
            with nc.Block("a0b") as blk:
                @blk.sync
                def _(sy):
                    castR.wait_done(sy, 0)
                    r0 = DTSTRIP * 128
                    for k in range(KD):
                        S["abWt"].inc(sy.dma_start_transpose(
                            wt144[:, k * 128:(k + 1) * 128],
                            WIN_BF[r0:r0 + 128, k * 128:(k + 1) * 128]), 16)


                @blk.tensor
                def _(t):
                    t.wait_ge(S["abWt"].sem, 16 * KD)
                    t.wait_ge(S["xlf"].sem, 16 * KD)
                    for tb in range(NTB):
                        for k in range(KD):
                            mm = t.matmul(bank(tb), wt144[:, k * 128:(k + 1) * 128],
                                          sb_xlnf[:, k * L + tb * 512:k * L + (tb + 1) * 512],
                                          start=(k == 0), stop=(k == KD - 1))
                        S["abpe"].inc(mm)
                    t.wait_ge(S["abdv"].sem, AB_DDT)
                    for i in range(NT128):
                        if i >= 2:
                            t.wait_ge(S["abev"].sem, i - 1)
                        S["abps"].inc(t.transpose(bank(4 + (i % 2), 0, 128),
                                                  scr2[:, i * 128:(i + 1) * 128], sb_i128f[:]))
                    t.wait_ge(S["abdv"].sem, AB_Q)
                    for i in range(NT128):
                        t.wait_ge(S["abev"].sem, NT128 + i - 1)
                        S["abps"].inc(t.transpose(bank(4 + (i % 2), 0, 128),
                                                  dtraw[:, i * 128:(i + 1) * 128], sb_i128f[:]))
                    t.wait_ge(S["abac"].sem, AC_EA)
                    t.wait_ge(S["abev"].sem, 2 * NT128)
                    S["abps"].inc(t.transpose(bank(4, 0, 128, 0, NCH), eav[:], sb_i128f[:]))

                @blk.scalar
                def _(sc):
                    sc.wait_ge(S["abdv"].sem, AB_EVAC)
                    S["abac"].inc(sc.activation(scr1[:], dtraw[:], AF.Identity, bias=sb_dtb[:, 0:1]))
                    sc.wait_ge(S["abac"].sem, 1)
                    S["abac"].inc(sc.activation(scr2[:], scr1[:], AF.Abs))
                    sc.wait_ge(S["abac"].sem, 2)
                    S["abac"].inc(sc.activation(scr3[:], scr2[:], AF.Exp, scale=-1.0))
                    sc.wait_ge(S["abac"].sem, 3)
                    S["abac"].inc(sc.activation(scr2[:], scr3[:], AF.Ln, bias=sb_onecol[:, 0:1]))
                    sc.wait_ge(S["abdv"].sem, AB_CS)
                    for c in range(NCH):
                        S["abac"].inc(sc.activation(scr2[:, c * CH:(c + 1) * CH],
                                                    scr3[:, c * CH:(c + 1) * CH],
                                                    AF.Exp, scale=-1.0,
                                                    bias=scr3[:, c * CH + CH - 1:c * CH + CH]))
                    S["abac"].inc(sc.activation(odb[:], scr3[:], AF.Exp))
                    S["abac"].inc(sc.activation(eav[:], scr3[:, CH - 1::CH], AF.Exp))
                    sc.wait_ge(S["abdv"].sem, AB_DDT)
                    S["abac"].inc(sc.activation(scr1[:], scr1[:], AF.Ln))

                @blk.vector
                def _(v):
                    for tb in range(NTB):
                        v.wait_ge(S["abpe"].sem, tb + 1)
                        S["abdv"].inc(v.tensor_copy(dtraw[:, tb * 512:(tb + 1) * 512], bank(tb)))
                    v.wait_ge(S["abac"].sem, AC_SP)
                    S["abdv"].inc(v.tensor_scalar(scr3[:], scr1[:], 0.0, None, ALU.max))
                    v.wait_ge(S["abdv"].sem, AB_RELU)
                    S["abdv"].inc(v.tensor_tensor(scr1[:], scr2[:], scr3[:], ALU.add))
                    v.wait_ge(S["abdv"].sem, AB_DT)
                    S["abdv"].inc(v.tensor_scalar(scr2[:], scr1[:], sb_ealog[:, 0:1], -1.0,
                                                  ALU.mult, ALU.mult))
                    S["abdv"].inc(v.memset(z256[:], 0.0))
                    v.wait_ge(S["abdv"].sem, AB_MZ)
                    for c in range(NCH):
                        S["abdv"].inc(v.tensor_tensor_scan(scr3[:, c * CH:(c + 1) * CH],
                                                           scr2[:, c * CH:(c + 1) * CH],
                                                           z256[:], 0.0, ALU.add, ALU.add))
                    v.wait_ge(S["abac"].sem, AC_DDX)
                    S["abdv"].inc(v.tensor_tensor(scr2[:], scr2[:], scr1[:], ALU.mult))
                    v.wait_ge(S["abac"].sem, AC_LNDT)
                    S["abdv"].inc(v.tensor_tensor(dtraw[:], scr3[:], scr1[:], ALU.subtract))
                    for i in range(NT128):
                        v.wait_ge(S["abps"].sem, i + 1)
                        S["abev"].inc(v.tensor_copy(sb_ddtT[:, i * 128:(i + 1) * 128],
                                                    bank(4 + (i % 2), 0, 128)))
                    for i in range(NT128):
                        v.wait_ge(S["abps"].sem, NT128 + i + 1)
                        S["abev"].inc(v.tensor_copy(sb_qT[:, i * 128:(i + 1) * 128],
                                                    bank(4 + (i % 2), 0, 128)))
                    v.wait_ge(S["abps"].sem, 2 * NT128 + 1)
                    S["abev"].inc(v.tensor_copy(eat[:], bank(4, 0, 128, 0, NCH)))

                @blk.gpsimd
                def _(gp):
                    gp.wait_ge(S["abdv"].sem, AB_CS)
                    S["absp"].inc(gp.dma_start(CS_SP[:, :], scr3[:]), 16)
                    gp.wait_ge(S["abac"].sem, AC_OD)
                    S["absp"].inc(gp.dma_start(OD_SP[:, :], odb[:]), 16)
                    gp.wait_ge(S["abev"].sem, 2 * NT128 + 1)
                    S["absp"].inc(gp.dma_start(EA_SP[:, :], eat[:]), 16)
                    gp.wait_ge(S["absp"].sem, 48)

            ab_guards = [(S["absp"], 48), (S["abev"], 2 * NT128 + 1),
                         (S["abac"], AC_LNDT), (S["abdv"], AB_Q),
                         (S["abpe"], NTB), (S["abps"], 2 * NT128 + 1)]

        with ExitStack() as sa1:
            EA = sa1.enter_context
            wtbuf = EA(nc.sbuf_tensor("a1_wt", [128, 2 * KD * 256], BF16))
            rawring = [EA(nc.sbuf_tensor(f"a1_raw{i}", [128, L + 3], BF16)) for i in range(4)]
            t0 = EA(nc.sbuf_tensor("a1_t0", [128, L], BF16))
            t1 = EA(nc.sbuf_tensor("a1_t1", [128, L], BF16))

            def guards(eng):
                for cnt, val in ab_guards:
                    eng.wait_ge(cnt.sem, val)

            nconv_before = {}
            nc_ = 0
            for s_ in strip_order:
                nconv_before[s_] = nc_
                if s_ >= 64:
                    nc_ += 1
            NCONV = nc_

            with nc.Block("a1") as blk:
                @blk.sync
                def _(sy):
                    guards(sy)
                    wt2R.wait_all(sy, 8 * KD)
                    for g in range(NGRP):
                        sa, sbb = a1_groups[g]
                        sl = g % 2
                        if g >= 2:
                            sy.wait_ge(S["a1pe"].sem, (g - 1) * NTB)
                        wtR.inc(sy.dma_start(
                            wtbuf[:, sl * KD * 256:(sl + 1) * KD * 256]
                            .rearrange("p (k f) -> p k f", f=256),
                            WINT.rearrange("(k p) f -> p k f", p=128)[:, :, sa * 128:sa * 128 + 256]
                            if False else
                            WINT[:, sa * 128:sa * 128 + 256].rearrange("(k p) f -> p k f", p=128)), g)

                @blk.tensor
                def _(t):
                    guards(t)
                    for g in range(NGRP):
                        sl = g % 2
                        wtR.wait_done(t, g)
                        for tb in range(NTB):
                            s = g * NTB + tb
                            rr = s % 3
                            if s >= 3:
                                t.wait_ge(S["a1ev"].sem, 2 * (s - 2))
                            for k in range(KD):
                                w0 = sl * KD * 256 + k * 256
                                mm = t.matmul(bank(2 * rr),
                                              wtbuf[:, w0:w0 + 128],
                                              sb_xlnf[:, k * L + tb * 512:k * L + (tb + 1) * 512],
                                              start=(k == 0), stop=(k == KD - 1))
                                mm = t.matmul(bank(2 * rr + 1),
                                              wtbuf[:, w0 + 128:w0 + 256],
                                              sb_xlnf[:, k * L + tb * 512:k * L + (tb + 1) * 512],
                                              start=(k == 0), stop=(k == KD - 1))
                            S["a1pe"].inc(mm)

                @blk.vector
                def _(v):
                    guards(v)
                    for rr_ in range(4):
                        v.memset(rawring[rr_][:, 0:3], 0.0)
                    ncv = 0
                    for g in range(NGRP):
                        for tb in range(NTB):
                            s = g * NTB + tb
                            rr = s % 3
                            v.wait_ge(S["a1pe"].sem, s + 1)
                            for j in range(2):
                                strip = a1_groups[g][j]
                                idx = spill_idx[strip]
                                r = idx % 4
                                if tb == 0 and idx >= 4:
                                    a1spR.wait_done(v, idx - 4)
                                S["a1ev"].inc(v.tensor_copy(
                                    rawring[r][:, 3 + tb * 512:3 + (tb + 1) * 512],
                                    bank(2 * rr + j)))
                        if a1_groups[g][0] >= 64:
                            v.wait_ge(S["a1ev"].sem, 2 * NTB * (g + 1))
                        for j in range(2):
                            strip = a1_groups[g][j]
                            if strip < 64:
                                continue
                            idx = spill_idx[strip]
                            r = idx % 4
                            wc = (strip - 64) * CK
                            if nconv_before[strip] >= 1:
                                v.wait_ge(S["a1ac"].sem, nconv_before[strip])
                            raw = rawring[r]
                            S["a1cv"].inc(v.tensor_scalar(t0[:], raw[:, 0:L],
                                                          sb_convw[:, wc:wc + 1], None, ALU.mult))
                            v.wait_ge(S["a1cv"].sem, ncv + 1)
                            S["a1cv"].inc(v.scalar_tensor_tensor(t1[:], raw[:, 1:1 + L],
                                                                 sb_convw[:, wc + 1:wc + 2],
                                                                 t0[:], ALU.mult, ALU.add))
                            v.wait_ge(S["a1cv"].sem, ncv + 2)
                            S["a1cv"].inc(v.scalar_tensor_tensor(t0[:], raw[:, 2:2 + L],
                                                                 sb_convw[:, wc + 2:wc + 3],
                                                                 t1[:], ALU.mult, ALU.add))
                            v.wait_ge(S["a1cv"].sem, ncv + 3)
                            S["a1cv"].inc(v.scalar_tensor_tensor(t1[:], raw[:, 3:3 + L],
                                                                 sb_convw[:, wc + 3:wc + 4],
                                                                 t0[:], ALU.mult, ALU.add))
                            ncv += 4

                @blk.scalar
                def _(sc):
                    guards(sc)
                    nsilu = 0

                    def emit_silu(sc, g):
                        nonlocal nsilu
                        for j in range(2):
                            strip = a1_groups[g][j]
                            if strip < 64:
                                continue
                            idx = spill_idx[strip]
                            r = idx % 4
                            nsilu += 1
                            sc.wait_ge(S["a1cv"].sem, 4 * nsilu)
                            cb = strip - 64
                            S["a1ac"].inc(sc.activation(rawring[r][:, 3:3 + L], t1[:],
                                                        AF.Silu, bias=sb_convb[:, cb:cb + 1]))

                    for g in range(NGRP):
                        emit_silu(sc, g)

                @blk.gpsimd
                def _(gp):
                    guards(gp)
                    nsilu = 0
                    for g in range(NGRP):
                        for j in range(2):
                            strip = a1_groups[g][j]
                            idx = spill_idx[strip]
                            r = idx % 4
                            if strip >= 64:
                                nsilu += 1
                                gp.wait_ge(S["a1ac"].sem, nsilu)
                            else:
                                gp.wait_ge(S["a1ev"].sem, 2 * NTB * (g + 1))
                            a1spR.inc(gp.dma_start(ZX[strip, :, :],
                                                    rawring[r][:, 3:3 + L]), idx)

            ns_ = len(strip_order)
            a1_guards = [(S["a1sp_r0"], 16 * ((ns_ + 3) // 4)), (S["a1sp_r1"], 16 * ((ns_ + 2) // 4)),
                         (S["a1sp_r2"], 16 * ((ns_ + 1) // 4)), (S["a1sp_r3"], 16 * (ns_ // 4)),
                         (S["a1ev"], 2 * NTB * NGRP),
                         (S["a1ac"], NCONV), (S["a1cv"], 4 * NCONV),
                         (S["a1pe"], NTB * NGRP)]

        saX.close()

        if PH >= 1:
            with ExitStack() as sB:
                EB = sB.enter_context
                xtr = [EB(nc.sbuf_tensor(f"b_xt{i}", [128, 8192], BF16)) for i in range(3)]
                btr = [EB(nc.sbuf_tensor(f"b_bt{i}", [128, 1024], BF16)) for i in range(4)]
                bcb = [EB(nc.sbuf_tensor(f"b_bc{i}", [128, 4096], BF16)) for i in range(2)]
                csfl_t = EB(nc.sbuf_tensor("b_csf", [33, 2048], F32))
                odfl_t = EB(nc.sbuf_tensor("b_odf", [33, 2048], BF16))
                eafl_t = EB(nc.sbuf_tensor("b_eaf", [33, 128], F32))
                csfl = [csfl_t[0:1, :], csfl_t[32:33, :]]
                odfl = [odfl_t[0:1, :], odfl_t[32:33, :]]
                eafl = [eafl_t[0:1, :], eafl_t[32:33, :]]
                earep = [EB(nc.sbuf_tensor(f"b_ear{i}", [128, 128], F32)) for i in range(2)]
                gm = [EB(nc.sbuf_tensor(f"b_gm{i}", [128, 512], BF16)) for i in range(2)]
                earg = [EB(nc.sbuf_tensor(f"b_earg{i}", [128, 512], BF16)) for i in range(2)]
                ee = [EB(nc.sbuf_tensor(f"b_ee{i}", [128, 512], BF16)) for i in range(2)]
                mbf = [EB(nc.sbuf_tensor(f"b_m{i}", [128, 512], BF16)) for i in range(2)]
                wof = [EB(nc.sbuf_tensor(f"b_wof{i}", [128, 256], BF16)) for i in range(2)]
                odrep = [EB(nc.sbuf_tensor(f"b_odr{i}", [128, 256], BF16)) for i in range(2)]
                bddb = [EB(nc.sbuf_tensor(f"b_bdd{i}", [128, 256], BF16)) for i in range(2)]
                sbfb = [EB(nc.sbuf_tensor(f"b_sbf{i}", [128, 64], BF16)) for i in range(4)]
                smas = EB(nc.sbuf_tensor("b_s", [128, 8192], F32))
                yst = [EB(nc.sbuf_tensor(f"b_yst{i}", [128, 2048], BF16)) for i in range(2)]

                bcR = RingSem([S["bc_r0"], S["bc_r1"]])
                xtRg = RingSem([S["xt_r0"], S["xt_r1"], S["xt_r2"]])
                btRg = RingSem([S["bt_r0"], S["bt_r1"], S["bt_r2"], S["bt_r3"]])
                csfR = RingSem([S["csf_r0"], S["csf_r1"]])
                odfR = RingSem([S["odf_r0"], S["odf_r1"]])
                eafR = RingSem([S["eaf_r0"], S["eaf_r1"]])
                yspR = RingSem([S["ysp_r0"], S["ysp_r1"]])

                def bguards(eng):
                    for cnt, val in a1_guards:
                        eng.wait_ge(cnt.sem, val)
                    eng.wait_ge(S["absp"].sem, 48)

                XBCV = ZX[64:128, :, :].rearrange("s p l -> (s p) l")
                BTV = ZX[128:136, :, :].rearrange("s p l -> (s p) l")
                BCV = ZX[128:144, :, :].rearrange("s p l -> p s l")

                NGI = NCH * 16

                def flat_load(gp, gi):
                    c0, i = gi // 16, gi % 16
                    if gi >= 2:
                        gp.wait_ge(S["bpeB"].sem, 8 * (gi - 1))
                    csfR.inc(gp.dma_start(
                        csfl[gi % 2].rearrange("o (h l) -> o h l", l=CH),
                        CS_SP[8 * i:8 * i + 8, c0 * CH:(c0 + 1) * CH].rearrange("(o h) l -> o h l", o=1)), gi)
                    odfR.inc(gp.dma_start(
                        odfl[gi % 2].rearrange("o (h l) -> o h l", l=CH),
                        OD_SP[8 * i:8 * i + 8, c0 * CH:(c0 + 1) * CH].rearrange("(o h) l -> o h l", o=1)), gi)

                wtt = [EB(nc.sbuf_tensor(f"b_wtt{i}", [128, 4096], BF16)) for i in range(2)]
                NWT = KDIN + KD if PH >= 2 else 0
                wtrR = RingSem([S["p0"], S["p0t"]])
                wstR = RingSem([S["p0v"], S["dtrX"]])

                def emit_wt(sy, k):
                    sy.wait_ge(S["cast2"].sem, 16 * 15)
                    if k >= 2:
                        wstR.wait_done(sy, k - 2)
                    if k < KDIN:
                        wtrR.inc(sy.dma_start_transpose(
                            wtt[k % 2][:], WOUT_BF[:, k * 128:(k + 1) * 128]), k)
                        wtrR.wait_done(sy, k)
                        wstR.inc(sy.dma_start(WOUTT[k * 128:(k + 1) * 128, :],
                                              wtt[k % 2][:]), k)
                    else:
                        kk = k - KDIN
                        wtrR.inc(sy.dma_start_transpose(
                            wtt[k % 2][:], WBASE_BF[:, kk * 128:(kk + 1) * 128]), k)
                        wtrR.wait_done(sy, k)
                        wstR.inc(sy.dma_start(WBASET[kk * 128:(kk + 1) * 128, :],
                                              wtt[k % 2][:]), k)

                with nc.Block("ssd") as blk:
                    @blk.sync
                    def _(sy):
                        bguards(sy)
                        nwt = 0
                        for c in range(NCH):
                            for half in range(2):
                                hf = 2 * c + half
                                if hf >= 3:
                                    sy.wait_ge(S["bpeY"].sem, 128 * ((hf - 3) // 2 + 1))
                                xtRg.inc(sy.dma_start_transpose(
                                    xtr[hf % 3][:], XBCV[:, hf * 128:(hf + 1) * 128]), hf)
                                if hf >= 4:
                                    sy.wait_ge(S["bsr"].sem, 128 * ((hf - 4) // 2 + 1))
                                btRg.inc(sy.dma_start_transpose(
                                    btr[hf % 4][:], BTV[:, hf * 128:(hf + 1) * 128]), hf)
                            per = (NWT + NCH - 1) // NCH if NCH else 0
                            for _w in range(per):
                                if nwt >= NWT:
                                    break
                                k = nwt
                                emit_wt(sy, k)
                                nwt += 1
                        while nwt < NWT:
                            k = nwt
                            emit_wt(sy, k)
                            nwt += 1

                    @blk.gpsimd
                    def _(gp):
                        bguards(gp)
                        if PH >= 2:
                            for r in range(8):
                                S["cast2"].inc(gp.dma_start(
                                    WOUT_BF[r * 512:(r + 1) * 512, :],
                                    WOUT[r * 512:(r + 1) * 512, :]), 16)
                            for r in range(4):
                                S["cast2"].inc(gp.dma_start(
                                    WBASE_BF[r * 1024:(r + 1) * 1024, :],
                                    WBASE[r * 1024:(r + 1) * 1024, :]), 16)
                            S["cast2"].inc(gp.dma_start(A_BF, AEXP), 16)
                            S["cast2"].inc(gp.dma_start(B_BF, BEXP), 16)
                            S["cast2"].inc(gp.dma_start(WG_BF, WGATE9), 16)
                        S["bld"].inc(gp.memset(smas[:], 0.0))
                        bcv3 = lambda i: bcb[i % 2][:].rearrange("p (s l) -> p s l", l=CH)
                        bcR.inc(gp.dma_start(bcv3(0), BCV[:, :, 0:CH]), 0)
                        if NCH > 1:
                            bcR.inc(gp.dma_start(bcv3(1), BCV[:, :, CH:2 * CH]), 1)
                        eafR.inc(gp.dma_start(eafl[0], EA_SP[0:1, :]), 0)
                        if NCH > 1:
                            eafR.inc(gp.dma_start(eafl[1], EA_SP[1:2, :]), 1)
                        for gi in range(min(2, NGI)):
                            flat_load(gp, gi)
                        for c in range(NCH):
                            for h in range(128):
                                gh = c * 128 + h
                                g = h // 16
                                if h % 8 == 7:
                                    gi = (gh + 1) // 8 + 1
                                    if gi < NGI:
                                        flat_load(gp, gi)
                                if h == 0 and c + 2 < NCH:
                                    gp.wait_ge(S["bear"].sem, c + 1)
                                    eafR.inc(gp.dma_start(eafl[(c + 2) % 2],
                                                          EA_SP[c + 2:c + 3, :]), c + 2)
                                if gh == 0:
                                    gp.wait_ge(S["bld"].sem, 1)
                                if gh >= 4:
                                    gp.wait_ge(S["bpeY"].sem, gh - 3)
                                S["bsbf"].inc(gp.tensor_copy(sbfb[h % 4][:],
                                                             smas[:, h * 64:(h + 1) * 64]))
                                gp.wait_ge(S["bacX"].sem, gh + 1)
                                gp.wait_ge(S["bgev"].sem, 2 * (c * 8 + g + 1))
                                if gh >= 2:
                                    gp.wait_ge(S["bpeY"].sem, gh - 1)
                                S["bmtt"].inc(gp.tensor_tensor(mbf[h % 2][:, 0:256],
                                                               gm[g % 2][:, 0:256],
                                                               ee[h % 2][:, 0:256], ALU.mult))
                                S["bmtt"].inc(gp.tensor_tensor(mbf[h % 2][:, 256:512],
                                                               gm[g % 2][:, 256:512],
                                                               ee[h % 2][:, 256:512], ALU.mult))
                                if h % 16 == 15:
                                    b = h // 16
                                    gb = c * 8 + b
                                    gp.wait_ge(S["bev"].sem, 64 * c + 8 * (b + 1))
                                    yspR.inc(gp.dma_start(
                                        Y_SP[8 * b:8 * b + 8, :, c * CH:(c + 1) * CH]
                                        .rearrange("s p l -> p s l"),
                                        yst[b % 2][:].rearrange("p (s l) -> p s l", l=CH)), gb)
                            if c + 2 < NCH:
                                gp.wait_ge(S["bev"].sem, 64 * (c + 1))
                                bcR.inc(gp.dma_start(bcv3(c + 2),
                                                     BCV[:, :, (c + 2) * CH:(c + 3) * CH]), c + 2)

                    @blk.tensor
                    def _(t):
                        bguards(t)

                        def emit_G(c, g):
                            gg = c * 8 + g
                            bcR.wait_done(t, c)
                            if g == 0:
                                t.wait_ge(S["bear"].sem, c + 1)
                            if gg >= 1:
                                t.wait_ge(S["bgev"].sem, 2 * gg)
                            bb = bcb[c % 2]
                            t.matmul(bank(0, 0, 256), bb[:, g * 256:g * 256 + 128],
                                     bb[:, 2048 + g * 256:2048 + (g + 1) * 256],
                                     start=True, stop=True)
                            S["bpeG"].inc(t.matmul(bank(0, 256, 256),
                                                   bb[:, g * 256 + 128:(g + 1) * 256],
                                                   bb[:, 2048 + g * 256:2048 + (g + 1) * 256],
                                                   start=True, stop=True))

                        def emit_bcast(c, hb):
                            ghb = c * 128 + hb
                            gi = ghb // 8
                            csfR.wait_done(t, gi)
                            odfR.wait_done(t, gi)
                            if ghb >= 2:
                                t.wait_ge(S["bdvE"].sem, 2 * (ghb - 1))
                                t.wait_ge(S["bodv"].sem, ghb - 1)
                            i8 = hb % 8
                            sl32 = 32 * (gi % 2)
                            t.matmul(bank(2 + hb % 2, 0, 256),
                                     sb_ones_f[sl32:sl32 + 1, 0:128],
                                     csfl[gi % 2][:, i8 * 256:(i8 + 1) * 256],
                                     start=True, stop=True)
                            S["bpeB"].inc(t.matmul(bank(2 + hb % 2, 256, 256),
                                                   sb_ones_b[sl32:sl32 + 1, 0:128],
                                                   odfl[gi % 2][:, i8 * 256:(i8 + 1) * 256],
                                                   start=True, stop=True))

                        def emit_states(c, h):
                            gh = c * 128 + h
                            sbk = 1 if h % 2 == 0 else 6
                            t.wait_ge(S["bdd"].sem, 2 * (gh + 1))
                            if gh >= 2:
                                t.wait_ge(S["bsr"].sem, gh - 1)
                            t.matmul(bank(sbk, 0, 64), bddb[h % 2][:, 0:128],
                                     xtr[(2 * c) % 3][:, h * 64:(h + 1) * 64],
                                     start=True, stop=False)
                            S["bpeS"].inc(t.matmul(bank(sbk, 0, 64), bddb[h % 2][:, 128:256],
                                                   xtr[(2 * c + 1) % 3][:, h * 64:(h + 1) * 64],
                                                   start=False, stop=True))

                        def emit_Y(c, h):
                            gh = c * 128 + h
                            pp = h // 2
                            gpp = c * 64 + pp
                            t.wait_ge(S["bmtt"].sem, 2 * (gh + 1))
                            if c > 0:
                                t.wait_ge(S["bwof"].sem, gh + 1)
                                t.wait_ge(S["bsbf"].sem, gh + 1)
                            if h % 2 == 0 and gpp >= 2:
                                t.wait_ge(S["bev"].sem, gpp - 1)
                            reg = bank(4 + pp % 2, 0, 256, p0=(h % 2) * 64, p1=(h % 2) * 64 + 64)
                            last_d = (c == 0)
                            t.matmul(reg, xtr[(2 * c) % 3][:, h * 64:(h + 1) * 64],
                                     mbf[h % 2][:, 0:256], start=True, stop=False,
                                     skip_group_check=True)
                            mm = t.matmul(reg, xtr[(2 * c + 1) % 3][:, h * 64:(h + 1) * 64],
                                          mbf[h % 2][:, 256:512], start=False,
                                          stop=(last_d and h % 2 == 1), skip_group_check=True)
                            if c > 0:
                                mm = t.matmul(reg, sbfb[h % 4][:], wof[h % 2][:],
                                              start=False, stop=(h % 2 == 1),
                                              skip_group_check=True)
                            S["bpeY"].inc(mm)

                        for c in range(NCH):
                            eafR.wait_done(t, c)
                            if c >= 1:
                                t.wait_ge(S["bear"].sem, c)
                                t.wait_ge(S["bgev"].sem, 16 * c)
                                t.wait_ge(S["bsr"].sem, 128 * c)
                            ea32 = 32 * (c % 2)
                            S["bpea"].inc(t.matmul(bank(0, 0, 128),
                                                   sb_ones_f[ea32:ea32 + 1, 0:128],
                                                   eafl[c % 2], start=True, stop=True))
                            if c == 0:
                                xtRg.wait_done(t, 0)
                                xtRg.wait_done(t, 1)
                            else:
                                xtRg.wait_done(t, 2 * c)
                                xtRg.wait_done(t, 2 * c + 1)
                            emit_G(c, 0)
                            emit_bcast(c, 0)
                            emit_bcast(c, 1)
                            for h in range(128):
                                if (h + 2) < 128 and (h + 2) % 16 == 0:
                                    emit_G(c, (h + 2) // 16)
                                if h + 2 < 128:
                                    emit_bcast(c, h + 2)
                                emit_states(c, h)
                                emit_Y(c, h)

                    @blk.vector
                    def _(v):
                        bguards(v)
                        for c in range(NCH):
                            v.wait_ge(S["bpea"].sem, c + 1)
                            S["bear"].inc(v.tensor_copy(earep[c % 2][:], bank(0, 0, 128)))
                            if c == 0:
                                btRg.wait_done(v, 0)
                                btRg.wait_done(v, 1)
                            else:
                                btRg.wait_done(v, 2 * c)
                                btRg.wait_done(v, 2 * c + 1)
                            for h in range(-1, 128):
                                gh = c * 128 + h
                                if h + 1 < 128:
                                    hn = h + 1
                                    ghn = gh + 1
                                    gn = hn // 16
                                    if hn % 16 == 0:
                                        g = gn
                                        gg = c * 8 + g
                                        v.wait_ge(S["bpeG"].sem, gg + 1)
                                        if gg >= 2:
                                            v.wait_ge(S["bmtt"].sem, 32 * (gg - 1))
                                        S["bgev"].inc(v.tensor_tensor(
                                            gm[g % 2][:, 0:256], bank(0, 0, 256),
                                            sb_tril[:, 0:256], ALU.mult))
                                        S["bgev"].inc(v.tensor_tensor(
                                            gm[g % 2][:, 256:512], bank(0, 256, 256),
                                            sb_tril[:, 256:512], ALU.mult))
                                    if ghn >= 2:
                                        v.wait_ge(S["bpeS"].sem, ghn - 1)
                                    S["bdd"].inc(v.tensor_scalar(
                                        bddb[hn % 2][:, 0:128],
                                        btr[(2 * c) % 4][:, gn * 128:(gn + 1) * 128],
                                        sb_ddtT[:, (2 * c) * 128 + hn:(2 * c) * 128 + hn + 1],
                                        None, ALU.mult))
                                    S["bdd"].inc(v.tensor_scalar(
                                        bddb[hn % 2][:, 128:256],
                                        btr[(2 * c + 1) % 4][:, gn * 128:(gn + 1) * 128],
                                        sb_ddtT[:, (2 * c + 1) * 128 + hn:(2 * c + 1) * 128 + hn + 1],
                                        None, ALU.mult))
                                    v.wait_ge(S["bpeB"].sem, ghn + 1)
                                    if ghn >= 2:
                                        v.wait_ge(S["bacX"].sem, ghn - 1)
                                    S["bdvE"].inc(v.tensor_scalar(
                                        earg[hn % 2][:, 0:256], bank(2 + hn % 2, 0, 256),
                                        sb_qT[:, (2 * c) * 128 + hn:(2 * c) * 128 + hn + 1],
                                        ECLAMP, ALU.subtract, ALU.min))
                                    S["bdvE"].inc(v.tensor_scalar(
                                        earg[hn % 2][:, 256:512], bank(2 + hn % 2, 0, 256),
                                        sb_qT[:, (2 * c + 1) * 128 + hn:(2 * c + 1) * 128 + hn + 1],
                                        ECLAMP, ALU.subtract, ALU.min))
                                    v.wait_ge(S["bodv"].sem, ghn + 1)
                                    if ghn >= 2:
                                        v.wait_ge(S["bpeY"].sem, ghn - 1)
                                    S["bwof"].inc(v.tensor_tensor(
                                        wof[hn % 2][:],
                                        bcb[c % 2][:, 2048 + gn * 256:2048 + (gn + 1) * 256],
                                        odrep[hn % 2][:], ALU.mult))
                                if h >= 0:
                                    v.wait_ge(S["bpeS"].sem, gh + 1)
                                    v.wait_ge(S["bear"].sem, c + 1)
                                    v.wait_ge(S["bsbf"].sem, gh + 1)
                                    S["bsr"].inc(v.scalar_tensor_tensor(
                                        smas[:, h * 64:(h + 1) * 64],
                                        smas[:, h * 64:(h + 1) * 64],
                                        earep[c % 2][:, h:h + 1],
                                        bank(1 if h % 2 == 0 else 6, 0, 64),
                                        ALU.mult, ALU.add))
                                    if h % 2 == 1:
                                        pp = h // 2
                                        b = pp // 8
                                        gb = c * 8 + b
                                        v.wait_ge(S["bpeY"].sem, gh + 1)
                                        if gb >= 2:
                                            yspR.wait_done(v, gb - 2)
                                        S["bev"].inc(v.tensor_copy(
                                            yst[b % 2][:, (pp % 8) * 256:(pp % 8 + 1) * 256],
                                            bank(4 + pp % 2, 0, 256)))

                    @blk.scalar
                    def _(sc):
                        bguards(sc)
                        for c in range(NCH):
                            for h in range(128):
                                gh = c * 128 + h
                                sc.wait_ge(S["bdvE"].sem, 2 * (gh + 1))
                                if gh >= 2:
                                    sc.wait_ge(S["bmtt"].sem, 2 * (gh - 1))
                                S["bacX"].inc(sc.activation(ee[h % 2][:], earg[h % 2][:], AF.Exp))
                                sc.wait_ge(S["bpeB"].sem, gh + 1)
                                if gh >= 2:
                                    sc.wait_ge(S["bwof"].sem, gh - 1)
                                S["bodv"].inc(sc.activation(odrep[h % 2][:],
                                                            bank(2 + h % 2, 256, 256),
                                                            AF.Identity))

            b_guards = [(S["p0"], S["p0"].n), (S["p0t"], S["p0t"].n),
                        (S["p0v"], S["p0v"].n), (S["dtrX"], S["dtrX"].n),
                        (S["bsr"], 128 * NCH), (S["bev"], 64 * NCH),
                        (S["ysp_r0"], 16 * (4 * NCH)), (S["ysp_r1"], 16 * (4 * NCH)),
                        (S["bacX"], 128 * NCH), (S["bodv"], 128 * NCH),
                        (S["bmtt"], 256 * NCH), (S["bdd"], 256 * NCH),
                        (S["bdvE"], 256 * NCH), (S["bpeY"], 128 * NCH),
                        (S["bgev"], 16 * NCH)]

        sQ.close()

        if PH >= 2:
            NQ = L // 512
            cwR = RingSem([S["cw_r0"], S["cw_r1"]])
            zldR = RingSem([S["zl_r0"], S["zl_r1"]])
            xdlR = RingSem([S["xl_r0"], S["xl_r1"]])
            ylR = RingSem([S["yl_r0"]])
            aspR = RingSem([S["as_r0"], S["as_r1"]])

            def cdvp(gq, st, op):
                return gq * 320 + st * 4 + op

            def cacp(gq, st):
                return gq * 66 + st + 1

            with ExitStack() as sC:
                EC = sC.enter_context
                ybuf = EC(nc.sbuf_tensor("c_y", [128, 64 * 512], BF16))
                wtog = [EC(nc.sbuf_tensor(f"c_wt{i}", [128, KDIN * 256], BF16)) for i in range(2)]
                zring = [EC(nc.sbuf_tensor(f"c_z{i}", [128, 512], BF16)) for i in range(4)]
                szb = [EC(nc.sbuf_tensor(f"c_sz{i}", [128, 512], BF16)) for i in range(4)]
                xdr = [EC(nc.sbuf_tensor(f"c_xd{i}", [128, 512], BF16)) for i in range(4)]
                sqb2 = EC(nc.sbuf_tensor("c_sq", [128, 512], F32))
                acc = EC(nc.sbuf_tensor("c_acc", [128, 512], F32))
                rsqv = EC(nc.sbuf_tensor("c_rsq", [1, 512], F32))
                ost = [EC(nc.sbuf_tensor(f"c_o{i}", [128, 512], BF16)) for i in range(4)]

                def cguards(eng):
                    for cnt, val in b_guards:
                        eng.wait_ge(cnt.sem, val)

                with nc.Block("c1") as blk:
                    @blk.gpsimd
                    def _(gp):
                        cguards(gp)
                        for q in range(NQ):
                            if q >= 1:
                                gp.wait_ge(S["cevs"].sem, 32 * q)
                            ylR.inc(gp.dma_start(
                                ybuf[:].rearrange("p (s l) -> p s l", l=512),
                                Y_SP[:, :, q * 512:(q + 1) * 512]
                                .rearrange("s p l -> p s l")), q)
                            for st in range(64):
                                gi = q * 64 + st
                                if gi >= 2:
                                    pq, pst = divmod(gi - 2, 64)
                                    gp.wait_ge(S["cac"].sem, cacp(pq, pst))
                                    gp.wait_ge(S["cdv"].sem, cdvp(pq, pst, 1))
                                zldR.inc(gp.dma_start(zring[st % 4][:],
                                                      ZX[st, :, q * 512:(q + 1) * 512]), gi)
                                xdlR.inc(gp.dma_start(xdr[st % 4][:],
                                                      ZX[64 + st, :, q * 512:(q + 1) * 512]), gi)
                            for og in range(16):
                                for j in range(2):
                                    sidx = q * 32 + 2 * og + j
                                    gp.wait_ge(S["cevs"].sem, sidx + 1)
                                    if sidx >= 2:
                                        aspR.wait_done(gp, sidx - 2)
                                    aspR.inc(gp.dma_start(
                                        ATTN_SP[2 * og + j, :, q * 512:(q + 1) * 512],
                                        ost[sidx % 4][:]), sidx)

                    @blk.scalar
                    def _(sc):
                        cguards(sc)
                        for q in range(NQ):
                            for st in range(64):
                                gi = q * 64 + st
                                zldR.wait_done(sc, gi)
                                if gi >= 4:
                                    pq, pst = divmod(gi - 4, 64)
                                    sc.wait_ge(S["cdv"].sem, cdvp(pq, pst, 2))
                                S["cac"].inc(sc.activation(szb[st % 4][:], zring[st % 4][:],
                                                           AF.Silu))
                            sc.wait_ge(S["cpe"].sem, q * 18 + 1)
                            S["cac"].inc(sc.activation(rsqv[0:1, :], bank(0, 0, 512, 0, 1),
                                                       AF.Ln, bias=sb_epscol[0:1, 0:1],
                                                       scale=1.0 / DIN))
                            sc.wait_ge(S["cac"].sem, q * 66 + 65)
                            S["cac"].inc(sc.activation(rsqv[0:1, :], rsqv[0:1, :],
                                                       AF.Exp, scale=-0.5))

                    @blk.vector
                    def _(v):
                        cguards(v)
                        for q in range(NQ):
                            ylR.wait_done(v, q)
                            for st in range(64):
                                gi = q * 64 + st
                                xdlR.wait_done(v, gi)
                                ys = ybuf[:, st * 512:(st + 1) * 512]
                                S["cdv"].inc(v.scalar_tensor_tensor(
                                    ys, xdr[st % 4][:], sb_dexp[:, st:st + 1], ys,
                                    ALU.mult, ALU.add))
                                v.wait_ge(S["cac"].sem, cacp(q, st))
                                v.wait_ge(S["cdv"].sem, cdvp(q, st, 1))
                                S["cdv"].inc(v.tensor_tensor(ys, ys, szb[st % 4][:], ALU.mult))
                                v.wait_ge(S["cdv"].sem, cdvp(q, st, 2))
                                S["cdv"].inc(v.tensor_tensor(sqb2[:], ys, ys, ALU.mult))
                                v.wait_ge(S["cdv"].sem, cdvp(q, st, 3))
                                if st == 0:
                                    S["cdv"].inc(v.tensor_copy(acc[:], sqb2[:]))
                                else:
                                    S["cdv"].inc(v.tensor_tensor(acc[:], acc[:], sqb2[:],
                                                                 ALU.add))
                            v.wait_ge(S["cpe"].sem, q * 18 + 2)
                            for st in range(64):
                                ys = ybuf[:, st * 512:(st + 1) * 512]
                                S["cdv"].inc(v.scalar_tensor_tensor(
                                    ys, ys, sb_nrmw[:, st:st + 1], bank(1, 0, 512),
                                    ALU.mult, ALU.mult))
                            for og in range(16):
                                gog = q * 16 + og
                                rr = gog % 2
                                v.wait_ge(S["cpe"].sem, q * 18 + 2 + og + 1)
                                for j in range(2):
                                    sidx = q * 32 + 2 * og + j
                                    if sidx >= 4:
                                        aspR.wait_done(v, sidx - 4, uses=(sidx - 4) // 2 + 1)
                                    S["cevs"].inc(v.tensor_copy(ost[sidx % 4][:],
                                                                bank(2 + 2 * ((rr * 2 + j) // 2) + j, 0, 512)
                                                                if False else
                                                                bank(2 + 2 * rr + j, 0, 512)))

                    @blk.tensor
                    def _(t):
                        cguards(t)
                        wstR.wait_done(t, KDIN - 2)
                        wstR.wait_done(t, KDIN - 1)
                        for q in range(NQ):
                            t.wait_ge(S["cdv"].sem, q * 320 + 256)
                            S["cpe"].inc(t.matmul(bank(0, 0, 512, 0, 1), sb_onecol[:, 0:1],
                                                  acc[:], start=True, stop=True))
                            t.wait_ge(S["cac"].sem, q * 66 + 66)
                            S["cpe"].inc(t.matmul(bank(1, 0, 512), sb_ones_f[0:1, 0:128],
                                                  rsqv[0:1, :], start=True, stop=True))
                            for og in range(16):
                                gog = q * 16 + og
                                cwR.wait_done(t, gog)
                                rr = gog % 2
                                t.wait_ge(S["cdv"].sem, (q + 1) * 320)
                                if gog >= 2:
                                    t.wait_ge(S["cevs"].sem, 2 * (gog - 1))
                                for k in range(KDIN):
                                    for j in range(2):
                                        mm = t.matmul(
                                            bank(2 + 2 * rr + j, 0, 512),
                                            wtog[gog % 2][:, k * 256 + j * 128:k * 256 + (j + 1) * 128],
                                            ybuf[:, k * 512:(k + 1) * 512],
                                            start=(k == 0), stop=(k == KDIN - 1))
                                    if k == KDIN - 1:
                                        S["cpe"].inc(mm)

                    @blk.sync
                    def _(sy):
                        cguards(sy)
                        wstR.wait_done(sy, KDIN - 2)
                        wstR.wait_done(sy, KDIN - 1)
                        for q in range(NQ):
                            for og in range(16):
                                gog = q * 16 + og
                                if gog >= 2:
                                    pq, pog = divmod(gog - 2, 16)
                                    sy.wait_ge(S["cpe"].sem, pq * 18 + 2 + pog + 1)
                                cwR.inc(sy.dma_start(
                                    wtog[gog % 2][:].rearrange("p (k o) -> p k o", o=256),
                                    WOUTT[:, og * 256:(og + 1) * 256]
                                    .rearrange("(k p) o -> p k o", p=128)), gog)

            with ExitStack() as sC2:
                EC = sC2.enter_context
                att = [EC(nc.sbuf_tensor(f"c2_a{i}", [128, D], BF16)) for i in range(2)]
                xtl = [EC(nc.sbuf_tensor(f"c2_x{i}", [128, D], F32)) for i in range(2)]
                xlb = [EC(nc.sbuf_tensor(f"c2_l{i}", [128, D], BF16)) for i in range(2)]
                x2b = [EC(nc.sbuf_tensor(f"c2_x2{i}", [128, D], F32)) for i in range(2)]
                x2l = EC(nc.sbuf_tensor("c2_xl", [128, D], BF16))
                sqc = EC(nc.sbuf_tensor("c2_sq", [128, D], F32))
                stc = EC(nc.sbuf_tensor("c2_st", [128, 8], F32))
                atlR = RingSem([S["at_r0"], S["at_r1"]])
                xtlR = RingSem([S["xt2_r0"], S["xt2_r1"]])
                xlbR = RingSem([S["zl_r0"], S["zl_r1"]])
                x2sR = RingSem([S["x2_r0"], S["x2_r1"]])
                xlsR = RingSem([S["xl2_r0"], S["xl2_r1"]])

                ATV = ATTN_SP.rearrange("s p l -> (s p) l")

                with nc.Block("c2") as blk:
                    @blk.sync
                    def _(sy):
                        aspR.wait_all(sy, 32 * NQ)
                        for i in range(NLT):
                            if i >= 2:
                                sy.wait_ge(S["cx2v"].sem, 9 * (i - 1))
                            atlR.inc(sy.dma_start_transpose(
                                att[i % 2][:], ATV[:, i * 128:(i + 1) * 128]), i)

                    @blk.gpsimd
                    def _(gp):
                        for i in range(NLT):
                            if i >= 2:
                                gp.wait_ge(S["cx2v"].sem, 9 * (i - 1))
                            xtlR.inc(gp.dma_start(xtl[i % 2][:], X[i * 128:(i + 1) * 128, :]), i)
                            xlbR.inc(gp.dma_start(xlb[i % 2][:],
                                                  XLNT[i * 128:(i + 1) * 128, :]), i)
                            gp.wait_ge(S["cx2v"].sem, 9 * i + 2)
                            x2sR.inc(gp.dma_start(X2_SP[i * 128:(i + 1) * 128, :],
                                                  x2b[i % 2][:]), i)
                            gp.wait_ge(S["cx2a"].sem, 4 * i + 4)
                            xlsR.inc(gp.dma_start(X2LNT[i * 128:(i + 1) * 128, :], x2l[:]), i)

                    @blk.vector
                    def _(v):
                        for i in range(NLT):
                            atlR.wait_done(v, i)
                            xtlR.wait_done(v, i)
                            xlbR.wait_done(v, i)
                            if i >= 2:
                                x2sR.wait_done(v, i - 2)
                                v.wait_ge(S["cx2a"].sem, 4 * (i - 1))
                            S["cx2v"].inc(v.tensor_tensor(x2b[i % 2][:], xtl[i % 2][:],
                                                          att[i % 2][:], ALU.add))
                            v.wait_ge(S["cx2v"].sem, 9 * i + 1)
                            S["cx2v"].inc(v.tensor_tensor(x2b[i % 2][:], x2b[i % 2][:],
                                                          xlb[i % 2][:], ALU.add))
                            v.wait_ge(S["cx2v"].sem, 9 * i + 2)
                            S["cx2v"].inc(v.reduce_sum(stc[:, 0:1], x2b[i % 2][:], axis=AX.X))
                            v.wait_ge(S["cx2a"].sem, 4 * i + 1)
                            S["cx2v"].inc(v.reduce_sum(stc[:, 1:2], sqc[:], axis=AX.X))
                            v.wait_ge(S["cx2v"].sem, 9 * i + 4)
                            S["cx2v"].inc(v.tensor_scalar(stc[:, 2:3], stc[:, 0:1], 1.0 / D,
                                                          None, ALU.mult))
                            v.wait_ge(S["cx2v"].sem, 9 * i + 5)
                            S["cx2v"].inc(v.tensor_tensor(stc[:, 3:4], stc[:, 2:3],
                                                          stc[:, 2:3], ALU.mult))
                            v.wait_ge(S["cx2v"].sem, 9 * i + 6)
                            S["cx2v"].inc(v.scalar_tensor_tensor(
                                stc[:, 4:5], stc[:, 1:2], 1.0 / D, stc[:, 3:4],
                                ALU.mult, ALU.subtract))
                            v.wait_ge(S["cx2a"].sem, 4 * i + 3)
                            S["cx2v"].inc(v.tensor_tensor(stc[:, 7:8], stc[:, 2:3],
                                                          stc[:, 6:7], ALU.mult))
                            v.wait_ge(S["cx2v"].sem, 9 * i + 8)
                            S["cx2v"].inc(v.tensor_scalar(stc[:, 7:8], stc[:, 7:8], -1.0,
                                                          None, ALU.mult))

                    @blk.scalar
                    def _(sc):
                        for i in range(NLT):
                            sc.wait_ge(S["cx2v"].sem, 9 * i + 2)
                            S["cx2a"].inc(sc.activation(sqc[:], x2b[i % 2][:], AF.Square))
                            sc.wait_ge(S["cx2v"].sem, 9 * i + 7)
                            S["cx2a"].inc(sc.activation(stc[:, 5:6], stc[:, 4:5], AF.Ln,
                                                        bias=sb_epscol[:, 0:1]))
                            sc.wait_ge(S["cx2a"].sem, 4 * i + 2)
                            S["cx2a"].inc(sc.activation(stc[:, 6:7], stc[:, 5:6], AF.Exp,
                                                        scale=-0.5))
                            sc.wait_ge(S["cx2v"].sem, 9 * i + 9)
                            if i >= 1:
                                xlsR.wait_done(sc, i - 1)
                            S["cx2a"].inc(sc.activation(x2l[:], x2b[i % 2][:], AF.Identity,
                                                        scale=stc[:, 6:7], bias=stc[:, 7:8]))

            c2_guards = [(S["x2_r0"], 16 * ((NLT + 1) // 2)), (S["x2_r1"], 16 * (NLT // 2)),
                         (S["xl2_r0"], 16 * ((NLT + 1) // 2)), (S["xl2_r1"], 16 * (NLT // 2)),
                         (S["cx2a"], 4 * NLT), (S["cx2v"], 9 * NLT)]

        if PH >= 3:
            d_guards = list(b_guards) + [
                (S["cevs"], 32 * NQ), (S["cpe"], 18 * NQ),
                (S["cdv"], 320 * NQ), (S["cac"], 66 * NQ),
                (S["cx2v"], 9 * NLT), (S["cx2a"], 4 * NLT),
                (S["x2_r0"], S["x2_r0"].n), (S["x2_r1"], S["x2_r1"].n),
                (S["xl2_r0"], S["xl2_r0"].n), (S["xl2_r1"], S["xl2_r1"].n),
                (S["as_r0"], S["as_r0"].n), (S["as_r1"], S["as_r1"].n)]
            wbR = RingSem([S["wb2_r0"], S["wb2_r1"]])
            ffsR = RingSem([S["ffs_r0"], S["ffs_r1"]])
            bldR = RingSem([S["dbl"]])
            xlfD = Cnt(S["xlf"].sem)
            xlfD.n = S["xlf"].n
            S["xlf"] = xlfD
            dtrD = S["dtrX"]
            dtr_base = dtrD.n
            xlf_base = xlfD.n

            def DPE_BFT(i):
                return i + 1
            DPE_WGT0 = 64
            DPE_GATE0 = 96
            def GDONE(i):
                return 97 + i if i <= 2 else 95 + 2 * i
            def TRDONE(j):
                if j <= NLT - 3:
                    return 100 + 2 * j
                return 95 + 2 * NLT + (j - (NLT - 2))
            DPE_WEBC0 = 96 + 2 * NLT
            DPE_H0 = DPE_WEBC0 + NTB
            DPE_OG0 = DPE_H0 + NTB
            def DDV_BFT(i):
                return i + 2
            DDV_WGT0 = 65
            DDV_GATE0 = 97
            NGOP = 11
            DDV_WT0 = DDV_GATE0 + NGOP * NLT
            DDV_WEBC0 = DDV_WT0
            DDV_H0 = DDV_WEBC0 + NTB
            DDV_HW = DDV_H0 + NTB + 1
            DDV_EV0 = DDV_HW

            with ExitStack() as sD:
                ED = sD.enter_context
                x2lnf = ED(nc.sbuf_tensor("d_xlf", [128, KD * L], BF16))
                wtb = [ED(nc.sbuf_tensor(f"d_wt{i}", [128, KD * 256], BF16)) for i in range(2)]
                atb = ED(nc.sbuf_tensor("d_at", [128, KD * 128], BF16))
                wgbf = ED(nc.sbuf_tensor("d_wg", [9, D], BF16))
                wgT = ED(nc.sbuf_tensor("d_wgT", [128, KD * 16], BF16))
                ebc8b = ED(nc.sbuf_tensor("d_ebc", [8, 128], BF16))
                bft = ED(nc.sbuf_tensor("d_bft", [128, D], BF16))
                btmp = ED(nc.sbuf_tensor("d_btmp", [128, 64], BF16))
                gbuf = ED(nc.sbuf_tensor("d_gb", [128, 40], F32))
                wnb = [ED(nc.sbuf_tensor(f"d_wn{i}", [128, 8], BF16)) for i in range(2)]
                wT = ED(nc.sbuf_tensor("d_wT", [8, L], BF16))
                webc = ED(nc.sbuf_tensor("d_webc", [128, L], BF16))
                hbuf = ED(nc.sbuf_tensor("d_h", [128, L], BF16))
                fstg = [ED(nc.sbuf_tensor(f"d_f{i}", [128, 512], BF16)) for i in range(4)]

                def dguards(eng):
                    for cnt, val in d_guards:
                        eng.wait_ge(cnt.sem, val)

                DCUT2 = os.environ.get("MK_DCUT") == "2"
                DCUT3 = os.environ.get("MK_DCUT") in ("3", "4")
                DCUT4 = os.environ.get("MK_DCUT") in ("4", "5")
                DCUT5 = os.environ.get("MK_DCUT") in ("5", "6")
                DCUT6 = os.environ.get("MK_DCUT") == "6"
                with nc.Block("d12") as blk:
                    @blk.sync
                    def _(sy):
                        dguards(sy)
                        for k in range(KD):
                            dtrD.inc(sy.dma_start_transpose(
                                x2lnf[:, k * L:(k + 1) * L],
                                X2LNT[:, k * 128:(k + 1) * 128]), 16)
                        for k in range(KD):
                            xlfD.inc(sy.dma_start_transpose(
                                atb[:, k * 128:(k + 1) * 128],
                                A_BF[:, k * 128:(k + 1) * 128]), 16)
                        xlfD.inc(sy.dma_start(wgbf[:], WG_BF), 16)
                        xlfD.inc(sy.dma_start(ebc8b[:], EBC8), 16)
                        for og in range(0 if (DCUT2 or DCUT3) else 16):
                            if og >= 2:
                                sy.wait_ge(S["dpe"].sem, DPE_OG0 + (og - 1) * NTB)
                            wbR.inc(sy.dma_start(
                                wtb[og % 2][:].rearrange("p (k o) -> p k o", o=256),
                                WBASET[:, og * 256:(og + 1) * 256]
                                .rearrange("(k p) o -> p k o", p=128)), og)

                    @blk.gpsimd
                    def _(gp):
                        dguards(gp)
                        for it in range(64):
                            g4, ot = divmod(it, KD)
                            if it >= 1:
                                gp.wait_ge(S["ddv"].sem, DDV_BFT(it - 1))
                            for j in range(4):
                                e = 4 * g4 + j
                                bldR.inc(gp.dma_start(
                                    btmp[:, j * 16:(j + 1) * 16],
                                    B_BF[e * D + ot * 128:e * D + (ot + 1) * 128, :]),
                                    it * 4 + j)
                        for og in range(0 if (DCUT2 or DCUT3) else 16):
                            for tb in range(NTB):
                                for j in range(2):
                                    sidx = (og * NTB + tb) * 2 + j
                                    gp.wait_ge(S["ddv"].sem, DDV_EV0 + sidx + 1)
                                    if sidx >= 2:
                                        ffsR.wait_done(gp, sidx - 2)
                                    ffsR.inc(gp.dma_start(
                                        FFN_SP[2 * og + j, :, tb * 512:(tb + 1) * 512],
                                        fstg[sidx % 4][:]), sidx)

                    @blk.tensor
                    def _(t):
                        dguards(t)
                        for it in range(64):
                            t.wait_ge(S["dbl"].sem, 16 * 4 * (it + 1))
                            S["dpe"].inc(t.transpose(psB16[0:64, 0:128], btmp[:],
                                                     sb_i128b[:]))
                        t.wait_ge(S["xlf"].sem, xlf_base + 16 * (KD + 2))
                        t.wait_ge(S["ddv"].sem, 65)
                        for k in range(KD):
                            if k >= 1:
                                t.wait_ge(S["ddv"].sem, DDV_WGT0 + k)
                            S["dpe"].inc(t.transpose(
                                psB16[0:128, 128 + (k % 2) * 16:128 + (k % 2) * 16 + 9],
                                wgbf[0:9, k * 128:(k + 1) * 128], sb_i128b[0:9, 0:9]))
                        t.wait_ge(S["dtrX"].sem, dtr_base + 16 * KD)
                        t.wait_ge(S["ddv"].sem, DDV_WGT0 + KD)
                        if DCUT2:
                            return

                        def wn_tr(i):
                            t.wait_ge(S["ddv"].sem, DDV_GATE0 + i * NGOP + NGOP)
                            if i >= 1:
                                t.wait_ge(S["dtrF"].sem, i)
                            S["dpe"].inc(t.transpose(
                                psB16[0:8, 512 + (i % 2) * 128:512 + (i % 2) * 128 + 128],
                                wnb[i % 2][:], sb_i128b[:]))

                        ntr = 0
                        for i in range(NLT if "gates" not in DSKIP else 0):
                            if i >= 2:
                                t.wait_ge(S["ddv"].sem, DDV_GATE0 + (i - 1) * NGOP)
                            for k in range(KD):
                                mm = t.matmul(bank(i % 2, 0, 16),
                                              x2lnf[:, k * L + i * 128:k * L + (i + 1) * 128],
                                              wgT[:, k * 16:(k + 1) * 16],
                                              start=(k == 0), stop=(k == KD - 1))
                            S["dpe"].inc(mm)
                            if not DCUT5 and i >= 2:
                                wn_tr(i - 2)
                                ntr += 1
                        while not DCUT5 and ntr < NLT:
                            wn_tr(ntr)
                            ntr += 1
                        for tb in range(0 if DCUT4 else NTB):
                            t.wait_ge(S["ddv"].sem, DDV_WT0)
                            t.wait_ge(S["dtrF"].sem, NLT)
                            if tb >= 2:
                                t.wait_ge(S["ddv"].sem, DDV_WEBC0 + tb - 1)
                            S["dpe"].inc(t.matmul(bank(tb % 2), ebc8b[:],
                                                  wT[:, tb * 512:(tb + 1) * 512],
                                                  start=True, stop=True))
                        for tb in range(0 if DCUT4 else NTB):
                            if tb >= 2:
                                t.wait_ge(S["ddv"].sem, DDV_H0 + tb - 1)
                            for k in range(KD):
                                mm = t.matmul(bank(2 + tb % 2),
                                              atb[:, k * 128:(k + 1) * 128],
                                              x2lnf[:, k * L + tb * 512:k * L + (tb + 1) * 512],
                                              start=(k == 0), stop=(k == KD - 1))
                            S["dpe"].inc(mm)
                        t.wait_ge(S["ddv"].sem, DDV_HW)
                        for og in range(0 if DCUT3 else 16):
                            wbR.wait_done(t, og)
                            for tb in range(NTB):
                                gg = og * NTB + tb
                                rr = gg % 2
                                if gg >= 2:
                                    t.wait_ge(S["ddv"].sem, DDV_EV0 + 2 * (gg - 1))
                                for j in range(2):
                                    for k in range(KD):
                                        mm = t.matmul(
                                            bank(2 + 2 * rr + j),
                                            wtb[og % 2][:, k * 256 + j * 128:k * 256 + (j + 1) * 128],
                                            x2lnf[:, k * L + tb * 512:k * L + (tb + 1) * 512],
                                            start=(k == 0), stop=False)
                                    mm = t.matmul(
                                        bank(2 + 2 * rr + j),
                                        bft[:, (2 * og + j) * 128:(2 * og + j + 1) * 128],
                                        hbuf[:, tb * 512:(tb + 1) * 512],
                                        start=False, stop=True)
                                S["dpe"].inc(mm)

                    @blk.vector
                    def _(v):
                        dguards(v)
                        S["ddv"].inc(v.memset(wgT[:], 0.0))
                        for it in range(64):
                            g4 = it // KD
                            ot = it % KD
                            v.wait_ge(S["dpe"].sem, DPE_BFT(it))
                            S["ddv"].inc(v.tensor_copy(
                                bft[64 * g4:64 * g4 + 64, ot * 128:(ot + 1) * 128],
                                psB16[0:64, 0:128]))
                        for k in range(KD):
                            v.wait_ge(S["dpe"].sem, DPE_WGT0 + k + 1)
                            S["ddv"].inc(v.tensor_copy(
                                wgT[:, k * 16:k * 16 + 9],
                                psB16[0:128, 128 + (k % 2) * 16:128 + (k % 2) * 16 + 9]))
                        if DCUT2:
                            return
                        for i in range(NLT):
                            v.wait_ge(S["dpe"].sem, GDONE(i))
                            gb = gbuf[:, (i % 2) * 16:(i % 2) * 16 + 16]
                            S["ddv"].inc(v.tensor_copy(gb[:, 0:9], bank(i % 2, 0, 9)))
                            if DCUT6:
                                for _pad in range(NGOP - 1):
                                    S["ddv"].inc(v.tensor_copy(gbuf[:, 32 + (_pad % 8):33 + (_pad % 8)],
                                                               gbuf[:, 0:1]))
                                continue
                            v.wait_ge(S["ddv"].sem, DDV_GATE0 + i * NGOP + 1)
                            S["ddv"].inc(v.reduce_max(gb[:, 9:10], gb[:, 0:8], axis=AX.X))
                            v.wait_ge(S["ddv"].sem, DDV_GATE0 + i * NGOP + 2)
                            S["ddv"].inc(v.tensor_scalar(gb[:, 10:11], gb[:, 9:10], -1.0,
                                                         None, ALU.mult))
                            v.wait_ge(S["dac"].sem, 3 * i + 1)
                            S["ddv"].inc(v.reduce_sum(gb[:, 11:12], gb[:, 0:8], axis=AX.X))
                            v.wait_ge(S["ddv"].sem, DDV_GATE0 + i * NGOP + 4)
                            S["ddv"].inc(v.reciprocal(gb[:, 12:13], gb[:, 11:12]))
                            v.wait_ge(S["dac"].sem, 3 * i + 3)
                            v.wait_ge(S["ddv"].sem, DDV_GATE0 + i * NGOP + 5)
                            S["ddv"].inc(v.tensor_scalar(gb[:, 0:8], gb[:, 0:8],
                                                         gb[:, 12:13], None, ALU.mult))
                            v.wait_ge(S["ddv"].sem, DDV_GATE0 + i * NGOP + 6)
                            S["ddv"].inc(v.tensor_scalar(gb[:, 0:8], gb[:, 0:8],
                                                         gb[:, 13:14], 0.0,
                                                         ALU.subtract, ALU.max))
                            v.wait_ge(S["ddv"].sem, DDV_GATE0 + i * NGOP + 7)
                            S["ddv"].inc(v.reduce_sum(gb[:, 14:15], gb[:, 0:8], axis=AX.X))
                            v.wait_ge(S["ddv"].sem, DDV_GATE0 + i * NGOP + 8)
                            S["ddv"].inc(v.tensor_scalar(gb[:, 15:16], gb[:, 14:15], 1e-30,
                                                         None, ALU.max))
                            v.wait_ge(S["ddv"].sem, DDV_GATE0 + i * NGOP + 9)
                            S["ddv"].inc(v.reciprocal(gb[:, 15:16], gb[:, 15:16]))
                            v.wait_ge(S["ddv"].sem, DDV_GATE0 + i * NGOP + 10)
                            if i >= 2 and not DCUT5:
                                v.wait_ge(S["dpe"].sem, TRDONE(i - 2))
                            S["ddv"].inc(v.tensor_scalar(wnb[i % 2][:], gb[:, 0:8],
                                                         gb[:, 15:16], None, ALU.mult))
                            if not DCUT5 and i >= 1:
                                v.wait_ge(S["dpe"].sem, TRDONE(i - 1))
                                S["dtrF"].inc(v.tensor_copy(
                                    wT[:, (i - 1) * 128:i * 128],
                                    psB16[0:8, 512 + ((i - 1) % 2) * 128:512 + ((i - 1) % 2) * 128 + 128]))
                        if not DCUT5:
                            v.wait_ge(S["dpe"].sem, TRDONE(NLT - 1))
                            S["dtrF"].inc(v.tensor_copy(
                                wT[:, (NLT - 1) * 128:NLT * 128],
                                psB16[0:8, 512 + ((NLT - 1) % 2) * 128:512 + ((NLT - 1) % 2) * 128 + 128]))
                        for tb in range(0 if DCUT4 else NTB):
                            v.wait_ge(S["dpe"].sem, DPE_WEBC0 + tb + 1)
                            S["ddv"].inc(v.tensor_scalar(webc[:, tb * 512:(tb + 1) * 512],
                                                         bank(tb % 2), 2.0, None, ALU.mult))
                        for tb in range(0 if DCUT4 else NTB):
                            v.wait_ge(S["dpe"].sem, DPE_H0 + tb + 1)
                            S["ddv"].inc(v.tensor_copy(hbuf[:, tb * 512:(tb + 1) * 512],
                                                       bank(2 + tb % 2)))
                        if not DCUT4:
                            v.wait_ge(S["ddv"].sem, DDV_H0 + NTB)
                            S["ddv"].inc(v.tensor_tensor(hbuf[:], hbuf[:], webc[:], ALU.mult))
                        for og in range(0 if DCUT3 else 16):
                            for tb in range(NTB):
                                gg = og * NTB + tb
                                rr = gg % 2
                                v.wait_ge(S["dpe"].sem, DPE_OG0 + gg + 1)
                                for j in range(2):
                                    sidx = gg * 2 + j
                                    if sidx >= 4:
                                        ffsR.wait_done(v, sidx - 4,
                                                       uses=(sidx - 4) // 2 + 1)
                                    S["ddv"].inc(v.tensor_scalar(
                                        fstg[sidx % 4][:], bank(2 + 2 * rr + j),
                                        sb_bbase[:, 2 * og + j:2 * og + j + 1], None,
                                        ALU.add))

                    @blk.scalar
                    def _(sc):
                        dguards(sc)
                        for i in range(0 if (DCUT2 or DCUT6) else NLT):
                            gb = gbuf[:, (i % 2) * 16:(i % 2) * 16 + 16]
                            sc.wait_ge(S["ddv"].sem, DDV_GATE0 + i * NGOP + 3)
                            S["dac"].inc(sc.activation(gb[:, 0:8], gb[:, 0:8], AF.Exp,
                                                       bias=gb[:, 10:11]))
                            S["dac"].inc(sc.activation(gb[:, 13:14], gb[:, 8:9], AF.Sigmoid,
                                                       bias=sb_bthr[:, 0:1]))
                            sc.wait_ge(S["dac"].sem, 3 * i + 2)
                            S["dac"].inc(sc.activation(gb[:, 13:14], gb[:, 13:14],
                                                       AF.Identity, scale=0.125))

            if os.environ.get("MK_DCUT") in ("1", "2", "3", "4", "5", "6"):
                with nc.Block("fincut") as blk:
                    @blk.gpsimd
                    def _(gp):
                        if os.environ.get("MK_DCUT") == "2":
                            gp.wait_ge(S["ddv"].sem, DDV_WGT0 + KD)
                            gp.wait_ge(S["dpe"].sem, 96)
                        elif os.environ.get("MK_DCUT") == "3":
                            gp.wait_ge(S["ddv"].sem, DDV_HW)
                            gp.wait_ge(S["dpe"].sem, DPE_OG0)
                        elif os.environ.get("MK_DCUT") == "4":
                            gp.wait_ge(S["ddv"].sem, DDV_WT0 + NLT)
                            gp.wait_ge(S["dpe"].sem, DPE_WEBC0)
                        elif os.environ.get("MK_DCUT") == "5":
                            gp.wait_ge(S["ddv"].sem, DDV_GATE0 + NLT * NGOP)
                            gp.wait_ge(S["dpe"].sem, DPE_WTR0)
                        else:
                            ffsR.wait_all(gp, 32 * NTB)
                            gp.wait_ge(S["ddv"].sem, DDV_EV0 + 2 * 32 * NTB)
                        S["dout"].inc(gp.dma_start(OUT, X), 16)
                        gp.wait_ge(S["dout"].sem, 16)
                return nc
            with ExitStack() as sD3:
                ED = sD3.enter_context
                ftb = [ED(nc.sbuf_tensor(f"d3_f{i}", [128, D], BF16)) for i in range(2)]
                x2t = [ED(nc.sbuf_tensor(f"d3_x{i}", [128, D], F32)) for i in range(2)]
                ostg = [ED(nc.sbuf_tensor(f"d3_o{i}", [128, D], F32)) for i in range(2)]
                ftR = RingSem([S["ft2_r0"], S["ft2_r1"]])
                xfR = RingSem([S["xf_r0"], S["xf_r1"]])
                ouR = RingSem([S["ou_r0"], S["ou_r1"]])
                FFV = FFN_SP.rearrange("s p l -> (s p) l")

                with nc.Block("d3") as blk:
                    @blk.sync
                    def _(sy):
                        ffsR.wait_all(sy, 32 * NTB)
                        for i in range(NLT):
                            if i >= 2:
                                sy.wait_ge(S["ddv"].sem, DDV_EV0 + 32 * NTB + i - 1)
                            ftR.inc(sy.dma_start_transpose(
                                ftb[i % 2][:], FFV[:, i * 128:(i + 1) * 128]), i)

                    @blk.gpsimd
                    def _(gp):
                        for i in range(NLT):
                            if i >= 2:
                                gp.wait_ge(S["ddv"].sem, DDV_EV0 + 32 * NTB + i - 1)
                            xfR.inc(gp.dma_start(x2t[i % 2][:],
                                                 X2_SP[i * 128:(i + 1) * 128, :]), i)
                            gp.wait_ge(S["ddv"].sem, DDV_EV0 + 32 * NTB + i + 1)
                            ouR.inc(gp.dma_start(OUT[i * 128:(i + 1) * 128, :],
                                                 ostg[i % 2][:]), i)
                        ouR.wait_all(gp, NLT)

                    @blk.vector
                    def _(v):
                        for i in range(NLT):
                            ftR.wait_done(v, i)
                            xfR.wait_done(v, i)
                            if i >= 2:
                                ouR.wait_done(v, i - 2)
                            S["ddv"].inc(v.tensor_tensor(ostg[i % 2][:], x2t[i % 2][:],
                                                         ftb[i % 2][:], ALU.add))

        fin_guards = a1_guards if PH == 0 else b_guards
        if PH < 3:
            fin_guards = a1_guards if PH == 0 else b_guards
            with nc.Block("fin") as blk:
                @blk.gpsimd
                def _(gp):
                    for cnt, val in fin_guards:
                        gp.wait_ge(cnt.sem, val)
                    S["dout"].inc(gp.dma_start(OUT, X), 16)
                    gp.wait_ge(S["dout"].sem, 16)
        return nc

    return nc


def _consts(L):
    tril = np.zeros((128, 512), np.float32)
    for p in range(128):
        tril[p, p:256] = 1.0
        if 128 + p < 256:
            pass
        tril[p, 256 + 128 + p:512] = 1.0
    ebc = np.zeros((8, 128), np.float32)
    for e in range(8):
        ebc[e, e * 16:(e + 1) * 16] = 1.0
    return dict(
        tril=tril.astype(BF),
        i128f=np.eye(128, dtype=np.float32),
        i128b=np.eye(128, dtype=np.float32).astype(BF),
        ebc8=ebc.astype(BF),
        onesb=np.ones((1, 128), BF),
        onesf=np.ones((1, 128), np.float32),
    )


def make_in_maps(inputs, L=2048, ncores=8):
    w = {
        "win": np.ascontiguousarray(inputs["L0_attn_in_proj_w"], np.float32),
        "convw": np.ascontiguousarray(inputs["L0_attn_conv_w"], np.float32),
        "convb": np.ascontiguousarray(inputs["L0_attn_conv_b"], np.float32),
        "dtbias": np.ascontiguousarray(inputs["L0_attn_dt_bias"], np.float32),
        "alog": np.ascontiguousarray(inputs["L0_attn_A_log"], np.float32),
        "dhead": np.ascontiguousarray(inputs["L0_attn_D"], np.float32).reshape(1, H),
        "nrmw": np.ascontiguousarray(inputs["L0_attn_norm_w"], np.float32),
        "wout": np.ascontiguousarray(inputs["L0_attn_out_proj_w"], np.float32),
        "wbase": np.ascontiguousarray(inputs["L0_ffn_W_base"], np.float32),
        "bbase": np.ascontiguousarray(inputs["L0_ffn_b_base"], np.float32),
        "wgate9": np.ascontiguousarray(
            np.concatenate([np.asarray(inputs["L0_ffn_W_gate"], np.float32),
                            np.asarray(inputs["L0_ffn_W_thr"], np.float32)], axis=0)),
        "bthr": np.ascontiguousarray(inputs["L0_ffn_b_thr"], np.float32).reshape(1, 1),
        "aexp": np.ascontiguousarray(inputs["L0_ffn_A"], np.float32).reshape(128, D),
        "bexp": np.ascontiguousarray(inputs["L0_ffn_B"], np.float32).reshape(8 * D, 16),
    }
    w.update(_consts(L))
    xs = np.asarray(inputs["L0_attn_hidden_states"], np.float32)
    maps = []
    for b in range(ncores):
        m = dict(w)
        m["x"] = np.ascontiguousarray(xs[b, :L, :])
        maps.append(m)
    return maps


def kernel(**inputs):
    L = 2048
    nc = build_program(L=L)
    maps = make_in_maps(inputs, L=L)
    res = run_bass_kernel_spmd(nc, maps, core_ids=list(range(8)))
    out = np.stack([np.asarray(r["out"], np.float32) for r in res.results], axis=0)
    return out


# version 06 (final)
# speedup vs baseline: 1.0829x; 1.0682x over previous
import os
import numpy as np
import ml_dtypes
from contextlib import ExitStack

import concourse.bass as bass
import concourse.mybir as mybir
from concourse.bass_utils import run_bass_kernel_spmd

F32 = mybir.dt.float32
BF16 = mybir.dt.bfloat16
AF = mybir.ActivationFunctionType
ALU = mybir.AluOpType
AX = mybir.AxisListType
BF = ml_dtypes.bfloat16

D, DIN, H, PDIM, NG, NSTATE, CK, CH = 4096, 8192, 128, 64, 8, 128, 4, 256
DPROJ = 2 * DIN + 2 * NG * NSTATE + H
CONVD = DIN + 2 * NG * NSTATE
NSTRIP = DPROJ // 128
DTSTRIP = 144
EPS = 1e-5
ECLAMP = 50.0
KD = D // 128
KDIN = DIN // 128


class Cnt:
    def __init__(self, sem):
        self.sem = sem
        self.n = 0

    def inc(self, inst, k=1):
        inst.then_inc(self.sem, k)
        self.n += k
        return self.n


class RingSem:
    def __init__(self, sems, per=16):
        self.sems = sems
        self.n = len(sems)
        self.per = per
        self.base = [c.n for c in sems]

    def inc(self, inst, idx):
        sl = idx % self.n
        inst.then_inc(self.sems[sl].sem, self.per)
        self.sems[sl].n += self.per

    def wait_done(self, eng, idx, uses=None):
        sl = idx % self.n
        u = (idx // self.n + 1) if uses is None else uses
        eng.wait_ge(self.sems[sl].sem, self.base[sl] + self.per * u)

    def wait_all(self, eng, total_items):
        for sl in range(self.n):
            cnt = (total_items - sl + self.n - 1) // self.n
            if cnt > 0:
                eng.wait_ge(self.sems[sl].sem, self.base[sl] + self.per * cnt)


def build_program(L=2048, debug=(), last_phase="D"):
    DSKIP = set(os.environ.get("MK_DSKIP", "").split(","))
    NCH = L // CH
    NTB = L // 512
    NLT = L // 128
    PH = {"A": 0, "B": 1, "C": 2, "D": 3}[last_phase]

    a1_groups = [(64 + 2 * i, 65 + 2 * i) for i in range(40)] + \
                [(2 * i, 2 * i + 1) for i in range(32)]
    NGRP = len(a1_groups)
    BUILD_ORDER = [3, 4, 5, 6, 7, 0, 1, 2]
    SLAB = 2304
    CAST_ROWS = [(DTSTRIP * 128, DPROJ)] + \
                [(r * SLAB, (r + 1) * SLAB) for r in BUILD_ORDER]
    a1_req = []
    for g_ in range(NGRP):
        c0_ = a1_groups[g_][0] * 128
        a1_req.append(max(BUILD_ORDER.index(s_)
                          for s_ in range(c0_ // SLAB, (c0_ + 255) // SLAB + 1)))
    strip_order = []
    for g in a1_groups:
        strip_order.extend(g)
    spill_idx = {s: i for i, s in enumerate(strip_order)}

    nc = bass.Bass("TRN2", target_bir_lowering=False)
    ap = lambda t: t.ap()

    X = ap(nc.dram_tensor("x", [L, D], F32, kind="ExternalInput"))
    WIN = ap(nc.dram_tensor("win", [DPROJ, D], F32, kind="ExternalInput"))
    CONVW = ap(nc.dram_tensor("convw", [CONVD, CK], F32, kind="ExternalInput"))
    CONVB = ap(nc.dram_tensor("convb", [CONVD], F32, kind="ExternalInput"))
    DTB = ap(nc.dram_tensor("dtbias", [H], F32, kind="ExternalInput"))
    ALOG = ap(nc.dram_tensor("alog", [H], F32, kind="ExternalInput"))
    DHEAD = ap(nc.dram_tensor("dhead", [1, H], F32, kind="ExternalInput"))
    NRMW = ap(nc.dram_tensor("nrmw", [DIN], F32, kind="ExternalInput"))
    WOUT = ap(nc.dram_tensor("wout", [D, DIN], F32, kind="ExternalInput"))
    WBASE = ap(nc.dram_tensor("wbase", [D, D], F32, kind="ExternalInput"))
    BBASE = ap(nc.dram_tensor("bbase", [D], F32, kind="ExternalInput"))
    WGATE9 = ap(nc.dram_tensor("wgate9", [9, D], F32, kind="ExternalInput"))
    BTHR = ap(nc.dram_tensor("bthr", [1, 1], F32, kind="ExternalInput"))
    AEXP = ap(nc.dram_tensor("aexp", [128, D], F32, kind="ExternalInput"))
    BEXP = ap(nc.dram_tensor("bexp", [8 * D, 16], F32, kind="ExternalInput"))
    TRIL = ap(nc.dram_tensor("tril", [128, 512], BF16, kind="ExternalInput"))
    I128F = ap(nc.dram_tensor("i128f", [128, 128], F32, kind="ExternalInput"))
    I128B = ap(nc.dram_tensor("i128b", [128, 128], BF16, kind="ExternalInput"))
    EBC8 = ap(nc.dram_tensor("ebc8", [8, 128], BF16, kind="ExternalInput"))
    ONESB = ap(nc.dram_tensor("onesb", [1, 128], BF16, kind="ExternalInput"))
    ONESF = ap(nc.dram_tensor("onesf", [1, 128], F32, kind="ExternalInput"))

    OUT = ap(nc.dram_tensor("out", [L, D], F32, kind="ExternalOutput"))

    def internal(name, shape, dt):
        kind = "ExternalOutput" if name in debug else "Internal"
        return ap(nc.dram_tensor(name, shape, dt, kind=kind))

    WIN_BF = internal("win_bf", [DPROJ, D], BF16)
    WINT = internal("wint", [D, DPROJ], BF16)
    WOUTT = internal("woutt", [DIN, D], BF16)
    WBASET = internal("wbaset", [D, D], BF16)
    WOUT_BF = internal("wout_bf", [D, DIN], BF16)
    WBASE_BF = internal("wbase_bf", [D, D], BF16)
    A_BF = internal("a_bf", [128, D], BF16)
    B_BF = internal("b_bf", [8 * D, 16], BF16)
    WG_BF = internal("wg_bf", [9, D], BF16)
    XLNT = internal("xlnt", [L, D], BF16)
    ZX = internal("zx", [NSTRIP, 128, L], BF16)
    CS_SP = internal("cs_sp", [128, L], F32)
    QN_SP = internal("qn_sp", [128, L], F32)
    OD_SP = internal("od_sp", [128, L], BF16)
    EA_SP = internal("ea_sp", [NCH, 128], F32)
    Y_SP = internal("y_sp", [64, 128, L], BF16)
    ATTN_SP = internal("attn_sp", [32, 128, L], BF16)
    X2_SP = internal("x2_sp", [L, D], F32)
    X2LNT = internal("x2lnt", [L, D], BF16)
    FFN_SP = internal("ffn_sp", [32, 128, L], BF16)

    big = ExitStack()
    with big:
        E = big.enter_context
        S = {}
        for sn in ["bc_r0", "bc_r1", "xt_r0", "xt_r1", "xt_r2",
                   "bt_r0", "bt_r1", "bt_r2", "bt_r3",
                   "csf_r0", "csf_r1", "odf_r0", "odf_r1",
                   "eaf_r0", "eaf_r1", "ysp_r0", "ysp_r1", "bdd", "bpea",
                   "cast_r0", "cast_r1", "a0ld_r0", "a0ld_r1",
                   "a0st_r0", "a0st_r1", "wt_r0", "wt_r1",
                   "a1sp_r0", "a1sp_r1", "a1sp_r2", "a1sp_r3",
                   "p0", "p0t", "p0v", "cast2",
                   "a0ac", "a0v", "xlf",
                   "abWt", "abpe", "abev", "abac", "abdv", "abps", "absp",
                   "a1pe", "a1ev", "a1cv", "a1ac",
                   "bld",
                   "bpeB", "bpeG", "bpeS", "bpeY",
                   "bgev", "bdvE", "bacX", "bodv", "bwof", "bmtt", "bsr",
                   "bsbf", "bev", "bear",
                   "cpe", "cdv", "cac", "cevs",
                   "cx2v", "cx2a",
                   "dpe", "ddv", "dac", "dtrX", "dtrF", "dout"]:
            S[sn] = Cnt(E(nc.semaphore("s_" + sn)))
        for sn in ["zl_r0", "zl_r1", "xl_r0", "xl_r1",
                   "yl_r0", "as_r0", "as_r1",
                   "xt2_r0", "xt2_r1", "x2_r0", "x2_r1",
                   "xl2_r0", "xl2_r1", "ffs_r0", "ffs_r1",
                   "xf_r0", "xf_r1", "ou_r0", "ou_r1", "dbl", "ft2_r0", "ft2_r1", "wb2_r0", "wb2_r1"]:
            S[sn] = Cnt(E(nc.semaphore("s_" + sn)))
        for alias, orig in [("cw_r0", "wt_r0"), ("cw_r1", "wt_r1"),
                            ("at_r0", "xt_r2"), ("at_r1", "bt_r0"),
                            ("ft_r0", "odf_r1"), ("ft_r1", "abWt"),
                            ("wb_r0", "abps"), ("wb_r1", "abpe")]:
            S[alias] = S[orig]

        sb_ones_f = E(nc.sbuf_tensor("c_ones_f", [33, 128], F32))
        sb_ones_b = E(nc.sbuf_tensor("c_ones_b", [33, 128], BF16))
        sb_tril = E(nc.sbuf_tensor("c_tril", [128, 512], BF16))
        sb_i128f = E(nc.sbuf_tensor("c_i128f", [128, 128], F32))
        sb_i128b = E(nc.sbuf_tensor("c_i128b", [128, 128], BF16))
        sb_bthr = E(nc.sbuf_tensor("c_bthr", [128, 1], F32))
        sb_convw = E(nc.sbuf_tensor("c_convw", [128, 80 * CK], F32))
        sb_convb = E(nc.sbuf_tensor("c_convb", [128, 80], F32))
        sb_dtb = E(nc.sbuf_tensor("c_dtb", [128, 1], F32))
        sb_onecol = E(nc.sbuf_tensor("c_onecol", [128, 1], F32))
        sb_epscol = E(nc.sbuf_tensor("c_epscol", [128, 1], F32))
        sb_alog = E(nc.sbuf_tensor("c_alog", [128, 1], F32))
        sb_ealog = E(nc.sbuf_tensor("c_ealog", [128, 1], F32))
        sb_dexp = E(nc.sbuf_tensor("c_dexp", [128, 64], F32))
        sb_dcols = E(nc.sbuf_tensor("c_dcols", [128, 128], F32))
        sb_nrmw = E(nc.sbuf_tensor("c_nrmw", [128, 64], F32))
        sb_bbase = E(nc.sbuf_tensor("c_bbase", [128, 32], F32))
        sQ = ExitStack()
        sb_qT = sQ.enter_context(nc.sbuf_tensor("p_qT", [128, L], F32))
        sb_ddtT = sQ.enter_context(nc.sbuf_tensor("p_ddtT", [128, L], F32))

        psF = E(nc.psum_tensor([128, 3584], F32))
        psB16 = E(nc.psum_tensor([128, 1024], BF16))

        def bank(i, o=0, w=512, p0=0, p1=128):
            return psF[p0:p1, i * 512 + o:i * 512 + o + w]

        castR = RingSem([S["cast_r0"], S["cast_r1"]])
        a0ldR = RingSem([S["a0ld_r0"], S["a0ld_r1"]])
        a0stR = RingSem([S["a0st_r0"], S["a0st_r1"]])
        wtR = RingSem([S["wt_r0"], S["wt_r1"]])
        a1spR = RingSem([S["a1sp_r0"], S["a1sp_r1"], S["a1sp_r2"], S["a1sp_r3"]])

        with nc.Block("p0") as blk:
            @blk.scalar
            def _(sc):
                def slow(dst, src):
                    return sc.dma_start(dst, src, allow_slow_non_contiguous=True)
                for dst, src in [(sb_ones_f[0:1, :], ONESF), (sb_ones_b[0:1, :], ONESB),
                                 (sb_ones_f[32:33, :], ONESF), (sb_ones_b[32:33, :], ONESB),
                                 (sb_tril[:], TRIL), (sb_i128f[:], I128F),
                                 (sb_i128b[:], I128B),
                                 (sb_dtb[:], DTB.rearrange("(p o) -> p o", o=1)),
                                 (sb_alog[:], ALOG.rearrange("(p o) -> p o", o=1)),
                                 (sb_nrmw[:], NRMW.rearrange("(s p) -> p s", p=128)),
                                 (sb_bbase[:], BBASE.rearrange("(s p) -> p s", p=128)),
                                 (sb_dcols[0:1, :], DHEAD)]:
                    S["p0"].inc(slow(dst, src), 16)
                cw = CONVW.rearrange("(s p) k -> p s k", p=128)
                cwd = sb_convw[:].rearrange("p (s k) -> p s k", k=CK)
                for i in range(4):
                    S["p0"].inc(sc.dma_start(cwd[:, i * 20:(i + 1) * 20, :],
                                             cw[:, i * 20:(i + 1) * 20, :]), 16)
                S["p0"].inc(sc.dma_start(sb_bthr[:], BTHR.broadcast_to([128, 1])
                                         if hasattr(BTHR, "broadcast_to") else BTHR), 16)
                cb = CONVB.rearrange("(s p) -> p s", p=128)
                for i in range(2):
                    S["p0"].inc(slow(sb_convb[:, i * 40:(i + 1) * 40],
                                     cb[:, i * 40:(i + 1) * 40]), 16)
                sc.wait_ge(S["p0"].sem, S["p0"].n)
                S["p0"].inc(sc.activation(sb_ealog[:], sb_alog[:], AF.Exp))

            @blk.gpsimd
            def _(gp):
                for c_ in range(3):
                    r0_, r1_ = CAST_ROWS[c_]
                    castR.inc(gp.dma_start(WIN_BF[r0_:r1_, :], WIN[r0_:r1_, :]), c_)

            @blk.tensor
            def _(t):
                t.wait_ge(S["p0"].sem, 16 * 19 + 1)
                S["p0t"].inc(t.matmul(bank(6, 0, 128), sb_ones_f[0:1, 0:128],
                                      sb_dcols[0:1, :], start=True, stop=True))

            @blk.vector
            def _(v):
                v.memset(sb_onecol[:], 1.0)
                v.memset(sb_epscol[:], EPS)
                v.wait_ge(S["p0t"].sem, 1)
                S["p0v"].inc(v.tensor_copy(sb_dcols[:], bank(6, 0, 128)))
                v.wait_ge(S["p0v"].sem, 1)
                dc = sb_dcols[:].rearrange("p (g two) -> p g two", two=2)
                S["p0v"].inc(v.tensor_copy(sb_dexp[0:64, :], dc[0:64, :, 0]))
                S["p0v"].inc(v.tensor_copy(sb_dexp[64:128, :], dc[64:128, :, 1]))

        wt2R = RingSem([S["p0t"], S["p0v"]])

        def mk_build_emitter(wintb_):
            nb = len(wintb_)

            def emit(sy, p):
                pos, k = p // KD, p % KD
                r = BUILD_ORDER[pos]
                if k == 0:
                    castR.wait_done(sy, 1 + pos)
                if p >= nb:
                    wt2R.wait_done(sy, p - nb)
                S["dout"].inc(sy.dma_start_transpose(
                    wintb_[p % nb][:],
                    WIN_BF[r * SLAB:(r + 1) * SLAB, k * 128:(k + 1) * 128]), 16)
                sy.wait_ge(S["dout"].sem, 16 * (p + 1))
                wt2R.inc(sy.dma_start(
                    WINT[k * 128:(k + 1) * 128, r * SLAB:(r + 1) * SLAB],
                    wintb_[p % nb][:]), p)
            return emit

        saX = ExitStack()
        sb_xlnf = saX.enter_context(nc.sbuf_tensor("a_xlnf", [128, KD * L], BF16))

        with ExitStack() as sa0:
            EA = sa0.enter_context
            xt = [EA(nc.sbuf_tensor(f"a0_xt{i}", [128, D], F32)) for i in range(2)]
            stat = EA(nc.sbuf_tensor("a0_stat", [128, 2 * NLT], F32))
            rstd = EA(nc.sbuf_tensor("a0_rstd", [128, NLT], F32))
            msh = EA(nc.sbuf_tensor("a0_msh", [128, NLT], F32))
            mb = EA(nc.sbuf_tensor("a0_m", [128, NLT], F32))
            vb = EA(nc.sbuf_tensor("a0_v", [128, NLT], F32))
            xlnb = EA(nc.sbuf_tensor("a0_xln", [128, D], BF16))

            with nc.Block("a0") as blk:
                @blk.sync
                def _(sy):
                    for i in range(NLT):
                        if i >= 2:
                            sy.wait_ge(S["a0v"].sem, 2 * (i - 1))
                        a0ldR.inc(sy.dma_start(xt[i % 2][:], X[i * 128:(i + 1) * 128, :]), i)
                    for i in range(NLT):
                        sy.wait_ge(S["a0v"].sem, 2 * NLT + 5)
                        if i >= 2:
                            sy.wait_ge(S["a0ac"].sem, NLT + 2 + (i - 1))
                        a0ldR.inc(sy.dma_start(xt[i % 2][:], X[i * 128:(i + 1) * 128, :]), NLT + i)
                        if i >= 1:
                            sy.wait_ge(S["a0ac"].sem, NLT + 2 + i)
                            a0stR.inc(sy.dma_start(XLNT[(i - 1) * 128:i * 128, :],
                                                   xlnb[:]), i - 1)
                    sy.wait_ge(S["a0ac"].sem, NLT + 2 + NLT)
                    a0stR.inc(sy.dma_start(XLNT[(NLT - 1) * 128:NLT * 128, :],
                                           xlnb[:]), NLT - 1)
                    a0stR.wait_all(sy, NLT)
                    for k in range(KD):
                        S["xlf"].inc(sy.dma_start_transpose(
                            sb_xlnf[:, k * L:(k + 1) * L], XLNT[:, k * 128:(k + 1) * 128]), 16)

                @blk.gpsimd
                def _(gp):
                    for c_ in range(3, 6):
                        r0_, r1_ = CAST_ROWS[c_]
                        castR.inc(gp.dma_start(WIN_BF[r0_:r1_, :], WIN[r0_:r1_, :]), c_)

                @blk.scalar
                def _(sc):
                    for i in range(NLT):
                        a0ldR.wait_done(sc, i)
                        if i >= 1:
                            sc.wait_ge(S["a0v"].sem, 2 * i)
                        S["a0ac"].inc(sc.activation(xlnb[:], xt[i % 2][:], AF.Square))
                    sc.wait_ge(S["a0v"].sem, 2 * NLT + 3)
                    S["a0ac"].inc(sc.activation(rstd[:], vb[:], AF.Ln, bias=sb_epscol[:, 0:1]))
                    sc.wait_ge(S["a0ac"].sem, NLT + 1)
                    S["a0ac"].inc(sc.activation(rstd[:], rstd[:], AF.Exp, scale=-0.5))
                    for i in range(NLT):
                        a0ldR.wait_done(sc, NLT + i)
                        sc.wait_ge(S["a0v"].sem, 2 * NLT + 5)
                        if i >= 1:
                            a0stR.wait_done(sc, i - 1)
                        S["a0ac"].inc(sc.activation(xlnb[:], xt[i % 2][:], AF.Identity,
                                                    bias=msh[:, i:i + 1], scale=rstd[:, i:i + 1]))

                @blk.vector
                def _(v):
                    for i in range(NLT):
                        v.wait_ge(S["a0ac"].sem, i + 1)
                        S["a0v"].inc(v.reduce_sum(stat[:, i:i + 1], xt[i % 2][:], axis=AX.X))
                        S["a0v"].inc(v.reduce_sum(stat[:, NLT + i:NLT + i + 1],
                                                  xlnb[:], axis=AX.X))
                    v.wait_ge(S["a0v"].sem, 2 * NLT)
                    S["a0v"].inc(v.tensor_scalar(mb[:], stat[:, 0:NLT], 1.0 / D, None, ALU.mult))
                    v.wait_ge(S["a0v"].sem, 2 * NLT + 1)
                    S["a0v"].inc(v.tensor_tensor(vb[:], mb[:], mb[:], ALU.mult))
                    v.wait_ge(S["a0v"].sem, 2 * NLT + 2)
                    S["a0v"].inc(v.scalar_tensor_tensor(vb[:], stat[:, NLT:2 * NLT], 1.0 / D,
                                                        vb[:], ALU.mult, ALU.subtract))
                    v.wait_ge(S["a0ac"].sem, NLT + 2)
                    S["a0v"].inc(v.tensor_tensor(msh[:], mb[:], rstd[:], ALU.mult))
                    v.wait_ge(S["a0v"].sem, 2 * NLT + 4)
                    S["a0v"].inc(v.tensor_scalar(msh[:], msh[:], -1.0, None, ALU.mult))

        with ExitStack() as sab:
            EA = sab.enter_context
            wt144 = EA(nc.sbuf_tensor("ab_wt", [128, KD * 128], BF16))
            dtraw = EA(nc.sbuf_tensor("ab_dtraw", [128, L], F32))
            scr1 = EA(nc.sbuf_tensor("ab_s1", [128, L], F32))
            scr2 = EA(nc.sbuf_tensor("ab_s2", [128, L], F32))
            scr3 = EA(nc.sbuf_tensor("ab_s3", [128, L], F32))
            odb = EA(nc.sbuf_tensor("ab_od", [128, L], BF16))
            z256 = EA(nc.sbuf_tensor("ab_z", [128, CH], F32))
            eav = EA(nc.sbuf_tensor("ab_ea", [128, NCH], F32))
            eat = EA(nc.sbuf_tensor("ab_eat", [NCH, 128], F32))
            wintb_ab = [EA(nc.sbuf_tensor(f"ab_wi{i}", [128, SLAB], BF16)) for i in range(2)]

            NT128 = L // 128
            AB_EVAC = NTB
            AB_RELU = AB_EVAC + 1
            AB_DT = AB_RELU + 1
            AB_DA = AB_DT + 1
            AB_MZ = AB_DA + 1
            AB_CS = AB_MZ + NCH
            AB_DDT = AB_CS + 1
            AB_Q = AB_DDT + 1
            AB_TEV = AB_Q + 2 * NT128
            AB_EAT = AB_TEV + 1
            AC_SP = 4
            AC_DDX = AC_SP + NCH
            AC_OD = AC_DDX + 1
            AC_EA = AC_OD + 1
            AC_LNDT = AC_EA + 1

            HDP = DPROJ // 2
            with nc.Block("a0b") as blk:
                @blk.sync
                def _(sy):
                    castR.wait_done(sy, 0)
                    r0 = DTSTRIP * 128
                    for k in range(KD):
                        S["abWt"].inc(sy.dma_start_transpose(
                            wt144[:, k * 128:(k + 1) * 128],
                            WIN_BF[r0:r0 + 128, k * 128:(k + 1) * 128]), 16)
                    emit_ab = mk_build_emitter(wintb_ab)
                    for p_ in range(KD):
                        emit_ab(sy, p_)


                @blk.tensor
                def _(t):
                    t.wait_ge(S["abWt"].sem, 16 * KD)
                    t.wait_ge(S["xlf"].sem, 16 * KD)
                    for tb in range(NTB):
                        for k in range(KD):
                            mm = t.matmul(bank(tb), wt144[:, k * 128:(k + 1) * 128],
                                          sb_xlnf[:, k * L + tb * 512:k * L + (tb + 1) * 512],
                                          start=(k == 0), stop=(k == KD - 1))
                        S["abpe"].inc(mm)
                    t.wait_ge(S["abdv"].sem, AB_DDT)
                    for i in range(NT128):
                        if i >= 2:
                            t.wait_ge(S["abev"].sem, i - 1)
                        S["abps"].inc(t.transpose(bank(4 + (i % 2), 0, 128),
                                                  scr2[:, i * 128:(i + 1) * 128], sb_i128f[:]))
                    t.wait_ge(S["abdv"].sem, AB_Q)
                    for i in range(NT128):
                        t.wait_ge(S["abev"].sem, NT128 + i - 1)
                        S["abps"].inc(t.transpose(bank(4 + (i % 2), 0, 128),
                                                  dtraw[:, i * 128:(i + 1) * 128], sb_i128f[:]))
                    t.wait_ge(S["abac"].sem, AC_EA)
                    t.wait_ge(S["abev"].sem, 2 * NT128)
                    S["abps"].inc(t.transpose(bank(4, 0, 128, 0, NCH), eav[:], sb_i128f[:]))

                @blk.scalar
                def _(sc):
                    sc.wait_ge(S["abdv"].sem, AB_EVAC)
                    S["abac"].inc(sc.activation(scr1[:], dtraw[:], AF.Identity, bias=sb_dtb[:, 0:1]))
                    sc.wait_ge(S["abac"].sem, 1)
                    S["abac"].inc(sc.activation(scr2[:], scr1[:], AF.Abs))
                    sc.wait_ge(S["abac"].sem, 2)
                    S["abac"].inc(sc.activation(scr3[:], scr2[:], AF.Exp, scale=-1.0))
                    sc.wait_ge(S["abac"].sem, 3)
                    S["abac"].inc(sc.activation(scr2[:], scr3[:], AF.Ln, bias=sb_onecol[:, 0:1]))
                    sc.wait_ge(S["abdv"].sem, AB_CS)
                    for c in range(NCH):
                        S["abac"].inc(sc.activation(scr2[:, c * CH:(c + 1) * CH],
                                                    scr3[:, c * CH:(c + 1) * CH],
                                                    AF.Exp, scale=-1.0,
                                                    bias=scr3[:, c * CH + CH - 1:c * CH + CH]))
                    S["abac"].inc(sc.activation(odb[:], scr3[:], AF.Exp))
                    S["abac"].inc(sc.activation(eav[:], scr3[:, CH - 1::CH], AF.Exp))
                    sc.wait_ge(S["abdv"].sem, AB_DDT)
                    S["abac"].inc(sc.activation(scr1[:], scr1[:], AF.Ln))
                    sc.wait_ge(S["abdv"].sem, AB_CS)
                    S["absp"].inc(sc.dma_start(CS_SP[:, :], scr3[:]), 16)
                    S["absp"].inc(sc.dma_start(OD_SP[:, :], odb[:]), 16)
                    sc.wait_ge(S["abev"].sem, 2 * NT128 + 1)
                    S["absp"].inc(sc.dma_start(EA_SP[:, :], eat[:]), 16)
                    sc.wait_ge(S["absp"].sem, 48)

                @blk.vector
                def _(v):
                    for tb in range(NTB):
                        v.wait_ge(S["abpe"].sem, tb + 1)
                        S["abdv"].inc(v.tensor_copy(dtraw[:, tb * 512:(tb + 1) * 512], bank(tb)))
                    v.wait_ge(S["abac"].sem, AC_SP)
                    S["abdv"].inc(v.tensor_scalar(scr3[:], scr1[:], 0.0, None, ALU.max))
                    v.wait_ge(S["abdv"].sem, AB_RELU)
                    S["abdv"].inc(v.tensor_tensor(scr1[:], scr2[:], scr3[:], ALU.add))
                    v.wait_ge(S["abdv"].sem, AB_DT)
                    S["abdv"].inc(v.tensor_scalar(scr2[:], scr1[:], sb_ealog[:, 0:1], -1.0,
                                                  ALU.mult, ALU.mult))
                    S["abdv"].inc(v.memset(z256[:], 0.0))
                    v.wait_ge(S["abdv"].sem, AB_MZ)
                    for c in range(NCH):
                        S["abdv"].inc(v.tensor_tensor_scan(scr3[:, c * CH:(c + 1) * CH],
                                                           scr2[:, c * CH:(c + 1) * CH],
                                                           z256[:], 0.0, ALU.add, ALU.add))
                    v.wait_ge(S["abac"].sem, AC_DDX)
                    S["abdv"].inc(v.tensor_tensor(scr2[:], scr2[:], scr1[:], ALU.mult))
                    v.wait_ge(S["abac"].sem, AC_LNDT)
                    S["abdv"].inc(v.tensor_tensor(dtraw[:], scr3[:], scr1[:], ALU.subtract))
                    for i in range(NT128):
                        v.wait_ge(S["abps"].sem, i + 1)
                        S["abev"].inc(v.tensor_copy(sb_ddtT[:, i * 128:(i + 1) * 128],
                                                    bank(4 + (i % 2), 0, 128)))
                    for i in range(NT128):
                        v.wait_ge(S["abps"].sem, NT128 + i + 1)
                        S["abev"].inc(v.tensor_copy(sb_qT[:, i * 128:(i + 1) * 128],
                                                    bank(4 + (i % 2), 0, 128)))
                    v.wait_ge(S["abps"].sem, 2 * NT128 + 1)
                    S["abev"].inc(v.tensor_copy(eat[:], bank(4, 0, 128, 0, NCH)))

                @blk.gpsimd
                def _(gp):
                    for c_ in range(6, 9):
                        r0_, r1_ = CAST_ROWS[c_]
                        castR.inc(gp.dma_start(WIN_BF[r0_:r1_, :], WIN[r0_:r1_, :]), c_)

            ab_guards = [(S["absp"], 48), (S["abev"], 2 * NT128 + 1),
                         (S["abac"], AC_LNDT), (S["abdv"], AB_Q),
                         (S["abpe"], NTB), (S["abps"], 2 * NT128 + 1)]

        with ExitStack() as sa1:
            EA = sa1.enter_context
            wtbuf = EA(nc.sbuf_tensor("a1_wt", [128, 2 * KD * 256], BF16))
            rawring = [EA(nc.sbuf_tensor(f"a1_raw{i}", [128, L + 3], BF16)) for i in range(3)]
            t0 = EA(nc.sbuf_tensor("a1_t0", [128, L], BF16))
            t1 = EA(nc.sbuf_tensor("a1_t1", [128, L], BF16))
            wintb_a1 = [EA(nc.sbuf_tensor("a1_wi0", [128, SLAB], BF16))]

            def guards(eng):
                for cnt, val in ab_guards:
                    eng.wait_ge(cnt.sem, val)

            nconv_before = {}
            nc_ = 0
            for s_ in strip_order:
                nconv_before[s_] = nc_
                if s_ >= 64:
                    nc_ += 1
            NCONV = nc_

            with nc.Block("a1") as blk:
                @blk.sync
                def _(sy):
                    guards(sy)
                    emit_a1 = mk_build_emitter(wintb_a1)
                    bp = KD
                    for g in range(NGRP):
                        sa, sbb = a1_groups[g]
                        sl = g % 2
                        needp = (a1_req[g] + 1) * KD
                        target = max(needp, min(8 * KD, KD + (g + 1) * 4))
                        while bp < target:
                            emit_a1(sy, bp)
                            bp += 1
                        if g >= 2:
                            sy.wait_ge(S["a1pe"].sem, (g - 1) * NTB)
                        wt2R.wait_done(sy, needp - 1)
                        wt2R.wait_done(sy, needp - 2)
                        wtR.inc(sy.dma_start(
                            wtbuf[:, sl * KD * 256:(sl + 1) * KD * 256]
                            .rearrange("p (k f) -> p k f", f=256),
                            WINT[:, sa * 128:sa * 128 + 256].rearrange("(k p) f -> p k f", p=128)), g)

                @blk.tensor
                def _(t):
                    guards(t)
                    for g in range(NGRP):
                        sl = g % 2
                        wtR.wait_done(t, g)
                        for tb in range(NTB):
                            s = g * NTB + tb
                            rr = s % 3
                            if s >= 3:
                                t.wait_ge(S["a1ev"].sem, 2 * (s - 2))
                            for k in range(KD):
                                w0 = sl * KD * 256 + k * 256
                                mm = t.matmul(bank(2 * rr),
                                              wtbuf[:, w0:w0 + 128],
                                              sb_xlnf[:, k * L + tb * 512:k * L + (tb + 1) * 512],
                                              start=(k == 0), stop=(k == KD - 1))
                                mm = t.matmul(bank(2 * rr + 1),
                                              wtbuf[:, w0 + 128:w0 + 256],
                                              sb_xlnf[:, k * L + tb * 512:k * L + (tb + 1) * 512],
                                              start=(k == 0), stop=(k == KD - 1))
                            S["a1pe"].inc(mm)

                @blk.vector
                def _(v):
                    guards(v)
                    for rr_ in range(3):
                        v.memset(rawring[rr_][:, 0:3], 0.0)
                    ncv = 0
                    for g in range(NGRP):
                        for tb in range(NTB):
                            s = g * NTB + tb
                            rr = s % 3
                            v.wait_ge(S["a1pe"].sem, s + 1)
                            for j in range(2):
                                strip = a1_groups[g][j]
                                idx = spill_idx[strip]
                                r = idx % 3
                                if tb == 0 and idx >= 3:
                                    a1spR.wait_done(v, idx - 3)
                                S["a1ev"].inc(v.tensor_copy(
                                    rawring[r][:, 3 + tb * 512:3 + (tb + 1) * 512],
                                    bank(2 * rr + j)))
                        if a1_groups[g][0] >= 64:
                            v.wait_ge(S["a1ev"].sem, 2 * NTB * (g + 1))
                        for j in range(2):
                            strip = a1_groups[g][j]
                            if strip < 64:
                                continue
                            idx = spill_idx[strip]
                            r = idx % 3
                            wc = (strip - 64) * CK
                            if nconv_before[strip] >= 1:
                                v.wait_ge(S["a1ac"].sem, nconv_before[strip])
                            raw = rawring[r]
                            S["a1cv"].inc(v.tensor_scalar(t0[:], raw[:, 0:L],
                                                          sb_convw[:, wc:wc + 1], None, ALU.mult))
                            v.wait_ge(S["a1cv"].sem, ncv + 1)
                            S["a1cv"].inc(v.scalar_tensor_tensor(t1[:], raw[:, 1:1 + L],
                                                                 sb_convw[:, wc + 1:wc + 2],
                                                                 t0[:], ALU.mult, ALU.add))
                            v.wait_ge(S["a1cv"].sem, ncv + 2)
                            S["a1cv"].inc(v.scalar_tensor_tensor(t0[:], raw[:, 2:2 + L],
                                                                 sb_convw[:, wc + 2:wc + 3],
                                                                 t1[:], ALU.mult, ALU.add))
                            v.wait_ge(S["a1cv"].sem, ncv + 3)
                            S["a1cv"].inc(v.scalar_tensor_tensor(t1[:], raw[:, 3:3 + L],
                                                                 sb_convw[:, wc + 3:wc + 4],
                                                                 t0[:], ALU.mult, ALU.add))
                            ncv += 4

                @blk.scalar
                def _(sc):
                    guards(sc)
                    nsilu = 0

                    def emit_silu(sc, g):
                        nonlocal nsilu
                        for j in range(2):
                            strip = a1_groups[g][j]
                            if strip < 64:
                                continue
                            idx = spill_idx[strip]
                            r = idx % 3
                            nsilu += 1
                            sc.wait_ge(S["a1cv"].sem, 4 * nsilu)
                            cb = strip - 64
                            S["a1ac"].inc(sc.activation(rawring[r][:, 3:3 + L], t1[:],
                                                        AF.Silu, bias=sb_convb[:, cb:cb + 1]))

                    for g in range(NGRP):
                        emit_silu(sc, g)
                        for j in range(2):
                            strip = a1_groups[g][j]
                            idx = spill_idx[strip]
                            r = idx % 3
                            if strip < 64:
                                sc.wait_ge(S["a1ev"].sem, 2 * NTB * (g + 1))
                            a1spR.inc(sc.dma_start(ZX[strip, :, :],
                                                   rawring[r][:, 3:3 + L]), idx)

            ns_ = len(strip_order)
            a1_guards = [(S["a1sp_r0"], 16 * ((ns_ + 3) // 4)), (S["a1sp_r1"], 16 * ((ns_ + 2) // 4)),
                         (S["a1sp_r2"], 16 * ((ns_ + 1) // 4)), (S["a1sp_r3"], 16 * (ns_ // 4)),
                         (S["a1ev"], 2 * NTB * NGRP),
                         (S["a1ac"], NCONV), (S["a1cv"], 4 * NCONV),
                         (S["a1pe"], NTB * NGRP)]

        saX.close()

        if PH >= 1:
            with ExitStack() as sB:
                EB = sB.enter_context
                xtr = [EB(nc.sbuf_tensor(f"b_xt{i}", [128, 8192], BF16)) for i in range(3)]
                btr = [EB(nc.sbuf_tensor(f"b_bt{i}", [128, 1024], BF16)) for i in range(4)]
                bcb = [EB(nc.sbuf_tensor(f"b_bc{i}", [128, 4096], BF16)) for i in range(2)]
                csfl_t = EB(nc.sbuf_tensor("b_csf", [33, 2048], F32))
                odfl_t = EB(nc.sbuf_tensor("b_odf", [33, 2048], BF16))
                eafl_t = EB(nc.sbuf_tensor("b_eaf", [33, 128], F32))
                csfl = [csfl_t[0:1, :], csfl_t[32:33, :]]
                odfl = [odfl_t[0:1, :], odfl_t[32:33, :]]
                eafl = [eafl_t[0:1, :], eafl_t[32:33, :]]
                earep = [EB(nc.sbuf_tensor(f"b_ear{i}", [128, 128], F32)) for i in range(2)]
                gm = [EB(nc.sbuf_tensor(f"b_gm{i}", [128, 512], BF16)) for i in range(2)]
                earg = [EB(nc.sbuf_tensor(f"b_earg{i}", [128, 512], BF16)) for i in range(2)]
                ee = [EB(nc.sbuf_tensor(f"b_ee{i}", [128, 512], BF16)) for i in range(2)]
                mbf = [EB(nc.sbuf_tensor(f"b_m{i}", [128, 512], BF16)) for i in range(2)]
                wof = [EB(nc.sbuf_tensor(f"b_wof{i}", [128, 256], BF16)) for i in range(2)]
                odrep = [EB(nc.sbuf_tensor(f"b_odr{i}", [128, 256], BF16)) for i in range(2)]
                bddb = [EB(nc.sbuf_tensor(f"b_bdd{i}", [128, 256], BF16)) for i in range(2)]
                sbfb = [EB(nc.sbuf_tensor(f"b_sbf{i}", [128, 64], BF16)) for i in range(4)]
                smas = EB(nc.sbuf_tensor("b_s", [128, 8192], F32))
                yst = [EB(nc.sbuf_tensor(f"b_yst{i}", [128, 2048], BF16)) for i in range(2)]

                bcR = RingSem([S["bc_r0"], S["bc_r1"]])
                xtRg = RingSem([S["xt_r0"], S["xt_r1"], S["xt_r2"]])
                btRg = RingSem([S["bt_r0"], S["bt_r1"], S["bt_r2"], S["bt_r3"]])
                csfR = RingSem([S["csf_r0"], S["csf_r1"]])
                odfR = RingSem([S["odf_r0"], S["odf_r1"]])
                eafR = RingSem([S["eaf_r0"], S["eaf_r1"]])
                yspR = RingSem([S["ysp_r0"], S["ysp_r1"]])

                def bguards(eng):
                    for cnt, val in a1_guards:
                        eng.wait_ge(cnt.sem, val)
                    eng.wait_ge(S["absp"].sem, 48)

                XBCV = ZX[64:128, :, :].rearrange("s p l -> (s p) l")
                BTV = ZX[128:136, :, :].rearrange("s p l -> (s p) l")
                BCV = ZX[128:144, :, :].rearrange("s p l -> p s l")

                NGI = NCH * 16

                def flat_load(gp, gi):
                    c0, i = gi // 16, gi % 16
                    if gi >= 2:
                        gp.wait_ge(S["bpeB"].sem, 8 * (gi - 1))
                    csfR.inc(gp.dma_start(
                        csfl[gi % 2].rearrange("o (h l) -> o h l", l=CH),
                        CS_SP[8 * i:8 * i + 8, c0 * CH:(c0 + 1) * CH].rearrange("(o h) l -> o h l", o=1)), gi)
                    odfR.inc(gp.dma_start(
                        odfl[gi % 2].rearrange("o (h l) -> o h l", l=CH),
                        OD_SP[8 * i:8 * i + 8, c0 * CH:(c0 + 1) * CH].rearrange("(o h) l -> o h l", o=1)), gi)

                wtt = [EB(nc.sbuf_tensor(f"b_wtt{i}", [128, 4096], BF16)) for i in range(2)]
                NWT = KDIN + KD if PH >= 2 else 0
                wtrR = RingSem([S["p0"], S["p0t"]])
                wstR = RingSem([S["p0v"], S["dtrX"]])

                def emit_wt(sy, k):
                    sy.wait_ge(S["cast2"].sem, 16 * 15)
                    if k >= 2:
                        wstR.wait_done(sy, k - 2)
                    if k < KDIN:
                        wtrR.inc(sy.dma_start_transpose(
                            wtt[k % 2][:], WOUT_BF[:, k * 128:(k + 1) * 128]), k)
                        wtrR.wait_done(sy, k)
                        wstR.inc(sy.dma_start(WOUTT[k * 128:(k + 1) * 128, :],
                                              wtt[k % 2][:]), k)
                    else:
                        kk = k - KDIN
                        wtrR.inc(sy.dma_start_transpose(
                            wtt[k % 2][:], WBASE_BF[:, kk * 128:(kk + 1) * 128]), k)
                        wtrR.wait_done(sy, k)
                        wstR.inc(sy.dma_start(WBASET[kk * 128:(kk + 1) * 128, :],
                                              wtt[k % 2][:]), k)

                with nc.Block("ssd") as blk:
                    @blk.sync
                    def _(sy):
                        bguards(sy)
                        nwt = 0
                        for c in range(NCH):
                            for half in range(2):
                                hf = 2 * c + half
                                if hf >= 3:
                                    sy.wait_ge(S["bpeY"].sem, 128 * ((hf - 3) // 2 + 1))
                                xtRg.inc(sy.dma_start_transpose(
                                    xtr[hf % 3][:], XBCV[:, hf * 128:(hf + 1) * 128]), hf)
                                if hf >= 4:
                                    sy.wait_ge(S["bsr"].sem, 128 * ((hf - 4) // 2 + 1))
                                btRg.inc(sy.dma_start_transpose(
                                    btr[hf % 4][:], BTV[:, hf * 128:(hf + 1) * 128]), hf)
                            per = (NWT + NCH - 1) // NCH if NCH else 0
                            for _w in range(per):
                                if nwt >= NWT:
                                    break
                                k = nwt
                                emit_wt(sy, k)
                                nwt += 1
                        while nwt < NWT:
                            k = nwt
                            emit_wt(sy, k)
                            nwt += 1

                    @blk.gpsimd
                    def _(gp):
                        bguards(gp)
                        if PH >= 2:
                            for r in range(8):
                                S["cast2"].inc(gp.dma_start(
                                    WOUT_BF[r * 512:(r + 1) * 512, :],
                                    WOUT[r * 512:(r + 1) * 512, :]), 16)
                            for r in range(4):
                                S["cast2"].inc(gp.dma_start(
                                    WBASE_BF[r * 1024:(r + 1) * 1024, :],
                                    WBASE[r * 1024:(r + 1) * 1024, :]), 16)
                            S["cast2"].inc(gp.dma_start(A_BF, AEXP), 16)
                            S["cast2"].inc(gp.dma_start(B_BF, BEXP), 16)
                            S["cast2"].inc(gp.dma_start(WG_BF, WGATE9), 16)
                        S["bld"].inc(gp.memset(smas[:], 0.0))
                        bcv3 = lambda i: bcb[i % 2][:].rearrange("p (s l) -> p s l", l=CH)
                        bcR.inc(gp.dma_start(bcv3(0), BCV[:, :, 0:CH]), 0)
                        if NCH > 1:
                            bcR.inc(gp.dma_start(bcv3(1), BCV[:, :, CH:2 * CH]), 1)
                        eafR.inc(gp.dma_start(eafl[0], EA_SP[0:1, :]), 0)
                        if NCH > 1:
                            eafR.inc(gp.dma_start(eafl[1], EA_SP[1:2, :]), 1)
                        for gi in range(min(2, NGI)):
                            flat_load(gp, gi)
                        for c in range(NCH):
                            for h in range(128):
                                gh = c * 128 + h
                                g = h // 16
                                if h % 8 == 7:
                                    gi = (gh + 1) // 8 + 1
                                    if gi < NGI:
                                        flat_load(gp, gi)
                                if h == 0 and c + 2 < NCH:
                                    gp.wait_ge(S["bear"].sem, c + 1)
                                    eafR.inc(gp.dma_start(eafl[(c + 2) % 2],
                                                          EA_SP[c + 2:c + 3, :]), c + 2)
                                if gh == 0:
                                    gp.wait_ge(S["bld"].sem, 1)
                                if gh >= 4:
                                    gp.wait_ge(S["bpeY"].sem, gh - 3)
                                S["bsbf"].inc(gp.tensor_copy(sbfb[h % 4][:],
                                                             smas[:, h * 64:(h + 1) * 64]))
                                gp.wait_ge(S["bacX"].sem, gh + 1)
                                gp.wait_ge(S["bgev"].sem, 2 * (c * 8 + g + 1))
                                if gh >= 2:
                                    gp.wait_ge(S["bpeY"].sem, gh - 1)
                                S["bmtt"].inc(gp.tensor_tensor(mbf[h % 2][:, 0:256],
                                                               gm[g % 2][:, 0:256],
                                                               ee[h % 2][:, 0:256], ALU.mult))
                                S["bmtt"].inc(gp.tensor_tensor(mbf[h % 2][:, 256:512],
                                                               gm[g % 2][:, 256:512],
                                                               ee[h % 2][:, 256:512], ALU.mult))
                                if h % 16 == 15:
                                    b = h // 16
                                    gb = c * 8 + b
                                    gp.wait_ge(S["bev"].sem, 64 * c + 8 * (b + 1))
                                    yspR.inc(gp.dma_start(
                                        Y_SP[8 * b:8 * b + 8, :, c * CH:(c + 1) * CH]
                                        .rearrange("s p l -> p s l"),
                                        yst[b % 2][:].rearrange("p (s l) -> p s l", l=CH)), gb)
                            if c + 2 < NCH:
                                gp.wait_ge(S["bev"].sem, 64 * (c + 1))
                                bcR.inc(gp.dma_start(bcv3(c + 2),
                                                     BCV[:, :, (c + 2) * CH:(c + 3) * CH]), c + 2)

                    @blk.tensor
                    def _(t):
                        bguards(t)

                        def emit_G(c, g):
                            gg = c * 8 + g
                            bcR.wait_done(t, c)
                            if g == 0:
                                t.wait_ge(S["bear"].sem, c + 1)
                            if gg >= 1:
                                t.wait_ge(S["bgev"].sem, 2 * gg)
                            bb = bcb[c % 2]
                            t.matmul(bank(0, 0, 256), bb[:, g * 256:g * 256 + 128],
                                     bb[:, 2048 + g * 256:2048 + (g + 1) * 256],
                                     start=True, stop=True)
                            S["bpeG"].inc(t.matmul(bank(0, 256, 256),
                                                   bb[:, g * 256 + 128:(g + 1) * 256],
                                                   bb[:, 2048 + g * 256:2048 + (g + 1) * 256],
                                                   start=True, stop=True))

                        def emit_bcast(c, hb):
                            ghb = c * 128 + hb
                            gi = ghb // 8
                            csfR.wait_done(t, gi)
                            odfR.wait_done(t, gi)
                            if ghb >= 2:
                                t.wait_ge(S["bdvE"].sem, 2 * (ghb - 1))
                                t.wait_ge(S["bodv"].sem, ghb - 1)
                            i8 = hb % 8
                            sl32 = 32 * (gi % 2)
                            t.matmul(bank(2 + hb % 2, 0, 256),
                                     sb_ones_f[sl32:sl32 + 1, 0:128],
                                     csfl[gi % 2][:, i8 * 256:(i8 + 1) * 256],
                                     start=True, stop=True)
                            S["bpeB"].inc(t.matmul(bank(2 + hb % 2, 256, 256),
                                                   sb_ones_b[sl32:sl32 + 1, 0:128],
                                                   odfl[gi % 2][:, i8 * 256:(i8 + 1) * 256],
                                                   start=True, stop=True))

                        def emit_states(c, h):
                            gh = c * 128 + h
                            sbk = 1 if h % 2 == 0 else 6
                            t.wait_ge(S["bdd"].sem, 2 * (gh + 1))
                            if gh >= 2:
                                t.wait_ge(S["bsr"].sem, gh - 1)
                            t.matmul(bank(sbk, 0, 64), bddb[h % 2][:, 0:128],
                                     xtr[(2 * c) % 3][:, h * 64:(h + 1) * 64],
                                     start=True, stop=False)
                            S["bpeS"].inc(t.matmul(bank(sbk, 0, 64), bddb[h % 2][:, 128:256],
                                                   xtr[(2 * c + 1) % 3][:, h * 64:(h + 1) * 64],
                                                   start=False, stop=True))

                        def emit_Y(c, h):
                            gh = c * 128 + h
                            pp = h // 2
                            gpp = c * 64 + pp
                            t.wait_ge(S["bmtt"].sem, 2 * (gh + 1))
                            if c > 0:
                                t.wait_ge(S["bwof"].sem, gh + 1)
                                t.wait_ge(S["bsbf"].sem, gh + 1)
                            if h % 2 == 0 and gpp >= 2:
                                t.wait_ge(S["bev"].sem, gpp - 1)
                            reg = bank(4 + pp % 2, 0, 256, p0=(h % 2) * 64, p1=(h % 2) * 64 + 64)
                            last_d = (c == 0)
                            t.matmul(reg, xtr[(2 * c) % 3][:, h * 64:(h + 1) * 64],
                                     mbf[h % 2][:, 0:256], start=True, stop=False,
                                     skip_group_check=True)
                            mm = t.matmul(reg, xtr[(2 * c + 1) % 3][:, h * 64:(h + 1) * 64],
                                          mbf[h % 2][:, 256:512], start=False,
                                          stop=(last_d and h % 2 == 1), skip_group_check=True)
                            if c > 0:
                                mm = t.matmul(reg, sbfb[h % 4][:], wof[h % 2][:],
                                              start=False, stop=(h % 2 == 1),
                                              skip_group_check=True)
                            S["bpeY"].inc(mm)

                        for c in range(NCH):
                            eafR.wait_done(t, c)
                            if c >= 1:
                                t.wait_ge(S["bear"].sem, c)
                                t.wait_ge(S["bgev"].sem, 16 * c)
                                t.wait_ge(S["bsr"].sem, 128 * c)
                            ea32 = 32 * (c % 2)
                            S["bpea"].inc(t.matmul(bank(0, 0, 128),
                                                   sb_ones_f[ea32:ea32 + 1, 0:128],
                                                   eafl[c % 2], start=True, stop=True))
                            if c == 0:
                                xtRg.wait_done(t, 0)
                                xtRg.wait_done(t, 1)
                            else:
                                xtRg.wait_done(t, 2 * c)
                                xtRg.wait_done(t, 2 * c + 1)
                            emit_G(c, 0)
                            emit_bcast(c, 0)
                            emit_bcast(c, 1)
                            for h in range(128):
                                if (h + 2) < 128 and (h + 2) % 16 == 0:
                                    emit_G(c, (h + 2) // 16)
                                if h + 2 < 128:
                                    emit_bcast(c, h + 2)
                                emit_states(c, h)
                                emit_Y(c, h)

                    @blk.vector
                    def _(v):
                        bguards(v)
                        for c in range(NCH):
                            v.wait_ge(S["bpea"].sem, c + 1)
                            S["bear"].inc(v.tensor_copy(earep[c % 2][:], bank(0, 0, 128)))
                            if c == 0:
                                btRg.wait_done(v, 0)
                                btRg.wait_done(v, 1)
                            else:
                                btRg.wait_done(v, 2 * c)
                                btRg.wait_done(v, 2 * c + 1)
                            for h in range(-1, 128):
                                gh = c * 128 + h
                                if h + 1 < 128:
                                    hn = h + 1
                                    ghn = gh + 1
                                    gn = hn // 16
                                    if hn % 16 == 0:
                                        g = gn
                                        gg = c * 8 + g
                                        v.wait_ge(S["bpeG"].sem, gg + 1)
                                        if gg >= 2:
                                            v.wait_ge(S["bmtt"].sem, 32 * (gg - 1))
                                        S["bgev"].inc(v.tensor_tensor(
                                            gm[g % 2][:, 0:256], bank(0, 0, 256),
                                            sb_tril[:, 0:256], ALU.mult))
                                        S["bgev"].inc(v.tensor_tensor(
                                            gm[g % 2][:, 256:512], bank(0, 256, 256),
                                            sb_tril[:, 256:512], ALU.mult))
                                    if ghn >= 2:
                                        v.wait_ge(S["bpeS"].sem, ghn - 1)
                                    S["bdd"].inc(v.tensor_scalar(
                                        bddb[hn % 2][:, 0:128],
                                        btr[(2 * c) % 4][:, gn * 128:(gn + 1) * 128],
                                        sb_ddtT[:, (2 * c) * 128 + hn:(2 * c) * 128 + hn + 1],
                                        None, ALU.mult))
                                    S["bdd"].inc(v.tensor_scalar(
                                        bddb[hn % 2][:, 128:256],
                                        btr[(2 * c + 1) % 4][:, gn * 128:(gn + 1) * 128],
                                        sb_ddtT[:, (2 * c + 1) * 128 + hn:(2 * c + 1) * 128 + hn + 1],
                                        None, ALU.mult))
                                    v.wait_ge(S["bpeB"].sem, ghn + 1)
                                    if ghn >= 2:
                                        v.wait_ge(S["bacX"].sem, ghn - 1)
                                    S["bdvE"].inc(v.tensor_scalar(
                                        earg[hn % 2][:, 0:256], bank(2 + hn % 2, 0, 256),
                                        sb_qT[:, (2 * c) * 128 + hn:(2 * c) * 128 + hn + 1],
                                        ECLAMP, ALU.subtract, ALU.min))
                                    S["bdvE"].inc(v.tensor_scalar(
                                        earg[hn % 2][:, 256:512], bank(2 + hn % 2, 0, 256),
                                        sb_qT[:, (2 * c + 1) * 128 + hn:(2 * c + 1) * 128 + hn + 1],
                                        ECLAMP, ALU.subtract, ALU.min))
                                    v.wait_ge(S["bodv"].sem, ghn + 1)
                                    if ghn >= 2:
                                        v.wait_ge(S["bpeY"].sem, ghn - 1)
                                    S["bwof"].inc(v.tensor_tensor(
                                        wof[hn % 2][:],
                                        bcb[c % 2][:, 2048 + gn * 256:2048 + (gn + 1) * 256],
                                        odrep[hn % 2][:], ALU.mult))
                                if h >= 0:
                                    v.wait_ge(S["bpeS"].sem, gh + 1)
                                    v.wait_ge(S["bear"].sem, c + 1)
                                    v.wait_ge(S["bsbf"].sem, gh + 1)
                                    S["bsr"].inc(v.scalar_tensor_tensor(
                                        smas[:, h * 64:(h + 1) * 64],
                                        smas[:, h * 64:(h + 1) * 64],
                                        earep[c % 2][:, h:h + 1],
                                        bank(1 if h % 2 == 0 else 6, 0, 64),
                                        ALU.mult, ALU.add))
                                    if h % 2 == 1:
                                        pp = h // 2
                                        b = pp // 8
                                        gb = c * 8 + b
                                        v.wait_ge(S["bpeY"].sem, gh + 1)
                                        if gb >= 2:
                                            yspR.wait_done(v, gb - 2)
                                        S["bev"].inc(v.tensor_copy(
                                            yst[b % 2][:, (pp % 8) * 256:(pp % 8 + 1) * 256],
                                            bank(4 + pp % 2, 0, 256)))

                    @blk.scalar
                    def _(sc):
                        bguards(sc)
                        for c in range(NCH):
                            for h in range(128):
                                gh = c * 128 + h
                                sc.wait_ge(S["bdvE"].sem, 2 * (gh + 1))
                                if gh >= 2:
                                    sc.wait_ge(S["bmtt"].sem, 2 * (gh - 1))
                                S["bacX"].inc(sc.activation(ee[h % 2][:], earg[h % 2][:], AF.Exp))
                                sc.wait_ge(S["bpeB"].sem, gh + 1)
                                if gh >= 2:
                                    sc.wait_ge(S["bwof"].sem, gh - 1)
                                S["bodv"].inc(sc.activation(odrep[h % 2][:],
                                                            bank(2 + h % 2, 256, 256),
                                                            AF.Identity))

            b_guards = [(S["p0"], S["p0"].n), (S["p0t"], S["p0t"].n),
                        (S["p0v"], S["p0v"].n), (S["dtrX"], S["dtrX"].n),
                        (S["bsr"], 128 * NCH), (S["bev"], 64 * NCH),
                        (S["ysp_r0"], 16 * (4 * NCH)), (S["ysp_r1"], 16 * (4 * NCH)),
                        (S["bacX"], 128 * NCH), (S["bodv"], 128 * NCH),
                        (S["bmtt"], 256 * NCH), (S["bdd"], 256 * NCH),
                        (S["bdvE"], 256 * NCH), (S["bpeY"], 128 * NCH),
                        (S["bgev"], 16 * NCH)]

        sQ.close()

        if PH >= 2:
            NQ = L // 512
            cwR = RingSem([S["cw_r0"], S["cw_r1"]])
            zldR = RingSem([S["zl_r0"], S["zl_r1"]])
            xdlR = RingSem([S["xl_r0"], S["xl_r1"]])
            ylR = RingSem([S["yl_r0"]])
            aspR = RingSem([S["as_r0"], S["as_r1"]])

            def cdvp(gq, st, op):
                return gq * 320 + st * 4 + op

            def cacp(gq, st):
                return gq * 66 + st + 1

            with ExitStack() as sC:
                EC = sC.enter_context
                ybuf = EC(nc.sbuf_tensor("c_y", [128, 64 * 512], BF16))
                wtog = [EC(nc.sbuf_tensor(f"c_wt{i}", [128, KDIN * 256], BF16)) for i in range(2)]
                zring = [EC(nc.sbuf_tensor(f"c_z{i}", [128, 512], BF16)) for i in range(4)]
                szb = [EC(nc.sbuf_tensor(f"c_sz{i}", [128, 512], BF16)) for i in range(4)]
                xdr = [EC(nc.sbuf_tensor(f"c_xd{i}", [128, 512], BF16)) for i in range(4)]
                sqb2 = EC(nc.sbuf_tensor("c_sq", [128, 512], F32))
                acc = EC(nc.sbuf_tensor("c_acc", [128, 512], F32))
                rsqv = EC(nc.sbuf_tensor("c_rsq", [1, 512], F32))
                ost = [EC(nc.sbuf_tensor(f"c_o{i}", [128, 512], BF16)) for i in range(4)]

                def cguards(eng):
                    for cnt, val in b_guards:
                        eng.wait_ge(cnt.sem, val)

                with nc.Block("c1") as blk:
                    @blk.gpsimd
                    def _(gp):
                        cguards(gp)
                        for q in range(NQ):
                            if q >= 1:
                                gp.wait_ge(S["cevs"].sem, 32 * q)
                            ylR.inc(gp.dma_start(
                                ybuf[:].rearrange("p (s l) -> p s l", l=512),
                                Y_SP[:, :, q * 512:(q + 1) * 512]
                                .rearrange("s p l -> p s l")), q)
                            for st in range(64):
                                gi = q * 64 + st
                                if gi >= 2:
                                    pq, pst = divmod(gi - 2, 64)
                                    gp.wait_ge(S["cac"].sem, cacp(pq, pst))
                                    gp.wait_ge(S["cdv"].sem, cdvp(pq, pst, 1))
                                zldR.inc(gp.dma_start(zring[st % 4][:],
                                                      ZX[st, :, q * 512:(q + 1) * 512]), gi)
                                xdlR.inc(gp.dma_start(xdr[st % 4][:],
                                                      ZX[64 + st, :, q * 512:(q + 1) * 512]), gi)
                            for og in range(16):
                                for j in range(2):
                                    sidx = q * 32 + 2 * og + j
                                    gp.wait_ge(S["cevs"].sem, sidx + 1)
                                    if sidx >= 2:
                                        aspR.wait_done(gp, sidx - 2)
                                    aspR.inc(gp.dma_start(
                                        ATTN_SP[2 * og + j, :, q * 512:(q + 1) * 512],
                                        ost[sidx % 4][:]), sidx)

                    @blk.scalar
                    def _(sc):
                        cguards(sc)
                        for q in range(NQ):
                            for st in range(64):
                                gi = q * 64 + st
                                zldR.wait_done(sc, gi)
                                if gi >= 4:
                                    pq, pst = divmod(gi - 4, 64)
                                    sc.wait_ge(S["cdv"].sem, cdvp(pq, pst, 2))
                                S["cac"].inc(sc.activation(szb[st % 4][:], zring[st % 4][:],
                                                           AF.Silu))
                            sc.wait_ge(S["cpe"].sem, q * 18 + 1)
                            S["cac"].inc(sc.activation(rsqv[0:1, :], bank(0, 0, 512, 0, 1),
                                                       AF.Ln, bias=sb_epscol[0:1, 0:1],
                                                       scale=1.0 / DIN))
                            sc.wait_ge(S["cac"].sem, q * 66 + 65)
                            S["cac"].inc(sc.activation(rsqv[0:1, :], rsqv[0:1, :],
                                                       AF.Exp, scale=-0.5))

                    @blk.vector
                    def _(v):
                        cguards(v)
                        for q in range(NQ):
                            ylR.wait_done(v, q)
                            for st in range(64):
                                gi = q * 64 + st
                                xdlR.wait_done(v, gi)
                                ys = ybuf[:, st * 512:(st + 1) * 512]
                                S["cdv"].inc(v.scalar_tensor_tensor(
                                    ys, xdr[st % 4][:], sb_dexp[:, st:st + 1], ys,
                                    ALU.mult, ALU.add))
                                v.wait_ge(S["cac"].sem, cacp(q, st))
                                v.wait_ge(S["cdv"].sem, cdvp(q, st, 1))
                                S["cdv"].inc(v.tensor_tensor(ys, ys, szb[st % 4][:], ALU.mult))
                                v.wait_ge(S["cdv"].sem, cdvp(q, st, 2))
                                S["cdv"].inc(v.tensor_tensor(sqb2[:], ys, ys, ALU.mult))
                                v.wait_ge(S["cdv"].sem, cdvp(q, st, 3))
                                if st == 0:
                                    S["cdv"].inc(v.tensor_copy(acc[:], sqb2[:]))
                                else:
                                    S["cdv"].inc(v.tensor_tensor(acc[:], acc[:], sqb2[:],
                                                                 ALU.add))
                            v.wait_ge(S["cpe"].sem, q * 18 + 2)
                            for st in range(64):
                                ys = ybuf[:, st * 512:(st + 1) * 512]
                                S["cdv"].inc(v.scalar_tensor_tensor(
                                    ys, ys, sb_nrmw[:, st:st + 1], bank(1, 0, 512),
                                    ALU.mult, ALU.mult))
                            for og in range(16):
                                gog = q * 16 + og
                                rr = gog % 2
                                v.wait_ge(S["cpe"].sem, q * 18 + 2 + og + 1)
                                for j in range(2):
                                    sidx = q * 32 + 2 * og + j
                                    if sidx >= 4:
                                        aspR.wait_done(v, sidx - 4, uses=(sidx - 4) // 2 + 1)
                                    S["cevs"].inc(v.tensor_copy(ost[sidx % 4][:],
                                                                bank(2 + 2 * ((rr * 2 + j) // 2) + j, 0, 512)
                                                                if False else
                                                                bank(2 + 2 * rr + j, 0, 512)))

                    @blk.tensor
                    def _(t):
                        cguards(t)
                        wstR.wait_done(t, KDIN - 2)
                        wstR.wait_done(t, KDIN - 1)
                        for q in range(NQ):
                            t.wait_ge(S["cdv"].sem, q * 320 + 256)
                            S["cpe"].inc(t.matmul(bank(0, 0, 512, 0, 1), sb_onecol[:, 0:1],
                                                  acc[:], start=True, stop=True))
                            t.wait_ge(S["cac"].sem, q * 66 + 66)
                            S["cpe"].inc(t.matmul(bank(1, 0, 512), sb_ones_f[0:1, 0:128],
                                                  rsqv[0:1, :], start=True, stop=True))
                            for og in range(16):
                                gog = q * 16 + og
                                cwR.wait_done(t, gog)
                                rr = gog % 2
                                t.wait_ge(S["cdv"].sem, (q + 1) * 320)
                                if gog >= 2:
                                    t.wait_ge(S["cevs"].sem, 2 * (gog - 1))
                                for k in range(KDIN):
                                    for j in range(2):
                                        mm = t.matmul(
                                            bank(2 + 2 * rr + j, 0, 512),
                                            wtog[gog % 2][:, k * 256 + j * 128:k * 256 + (j + 1) * 128],
                                            ybuf[:, k * 512:(k + 1) * 512],
                                            start=(k == 0), stop=(k == KDIN - 1))
                                    if k == KDIN - 1:
                                        S["cpe"].inc(mm)

                    @blk.sync
                    def _(sy):
                        cguards(sy)
                        wstR.wait_done(sy, KDIN - 2)
                        wstR.wait_done(sy, KDIN - 1)
                        for q in range(NQ):
                            for og in range(16):
                                gog = q * 16 + og
                                if gog >= 2:
                                    pq, pog = divmod(gog - 2, 16)
                                    sy.wait_ge(S["cpe"].sem, pq * 18 + 2 + pog + 1)
                                cwR.inc(sy.dma_start(
                                    wtog[gog % 2][:].rearrange("p (k o) -> p k o", o=256),
                                    WOUTT[:, og * 256:(og + 1) * 256]
                                    .rearrange("(k p) o -> p k o", p=128)), gog)

            with ExitStack() as sC2:
                EC = sC2.enter_context
                att = [EC(nc.sbuf_tensor(f"c2_a{i}", [128, D], BF16)) for i in range(2)]
                xtl = [EC(nc.sbuf_tensor(f"c2_x{i}", [128, D], F32)) for i in range(2)]
                xlb = [EC(nc.sbuf_tensor(f"c2_l{i}", [128, D], BF16)) for i in range(2)]
                x2b = [EC(nc.sbuf_tensor(f"c2_x2{i}", [128, D], F32)) for i in range(2)]
                x2l = EC(nc.sbuf_tensor("c2_xl", [128, D], BF16))
                sqc = EC(nc.sbuf_tensor("c2_sq", [128, D], F32))
                stc = EC(nc.sbuf_tensor("c2_st", [128, 8], F32))
                atlR = RingSem([S["at_r0"], S["at_r1"]])
                xtlR = RingSem([S["xt2_r0"], S["xt2_r1"]])
                xlbR = RingSem([S["zl_r0"], S["zl_r1"]])
                x2sR = RingSem([S["x2_r0"], S["x2_r1"]])
                xlsR = RingSem([S["xl2_r0"], S["xl2_r1"]])

                ATV = ATTN_SP.rearrange("s p l -> (s p) l")

                with nc.Block("c2") as blk:
                    @blk.sync
                    def _(sy):
                        aspR.wait_all(sy, 32 * NQ)
                        for i in range(NLT):
                            if i >= 2:
                                sy.wait_ge(S["cx2v"].sem, 9 * (i - 1))
                            atlR.inc(sy.dma_start_transpose(
                                att[i % 2][:], ATV[:, i * 128:(i + 1) * 128]), i)

                    @blk.gpsimd
                    def _(gp):
                        for i in range(NLT):
                            if i >= 2:
                                gp.wait_ge(S["cx2v"].sem, 9 * (i - 1))
                            xtlR.inc(gp.dma_start(xtl[i % 2][:], X[i * 128:(i + 1) * 128, :]), i)
                            xlbR.inc(gp.dma_start(xlb[i % 2][:],
                                                  XLNT[i * 128:(i + 1) * 128, :]), i)
                            gp.wait_ge(S["cx2v"].sem, 9 * i + 2)
                            x2sR.inc(gp.dma_start(X2_SP[i * 128:(i + 1) * 128, :],
                                                  x2b[i % 2][:]), i)
                            gp.wait_ge(S["cx2a"].sem, 4 * i + 4)
                            xlsR.inc(gp.dma_start(X2LNT[i * 128:(i + 1) * 128, :], x2l[:]), i)

                    @blk.vector
                    def _(v):
                        for i in range(NLT):
                            atlR.wait_done(v, i)
                            xtlR.wait_done(v, i)
                            xlbR.wait_done(v, i)
                            if i >= 2:
                                x2sR.wait_done(v, i - 2)
                                v.wait_ge(S["cx2a"].sem, 4 * (i - 1))
                            S["cx2v"].inc(v.tensor_tensor(x2b[i % 2][:], xtl[i % 2][:],
                                                          att[i % 2][:], ALU.add))
                            v.wait_ge(S["cx2v"].sem, 9 * i + 1)
                            S["cx2v"].inc(v.tensor_tensor(x2b[i % 2][:], x2b[i % 2][:],
                                                          xlb[i % 2][:], ALU.add))
                            v.wait_ge(S["cx2v"].sem, 9 * i + 2)
                            S["cx2v"].inc(v.reduce_sum(stc[:, 0:1], x2b[i % 2][:], axis=AX.X))
                            v.wait_ge(S["cx2a"].sem, 4 * i + 1)
                            S["cx2v"].inc(v.reduce_sum(stc[:, 1:2], sqc[:], axis=AX.X))
                            v.wait_ge(S["cx2v"].sem, 9 * i + 4)
                            S["cx2v"].inc(v.tensor_scalar(stc[:, 2:3], stc[:, 0:1], 1.0 / D,
                                                          None, ALU.mult))
                            v.wait_ge(S["cx2v"].sem, 9 * i + 5)
                            S["cx2v"].inc(v.tensor_tensor(stc[:, 3:4], stc[:, 2:3],
                                                          stc[:, 2:3], ALU.mult))
                            v.wait_ge(S["cx2v"].sem, 9 * i + 6)
                            S["cx2v"].inc(v.scalar_tensor_tensor(
                                stc[:, 4:5], stc[:, 1:2], 1.0 / D, stc[:, 3:4],
                                ALU.mult, ALU.subtract))
                            v.wait_ge(S["cx2a"].sem, 4 * i + 3)
                            S["cx2v"].inc(v.tensor_tensor(stc[:, 7:8], stc[:, 2:3],
                                                          stc[:, 6:7], ALU.mult))
                            v.wait_ge(S["cx2v"].sem, 9 * i + 8)
                            S["cx2v"].inc(v.tensor_scalar(stc[:, 7:8], stc[:, 7:8], -1.0,
                                                          None, ALU.mult))

                    @blk.scalar
                    def _(sc):
                        for i in range(NLT):
                            sc.wait_ge(S["cx2v"].sem, 9 * i + 2)
                            S["cx2a"].inc(sc.activation(sqc[:], x2b[i % 2][:], AF.Square))
                            sc.wait_ge(S["cx2v"].sem, 9 * i + 7)
                            S["cx2a"].inc(sc.activation(stc[:, 5:6], stc[:, 4:5], AF.Ln,
                                                        bias=sb_epscol[:, 0:1]))
                            sc.wait_ge(S["cx2a"].sem, 4 * i + 2)
                            S["cx2a"].inc(sc.activation(stc[:, 6:7], stc[:, 5:6], AF.Exp,
                                                        scale=-0.5))
                            sc.wait_ge(S["cx2v"].sem, 9 * i + 9)
                            if i >= 1:
                                xlsR.wait_done(sc, i - 1)
                            S["cx2a"].inc(sc.activation(x2l[:], x2b[i % 2][:], AF.Identity,
                                                        scale=stc[:, 6:7], bias=stc[:, 7:8]))

            c2_guards = [(S["x2_r0"], 16 * ((NLT + 1) // 2)), (S["x2_r1"], 16 * (NLT // 2)),
                         (S["xl2_r0"], 16 * ((NLT + 1) // 2)), (S["xl2_r1"], 16 * (NLT // 2)),
                         (S["cx2a"], 4 * NLT), (S["cx2v"], 9 * NLT)]

        if PH >= 3:
            d_guards = list(b_guards) + [
                (S["cevs"], 32 * NQ), (S["cpe"], 18 * NQ),
                (S["cdv"], 320 * NQ), (S["cac"], 66 * NQ),
                (S["cx2v"], 9 * NLT), (S["cx2a"], 4 * NLT),
                (S["x2_r0"], S["x2_r0"].n), (S["x2_r1"], S["x2_r1"].n),
                (S["xl2_r0"], S["xl2_r0"].n), (S["xl2_r1"], S["xl2_r1"].n),
                (S["as_r0"], S["as_r0"].n), (S["as_r1"], S["as_r1"].n)]
            wbR = RingSem([S["wb2_r0"], S["wb2_r1"]])
            ffsR = RingSem([S["ffs_r0"], S["ffs_r1"]])
            bldR = RingSem([S["dbl"]])
            xlfD = Cnt(S["xlf"].sem)
            xlfD.n = S["xlf"].n
            S["xlf"] = xlfD
            dtrD = S["dtrX"]
            dtr_base = dtrD.n
            xlf_base = xlfD.n

            def DPE_BFT(i):
                return i + 1
            DPE_WGT0 = 64
            DPE_GATE0 = 96
            def GDONE(i):
                return 97 + i if i <= 2 else 95 + 2 * i
            def TRDONE(j):
                if j <= NLT - 3:
                    return 100 + 2 * j
                return 95 + 2 * NLT + (j - (NLT - 2))
            DPE_WEBC0 = 96 + 2 * NLT
            DPE_H0 = DPE_WEBC0 + NTB
            DPE_OG0 = DPE_H0 + NTB
            def DDV_BFT(i):
                return i + 2
            DDV_WGT0 = 65
            DDV_GATE0 = 97
            NGOP = 11
            DDV_WT0 = DDV_GATE0 + NGOP * NLT
            DDV_WEBC0 = DDV_WT0
            DDV_H0 = DDV_WEBC0 + NTB
            DDV_HW = DDV_H0 + NTB + 1
            DDV_EV0 = DDV_HW

            with ExitStack() as sD:
                ED = sD.enter_context
                x2lnf = ED(nc.sbuf_tensor("d_xlf", [128, KD * L], BF16))
                wtb = [ED(nc.sbuf_tensor(f"d_wt{i}", [128, KD * 256], BF16)) for i in range(2)]
                atb = ED(nc.sbuf_tensor("d_at", [128, KD * 128], BF16))
                wgbf = ED(nc.sbuf_tensor("d_wg", [9, D], BF16))
                wgT = ED(nc.sbuf_tensor("d_wgT", [128, KD * 16], BF16))
                ebc8b = ED(nc.sbuf_tensor("d_ebc", [8, 128], BF16))
                bft = ED(nc.sbuf_tensor("d_bft", [128, D], BF16))
                btmp = ED(nc.sbuf_tensor("d_btmp", [128, 64], BF16))
                gbuf = ED(nc.sbuf_tensor("d_gb", [128, 40], F32))
                wnb = [ED(nc.sbuf_tensor(f"d_wn{i}", [128, 8], BF16)) for i in range(2)]
                wT = ED(nc.sbuf_tensor("d_wT", [8, L], BF16))
                webc = ED(nc.sbuf_tensor("d_webc", [128, L], BF16))
                hbuf = ED(nc.sbuf_tensor("d_h", [128, L], BF16))
                fstg = [ED(nc.sbuf_tensor(f"d_f{i}", [128, 512], BF16)) for i in range(4)]

                def dguards(eng):
                    for cnt, val in d_guards:
                        eng.wait_ge(cnt.sem, val)

                DCUT2 = os.environ.get("MK_DCUT") == "2"
                DCUT3 = os.environ.get("MK_DCUT") in ("3", "4")
                DCUT4 = os.environ.get("MK_DCUT") in ("4", "5")
                DCUT5 = os.environ.get("MK_DCUT") in ("5", "6")
                DCUT6 = os.environ.get("MK_DCUT") == "6"
                with nc.Block("d12") as blk:
                    @blk.sync
                    def _(sy):
                        dguards(sy)
                        for k in range(KD):
                            dtrD.inc(sy.dma_start_transpose(
                                x2lnf[:, k * L:(k + 1) * L],
                                X2LNT[:, k * 128:(k + 1) * 128]), 16)
                        for k in range(KD):
                            xlfD.inc(sy.dma_start_transpose(
                                atb[:, k * 128:(k + 1) * 128],
                                A_BF[:, k * 128:(k + 1) * 128]), 16)
                        xlfD.inc(sy.dma_start(wgbf[:], WG_BF), 16)
                        xlfD.inc(sy.dma_start(ebc8b[:], EBC8), 16)
                        for og in range(0 if (DCUT2 or DCUT3) else 16):
                            if og >= 2:
                                sy.wait_ge(S["dpe"].sem, DPE_OG0 + (og - 1) * NTB)
                            wbR.inc(sy.dma_start(
                                wtb[og % 2][:].rearrange("p (k o) -> p k o", o=256),
                                WBASET[:, og * 256:(og + 1) * 256]
                                .rearrange("(k p) o -> p k o", p=128)), og)

                    @blk.gpsimd
                    def _(gp):
                        dguards(gp)
                        for it in range(64):
                            g4, ot = divmod(it, KD)
                            if it >= 1:
                                gp.wait_ge(S["ddv"].sem, DDV_BFT(it - 1))
                            for j in range(4):
                                e = 4 * g4 + j
                                bldR.inc(gp.dma_start(
                                    btmp[:, j * 16:(j + 1) * 16],
                                    B_BF[e * D + ot * 128:e * D + (ot + 1) * 128, :]),
                                    it * 4 + j)
                        for og in range(0 if (DCUT2 or DCUT3) else 16):
                            for tb in range(NTB):
                                for j in range(2):
                                    sidx = (og * NTB + tb) * 2 + j
                                    gp.wait_ge(S["ddv"].sem, DDV_EV0 + sidx + 1)
                                    if sidx >= 2:
                                        ffsR.wait_done(gp, sidx - 2)
                                    ffsR.inc(gp.dma_start(
                                        FFN_SP[2 * og + j, :, tb * 512:(tb + 1) * 512],
                                        fstg[sidx % 4][:]), sidx)

                    @blk.tensor
                    def _(t):
                        dguards(t)
                        for it in range(64):
                            t.wait_ge(S["dbl"].sem, 16 * 4 * (it + 1))
                            S["dpe"].inc(t.transpose(psB16[0:64, 0:128], btmp[:],
                                                     sb_i128b[:]))
                        t.wait_ge(S["xlf"].sem, xlf_base + 16 * (KD + 2))
                        t.wait_ge(S["ddv"].sem, 65)
                        for k in range(KD):
                            if k >= 1:
                                t.wait_ge(S["ddv"].sem, DDV_WGT0 + k)
                            S["dpe"].inc(t.transpose(
                                psB16[0:128, 128 + (k % 2) * 16:128 + (k % 2) * 16 + 9],
                                wgbf[0:9, k * 128:(k + 1) * 128], sb_i128b[0:9, 0:9]))
                        t.wait_ge(S["dtrX"].sem, dtr_base + 16 * KD)
                        t.wait_ge(S["ddv"].sem, DDV_WGT0 + KD)
                        if DCUT2:
                            return

                        def wn_tr(i):
                            t.wait_ge(S["ddv"].sem, DDV_GATE0 + i * NGOP + NGOP)
                            if i >= 1:
                                t.wait_ge(S["dtrF"].sem, i)
                            S["dpe"].inc(t.transpose(
                                psB16[0:8, 512 + (i % 2) * 128:512 + (i % 2) * 128 + 128],
                                wnb[i % 2][:], sb_i128b[:]))

                        ntr = 0
                        for i in range(NLT if "gates" not in DSKIP else 0):
                            if i >= 2:
                                t.wait_ge(S["ddv"].sem, DDV_GATE0 + (i - 1) * NGOP)
                            for k in range(KD):
                                mm = t.matmul(bank(i % 2, 0, 16),
                                              x2lnf[:, k * L + i * 128:k * L + (i + 1) * 128],
                                              wgT[:, k * 16:(k + 1) * 16],
                                              start=(k == 0), stop=(k == KD - 1))
                            S["dpe"].inc(mm)
                            if not DCUT5 and i >= 2:
                                wn_tr(i - 2)
                                ntr += 1
                        while not DCUT5 and ntr < NLT:
                            wn_tr(ntr)
                            ntr += 1
                        for tb in range(0 if DCUT4 else NTB):
                            t.wait_ge(S["ddv"].sem, DDV_WT0)
                            t.wait_ge(S["dtrF"].sem, NLT)
                            if tb >= 2:
                                t.wait_ge(S["ddv"].sem, DDV_WEBC0 + tb - 1)
                            S["dpe"].inc(t.matmul(bank(tb % 2), ebc8b[:],
                                                  wT[:, tb * 512:(tb + 1) * 512],
                                                  start=True, stop=True))
                        for tb in range(0 if DCUT4 else NTB):
                            if tb >= 2:
                                t.wait_ge(S["ddv"].sem, DDV_H0 + tb - 1)
                            for k in range(KD):
                                mm = t.matmul(bank(2 + tb % 2),
                                              atb[:, k * 128:(k + 1) * 128],
                                              x2lnf[:, k * L + tb * 512:k * L + (tb + 1) * 512],
                                              start=(k == 0), stop=(k == KD - 1))
                            S["dpe"].inc(mm)
                        t.wait_ge(S["ddv"].sem, DDV_HW)
                        for og in range(0 if DCUT3 else 16):
                            wbR.wait_done(t, og)
                            for tb in range(NTB):
                                gg = og * NTB + tb
                                rr = gg % 2
                                if gg >= 2:
                                    t.wait_ge(S["ddv"].sem, DDV_EV0 + 2 * (gg - 1))
                                for j in range(2):
                                    for k in range(KD):
                                        mm = t.matmul(
                                            bank(2 + 2 * rr + j),
                                            wtb[og % 2][:, k * 256 + j * 128:k * 256 + (j + 1) * 128],
                                            x2lnf[:, k * L + tb * 512:k * L + (tb + 1) * 512],
                                            start=(k == 0), stop=False)
                                    mm = t.matmul(
                                        bank(2 + 2 * rr + j),
                                        bft[:, (2 * og + j) * 128:(2 * og + j + 1) * 128],
                                        hbuf[:, tb * 512:(tb + 1) * 512],
                                        start=False, stop=True)
                                S["dpe"].inc(mm)

                    @blk.vector
                    def _(v):
                        dguards(v)
                        S["ddv"].inc(v.memset(wgT[:], 0.0))
                        for it in range(64):
                            g4 = it // KD
                            ot = it % KD
                            v.wait_ge(S["dpe"].sem, DPE_BFT(it))
                            S["ddv"].inc(v.tensor_copy(
                                bft[64 * g4:64 * g4 + 64, ot * 128:(ot + 1) * 128],
                                psB16[0:64, 0:128]))
                        for k in range(KD):
                            v.wait_ge(S["dpe"].sem, DPE_WGT0 + k + 1)
                            S["ddv"].inc(v.tensor_copy(
                                wgT[:, k * 16:k * 16 + 9],
                                psB16[0:128, 128 + (k % 2) * 16:128 + (k % 2) * 16 + 9]))
                        if DCUT2:
                            return
                        for i in range(NLT):
                            v.wait_ge(S["dpe"].sem, GDONE(i))
                            gb = gbuf[:, (i % 2) * 16:(i % 2) * 16 + 16]
                            S["ddv"].inc(v.tensor_copy(gb[:, 0:9], bank(i % 2, 0, 9)))
                            if DCUT6:
                                for _pad in range(NGOP - 1):
                                    S["ddv"].inc(v.tensor_copy(gbuf[:, 32 + (_pad % 8):33 + (_pad % 8)],
                                                               gbuf[:, 0:1]))
                                continue
                            v.wait_ge(S["ddv"].sem, DDV_GATE0 + i * NGOP + 1)
                            S["ddv"].inc(v.reduce_max(gb[:, 9:10], gb[:, 0:8], axis=AX.X))
                            v.wait_ge(S["ddv"].sem, DDV_GATE0 + i * NGOP + 2)
                            S["ddv"].inc(v.tensor_scalar(gb[:, 10:11], gb[:, 9:10], -1.0,
                                                         None, ALU.mult))
                            v.wait_ge(S["dac"].sem, 3 * i + 1)
                            S["ddv"].inc(v.reduce_sum(gb[:, 11:12], gb[:, 0:8], axis=AX.X))
                            v.wait_ge(S["ddv"].sem, DDV_GATE0 + i * NGOP + 4)
                            S["ddv"].inc(v.reciprocal(gb[:, 12:13], gb[:, 11:12]))
                            v.wait_ge(S["dac"].sem, 3 * i + 3)
                            v.wait_ge(S["ddv"].sem, DDV_GATE0 + i * NGOP + 5)
                            S["ddv"].inc(v.tensor_scalar(gb[:, 0:8], gb[:, 0:8],
                                                         gb[:, 12:13], None, ALU.mult))
                            v.wait_ge(S["ddv"].sem, DDV_GATE0 + i * NGOP + 6)
                            S["ddv"].inc(v.tensor_scalar(gb[:, 0:8], gb[:, 0:8],
                                                         gb[:, 13:14], 0.0,
                                                         ALU.subtract, ALU.max))
                            v.wait_ge(S["ddv"].sem, DDV_GATE0 + i * NGOP + 7)
                            S["ddv"].inc(v.reduce_sum(gb[:, 14:15], gb[:, 0:8], axis=AX.X))
                            v.wait_ge(S["ddv"].sem, DDV_GATE0 + i * NGOP + 8)
                            S["ddv"].inc(v.tensor_scalar(gb[:, 15:16], gb[:, 14:15], 1e-30,
                                                         None, ALU.max))
                            v.wait_ge(S["ddv"].sem, DDV_GATE0 + i * NGOP + 9)
                            S["ddv"].inc(v.reciprocal(gb[:, 15:16], gb[:, 15:16]))
                            v.wait_ge(S["ddv"].sem, DDV_GATE0 + i * NGOP + 10)
                            if i >= 2 and not DCUT5:
                                v.wait_ge(S["dpe"].sem, TRDONE(i - 2))
                            S["ddv"].inc(v.tensor_scalar(wnb[i % 2][:], gb[:, 0:8],
                                                         gb[:, 15:16], None, ALU.mult))
                            if not DCUT5 and i >= 1:
                                v.wait_ge(S["dpe"].sem, TRDONE(i - 1))
                                S["dtrF"].inc(v.tensor_copy(
                                    wT[:, (i - 1) * 128:i * 128],
                                    psB16[0:8, 512 + ((i - 1) % 2) * 128:512 + ((i - 1) % 2) * 128 + 128]))
                        if not DCUT5:
                            v.wait_ge(S["dpe"].sem, TRDONE(NLT - 1))
                            S["dtrF"].inc(v.tensor_copy(
                                wT[:, (NLT - 1) * 128:NLT * 128],
                                psB16[0:8, 512 + ((NLT - 1) % 2) * 128:512 + ((NLT - 1) % 2) * 128 + 128]))
                        for tb in range(0 if DCUT4 else NTB):
                            v.wait_ge(S["dpe"].sem, DPE_WEBC0 + tb + 1)
                            S["ddv"].inc(v.tensor_scalar(webc[:, tb * 512:(tb + 1) * 512],
                                                         bank(tb % 2), 2.0, None, ALU.mult))
                        for tb in range(0 if DCUT4 else NTB):
                            v.wait_ge(S["dpe"].sem, DPE_H0 + tb + 1)
                            S["ddv"].inc(v.tensor_copy(hbuf[:, tb * 512:(tb + 1) * 512],
                                                       bank(2 + tb % 2)))
                        if not DCUT4:
                            v.wait_ge(S["ddv"].sem, DDV_H0 + NTB)
                            S["ddv"].inc(v.tensor_tensor(hbuf[:], hbuf[:], webc[:], ALU.mult))
                        for og in range(0 if DCUT3 else 16):
                            for tb in range(NTB):
                                gg = og * NTB + tb
                                rr = gg % 2
                                v.wait_ge(S["dpe"].sem, DPE_OG0 + gg + 1)
                                for j in range(2):
                                    sidx = gg * 2 + j
                                    if sidx >= 4:
                                        ffsR.wait_done(v, sidx - 4,
                                                       uses=(sidx - 4) // 2 + 1)
                                    S["ddv"].inc(v.tensor_scalar(
                                        fstg[sidx % 4][:], bank(2 + 2 * rr + j),
                                        sb_bbase[:, 2 * og + j:2 * og + j + 1], None,
                                        ALU.add))

                    @blk.scalar
                    def _(sc):
                        dguards(sc)
                        for i in range(0 if (DCUT2 or DCUT6) else NLT):
                            gb = gbuf[:, (i % 2) * 16:(i % 2) * 16 + 16]
                            sc.wait_ge(S["ddv"].sem, DDV_GATE0 + i * NGOP + 3)
                            S["dac"].inc(sc.activation(gb[:, 0:8], gb[:, 0:8], AF.Exp,
                                                       bias=gb[:, 10:11]))
                            S["dac"].inc(sc.activation(gb[:, 13:14], gb[:, 8:9], AF.Sigmoid,
                                                       bias=sb_bthr[:, 0:1]))
                            sc.wait_ge(S["dac"].sem, 3 * i + 2)
                            S["dac"].inc(sc.activation(gb[:, 13:14], gb[:, 13:14],
                                                       AF.Identity, scale=0.125))

            if os.environ.get("MK_DCUT") in ("1", "2", "3", "4", "5", "6"):
                with nc.Block("fincut") as blk:
                    @blk.gpsimd
                    def _(gp):
                        if os.environ.get("MK_DCUT") == "2":
                            gp.wait_ge(S["ddv"].sem, DDV_WGT0 + KD)
                            gp.wait_ge(S["dpe"].sem, 96)
                        elif os.environ.get("MK_DCUT") == "3":
                            gp.wait_ge(S["ddv"].sem, DDV_HW)
                            gp.wait_ge(S["dpe"].sem, DPE_OG0)
                        elif os.environ.get("MK_DCUT") == "4":
                            gp.wait_ge(S["ddv"].sem, DDV_WT0 + NLT)
                            gp.wait_ge(S["dpe"].sem, DPE_WEBC0)
                        elif os.environ.get("MK_DCUT") == "5":
                            gp.wait_ge(S["ddv"].sem, DDV_GATE0 + NLT * NGOP)
                            gp.wait_ge(S["dpe"].sem, DPE_WTR0)
                        else:
                            ffsR.wait_all(gp, 32 * NTB)
                            gp.wait_ge(S["ddv"].sem, DDV_EV0 + 2 * 32 * NTB)
                        S["dout"].inc(gp.dma_start(OUT, X), 16)
                        gp.wait_ge(S["dout"].sem, 16)
                return nc
            with ExitStack() as sD3:
                ED = sD3.enter_context
                ftb = [ED(nc.sbuf_tensor(f"d3_f{i}", [128, D], BF16)) for i in range(2)]
                x2t = [ED(nc.sbuf_tensor(f"d3_x{i}", [128, D], F32)) for i in range(2)]
                ostg = [ED(nc.sbuf_tensor(f"d3_o{i}", [128, D], F32)) for i in range(2)]
                ftR = RingSem([S["ft2_r0"], S["ft2_r1"]])
                xfR = RingSem([S["xf_r0"], S["xf_r1"]])
                ouR = RingSem([S["ou_r0"], S["ou_r1"]])
                FFV = FFN_SP.rearrange("s p l -> (s p) l")

                with nc.Block("d3") as blk:
                    @blk.sync
                    def _(sy):
                        ffsR.wait_all(sy, 32 * NTB)
                        for i in range(NLT):
                            if i >= 2:
                                sy.wait_ge(S["ddv"].sem, DDV_EV0 + 32 * NTB + i - 1)
                            ftR.inc(sy.dma_start_transpose(
                                ftb[i % 2][:], FFV[:, i * 128:(i + 1) * 128]), i)

                    @blk.gpsimd
                    def _(gp):
                        for i in range(NLT):
                            if i >= 2:
                                gp.wait_ge(S["ddv"].sem, DDV_EV0 + 32 * NTB + i - 1)
                            xfR.inc(gp.dma_start(x2t[i % 2][:],
                                                 X2_SP[i * 128:(i + 1) * 128, :]), i)
                            gp.wait_ge(S["ddv"].sem, DDV_EV0 + 32 * NTB + i + 1)
                            ouR.inc(gp.dma_start(OUT[i * 128:(i + 1) * 128, :],
                                                 ostg[i % 2][:]), i)
                        ouR.wait_all(gp, NLT)

                    @blk.vector
                    def _(v):
                        for i in range(NLT):
                            ftR.wait_done(v, i)
                            xfR.wait_done(v, i)
                            if i >= 2:
                                ouR.wait_done(v, i - 2)
                            S["ddv"].inc(v.tensor_tensor(ostg[i % 2][:], x2t[i % 2][:],
                                                         ftb[i % 2][:], ALU.add))

        fin_guards = a1_guards if PH == 0 else b_guards
        if PH < 3:
            fin_guards = a1_guards if PH == 0 else b_guards
            with nc.Block("fin") as blk:
                @blk.gpsimd
                def _(gp):
                    for cnt, val in fin_guards:
                        gp.wait_ge(cnt.sem, val)
                    S["dout"].inc(gp.dma_start(OUT, X), 16)
                    gp.wait_ge(S["dout"].sem, 16)
        return nc

    return nc


def _consts(L):
    tril = np.zeros((128, 512), np.float32)
    for p in range(128):
        tril[p, p:256] = 1.0
        if 128 + p < 256:
            pass
        tril[p, 256 + 128 + p:512] = 1.0
    ebc = np.zeros((8, 128), np.float32)
    for e in range(8):
        ebc[e, e * 16:(e + 1) * 16] = 1.0
    return dict(
        tril=tril.astype(BF),
        i128f=np.eye(128, dtype=np.float32),
        i128b=np.eye(128, dtype=np.float32).astype(BF),
        ebc8=ebc.astype(BF),
        onesb=np.ones((1, 128), BF),
        onesf=np.ones((1, 128), np.float32),
    )


def make_in_maps(inputs, L=2048, ncores=8):
    w = {
        "win": np.ascontiguousarray(inputs["L0_attn_in_proj_w"], np.float32),
        "convw": np.ascontiguousarray(inputs["L0_attn_conv_w"], np.float32),
        "convb": np.ascontiguousarray(inputs["L0_attn_conv_b"], np.float32),
        "dtbias": np.ascontiguousarray(inputs["L0_attn_dt_bias"], np.float32),
        "alog": np.ascontiguousarray(inputs["L0_attn_A_log"], np.float32),
        "dhead": np.ascontiguousarray(inputs["L0_attn_D"], np.float32).reshape(1, H),
        "nrmw": np.ascontiguousarray(inputs["L0_attn_norm_w"], np.float32),
        "wout": np.ascontiguousarray(inputs["L0_attn_out_proj_w"], np.float32),
        "wbase": np.ascontiguousarray(inputs["L0_ffn_W_base"], np.float32),
        "bbase": np.ascontiguousarray(inputs["L0_ffn_b_base"], np.float32),
        "wgate9": np.ascontiguousarray(
            np.concatenate([np.asarray(inputs["L0_ffn_W_gate"], np.float32),
                            np.asarray(inputs["L0_ffn_W_thr"], np.float32)], axis=0)),
        "bthr": np.ascontiguousarray(inputs["L0_ffn_b_thr"], np.float32).reshape(1, 1),
        "aexp": np.ascontiguousarray(inputs["L0_ffn_A"], np.float32).reshape(128, D),
        "bexp": np.ascontiguousarray(inputs["L0_ffn_B"], np.float32).reshape(8 * D, 16),
    }
    w.update(_consts(L))
    xs = np.asarray(inputs["L0_attn_hidden_states"], np.float32)
    maps = []
    for b in range(ncores):
        m = dict(w)
        m["x"] = np.ascontiguousarray(xs[b, :L, :])
        maps.append(m)
    return maps


def kernel(**inputs):
    L = 2048
    nc = build_program(L=L)
    maps = make_in_maps(inputs, L=L)
    res = run_bass_kernel_spmd(nc, maps, core_ids=list(range(8)))
    out = np.stack([np.asarray(r["out"], np.float32) for r in res.results], axis=0)
    return out
